# version 14 (final)
# baseline (speedup 1.0000x reference)
import numpy as np
import concourse.bass as bass
import concourse.mybir as mybir
from concourse.bass_utils import run_bass_kernel_spmd
from concourse.bass import IndirectOffsetOnAxis

F32 = mybir.dt.float32
I32 = mybir.dt.int32
BF16 = mybir.dt.bfloat16
AX = mybir.AxisListType
ALU = mybir.AluOpType
ACT = mybir.ActivationFunctionType

D = 1024
DI = 512
NH = 8
E = 8
CAP = 368
NR = 16
SLOTS = CAP + NR
THETA = 1e-4
LN_EPS = 1e-5
ARENA_COLS = 28000
_BF16 = True
_BF16_EXP = True
_F1SH_BF16 = True
_BF16_SH = True


class Sem:
    def __init__(self, nc, name):
        self.h = nc.alloc_semaphore(name)
        self.n = 0

    def inc(self, k=1):
        self.n += k
        return self.n


def build_core_program(T=2048, stages=("f1", "attn", "f3"), debug=False, attn_level=6):
    ntile = T // 128
    nc = bass.Bass("TRN2", target_bir_lowering=False, debug=False,
                   detect_race_conditions=False)
    streams = {k: [] for k in ("tensor", "vector", "scalar", "gpsimd", "sync")}

    def em(eng, fn):
        streams[eng].append(fn)

    st, sv, ss, sg, sy = (Sem(nc, "st"), Sem(nc, "sv"), Sem(nc, "ss"),
                          Sem(nc, "sg"), Sem(nc, "sy"))
    qw, qg, qr = Sem(nc, "qw"), Sem(nc, "qg"), Sem(nc, "qr")

    x_h = nc.dram_tensor("x", [T, D], F32, kind="ExternalInput").ap()
    y_h = nc.dram_tensor("y", [T, D], F32, kind="ExternalOutput").ap()
    upT_h = nc.dram_tensor("upTt", [E, 16, 8, 128, 128], F32, kind="ExternalInput").ap()
    gateT_h = nc.dram_tensor("gateTt", [E, 16, 8, 128, 128], F32, kind="ExternalInput").ap()
    downT_h = nc.dram_tensor("downTt", [E, 16, 128, D], F32, kind="ExternalInput").ap()
    wupT_h = nc.dram_tensor("wupTt", [16, 8, 128, 128], F32, kind="ExternalInput").ap()
    wgateT_h = nc.dram_tensor("wgateTt", [16, 8, 128, 128], F32, kind="ExternalInput").ap()
    wdownT_h = nc.dram_tensor("wdownTt", [16, 128, D], F32, kind="ExternalInput").ap()
    routerT_h = nc.dram_tensor("routerT", [128, 64], F32, kind="ExternalInput").ap()
    wq_h = nc.dram_tensor("Wq", [D, DI], F32, kind="ExternalInput").ap()
    wk_h = nc.dram_tensor("Wk", [D, DI], F32, kind="ExternalInput").ap()
    wv_h = nc.dram_tensor("Wv", [D, DI], F32, kind="ExternalInput").ap()
    wa_h = nc.dram_tensor("Wa", [D, DI], F32, kind="ExternalInput").ap()
    wo_h = nc.dram_tensor("Wo", [DI, D], F32, kind="ExternalInput").ap()
    ba_h = nc.dram_tensor("ba_tile", [128, DI], F32, kind="ExternalInput").ap()
    gam_h = nc.dram_tensor("gamma_tile", [128, D], F32, kind="ExternalInput").ap()
    bet_h = nc.dram_tensor("beta_tile", [128, D], F32, kind="ExternalInput").ap()
    ident_h = nc.dram_tensor("ident", [128, 128], F32, kind="ExternalInput").ap()
    ut_h = nc.dram_tensor("ut", [128, 128], F32, kind="ExternalInput").ap()
    onesr_h = nc.dram_tensor("ones_row", [1, 128], F32, kind="ExternalInput").ap()
    onesc_h = nc.dram_tensor("ones_col", [128, 1], F32, kind="ExternalInput").ap()
    iotaE_h = nc.dram_tensor("iotaE", [128, 8], F32, kind="ExternalInput").ap()
    iota16_h = nc.dram_tensor("iota16", [128, 16], F32, kind="ExternalInput").ap()
    cst_h = nc.dram_tensor("actconsts", [128, 8], F32, kind="ExternalInput").ap()
    xs_h = nc.dram_tensor("xs_scratch", [E * CAP, D], F32, kind="Internal").ap()
    ys_h = nc.dram_tensor("ys_scratch", [E * CAP, D], F32, kind="Internal").ap()
    eug_scr = nc.dram_tensor("eug_scr", [E, 16, 128, 4096], BF16, kind="Internal").ap()
    edn_scr = nc.dram_tensor("edn_scr", [E, 16, 128, 2048], BF16, kind="Internal").ap()
    sug_scr = nc.dram_tensor("sug_scr", [16, 128, 4096], BF16, kind="Internal").ap()
    sdn_scr = nc.dram_tensor("sdn_scr", [16, 128, 2048], BF16, kind="Internal").ap()
    if debug:
        dbg_desti_h = nc.dram_tensor("dbg_desti", [128, 16], I32, kind="ExternalOutput").ap()
        dbg_idxf_h = nc.dram_tensor("dbg_idxf", [128, 16], F32, kind="ExternalOutput").ap()
        dbg_ambc_h = nc.dram_tensor("dbg_ambc", [128, 16], F32, kind="ExternalOutput").ap()
        dbg_xs_h = nc.dram_tensor("dbg_xs", [E * CAP, D], F32, kind="ExternalOutput").ap()
        dbg_ys_h = nc.dram_tensor("dbg_ys", [E * CAP, D], F32, kind="ExternalOutput").ap()
        dbg_yrsel_h = nc.dram_tensor("dbg_yrsel", [16, D], F32, kind="ExternalOutput").ap()
        dbg_hall_h = nc.dram_tensor("dbg_hall", [128, 16 * SLOTS], F32, kind="ExternalOutput").ap()
        dbg_xst_h = nc.dram_tensor("dbg_xst", [128, 8 * SLOTS], F32, kind="ExternalOutput").ap()
        dbg_wdn_h = nc.dram_tensor("dbg_wdn", [128, 2 * D], F32, kind="ExternalOutput").ap()
        dbg_qs_h = nc.dram_tensor("dbg_qs", [128, 4 * DI], F32, kind="ExternalOutput").ap()
        dbg_ks_h = nc.dram_tensor("dbg_ks", [128, 4 * DI], F32, kind="ExternalOutput").ap()
        dbg_vs_h = nc.dram_tensor("dbg_vs", [128, 4 * DI], F32, kind="ExternalOutput").ap()
        dbg_kbb_h = nc.dram_tensor("dbg_kbb", [128, 4 * DI], F32, kind="ExternalOutput").ap()
        dbg_csb_h = nc.dram_tensor("dbg_csb", [128, 4 * DI], F32, kind="ExternalOutput").ap()
        dbg_ots_h = nc.dram_tensor("dbg_ots", [128, 4 * 128], F32, kind="ExternalOutput").ap()
        dbg_sst_h = nc.dram_tensor("dbg_sst", [64, NH * 64], F32, kind="ExternalOutput").ap()
        dbg_sdec_h = nc.dram_tensor("dbg_sdec", [64, 32], F32, kind="ExternalOutput").ap()
        dbg_yss_h = nc.dram_tensor("dbg_yss", [128, 3 * D], F32, kind="ExternalOutput").ap()

    A = lambda name, shape, dt=F32: nc.alloc_sbuf_tensor(name, shape, dt).ap()
    X = A("X", [128, ntile * D])
    XNT = A("XNT", [128, 8 * 512])
    XNB = A("XNB", [128, 2 * D])
    STAT = A("STAT", [128, 4 * 16])
    RT1 = A("RT1", [128, 32])
    KNRM = A("KNRM", [128, 32])
    TSC = A("TSC", [128, D])
    SGS = A("SGS", [128, D])
    DESTI = A("DESTI", [128, 16], I32)
    IDENT = A("IDENT", [128, 128])
    UT = A("UT", [128, 128])
    ONESR = A("ONESR", [1, 128])
    ONESC = A("ONESC", [128, 1])
    IOTAE = A("IOTAE", [128, 8])
    IOTA16 = A("IOTA16", [128, 16])
    CST = A("CST", [128, 8])
    ROUTT = A("ROUTT", [128, 64])

    ARENA = A("ARENA", [128, ARENA_COLS])
    _off = [0]

    def carve(ncols, parts=128):
        a = ARENA[0:parts, _off[0]:_off[0] + ncols]
        _off[0] += ncols
        return a

    HALL = carve(16 * SLOTS)
    XST = carve(8 * SLOTS)
    GX = carve(3 * D)
    WUG = carve(4096)
    WDN = carve(2 * D)
    YSG = carve(D)
    YSS = carve(3 * D)
    GR = carve(256)
    GRT = carve(2048, parts=16)
    XR = carve(D, parts=16)
    YRSEL = carve(D, parts=16)
    XRT = carve(128)
    OHR = carve(8, parts=16)
    LG = carve(16)
    BASE = carve(16, parts=1)
    IDXF = carve(16)
    AMBC = carve(16)
    RT2 = carve(40)
    ZEROT = YSG[:, 0:D]
    HS2 = ARENA[0:128, 0:16 * 512]
    assert _off[0] <= ARENA_COLS, _off[0]

    WUGb = WUG.bitcast(BF16)
    WDNb = WDN.bitcast(BF16)
    XSTb = XST.bitcast(BF16)
    XSTH = XSTb[:, 0:8 * SLOTS]
    XSTL = XSTb[:, 8 * SLOTS:16 * SLOTS]
    HALLb = HALL.bitcast(BF16)
    HALLH = HALLb[:, 0:16 * SLOTS]
    HALLL = HALLb[:, 16 * SLOTS:32 * SLOTS]
    HS2b = HS2.bitcast(BF16)
    HS2H = HS2b[:, 0:8192]
    HS2L = HS2b[:, 8192:16384]
    GXb = GX.bitcast(BF16)
    TSCb = TSC.bitcast(BF16)
    XNBb = XNB.bitcast(BF16)
    XNTb = XNT.bitcast(BF16)
    XNTH = XNTb[:, 0:4096]
    XNTL = XNTb[:, 4096:8192]

    _off[0] = 0
    QS = carve(4 * DI)
    KS = carve(4 * DI)
    VS = carve(4 * DI)
    ALA = carve(4 * DI)
    CSB = carve(2 * DI)
    KBB = carve(4 * DI)
    KBE = carve(2 * DI)
    QTT = carve(2 * NH * 128, parts=64)
    KHT = carve(2 * NH * 128, parts=64)
    WOS = carve(4 * D)
    EF = carve(2 * DI)
    OTS = carve(4 * 128)
    SST = carve(NH * 64, parts=64)
    SDEC = carve(32, parts=64)
    PMS = carve(128)
    WAT = carve(2 * DI)
    BA = carve(DI)
    GAM = carve(D)
    BET = carve(D)
    assert _off[0] <= ARENA_COLS, _off[0]

    PB = [nc.alloc_psum_tensor(f"pb{i}", [128, 512], F32).ap() for i in range(8)]

    def dma(engq, out, in_, sem):
        tgt = sem.inc(16)
        em(engq, lambda e, out=out, in_=in_: e.dma_start(out=out, in_=in_).then_inc(sem.h, 16))
        return tgt

    for pair in ((IDENT, ident_h), (UT, ut_h), (ONESR, onesr_h), (ONESC, onesc_h),
                 (IOTAE, iotaE_h), (IOTA16, iota16_h), (CST, cst_h), (ROUTT, routerT_h)):
        dma("sync", pair[0][:, :], pair[1][:, :], qw)
    for t in range(ntile):
        dma("sync", X[:, D * t:D * (t + 1)], x_h[128 * t:128 * (t + 1), :], qw)
    consts_loaded = qw.n

    nc.const_aps.register(F32, -1.0, CST[:, 0:1])
    nc.const_aps.register(F32, LN_EPS, CST[:, 1:2])
    nc.const_aps.register(F32, 0.5, CST[:, 2:3])

    def gz(e):
        e.wait_ge(qw.h, consts_loaded)
        e.memset(ZEROT, 0.0)
        e.drain()
    em("gpsimd", gz)
    for i in range((E * CAP) // 128):
        dma("gpsimd", xs_h[128 * i:128 * (i + 1), :], ZEROT, qg)
    xs_zeroed = qg.n

    def vinit(e):
        e.wait_ge(qw.h, consts_loaded)
        e.memset(SST[:, :], 0.0).then_inc(sv.h, 1)
    em("vector", vinit)
    vinit_done = sv.inc()

    ug_seq = []
    dn_seq = []
    wat_seq = []
    comb_seq = []
    rid_ug = []
    rid_dn = []

    def stage_barrier():
        def vbar(e, nst=st.n, nss=ss.n, nqw=qw.n, nqg=qg.n, nqr=qr.n):
            e.wait_ge(st.h, nst)
            e.wait_ge(ss.h, nss)
            e.wait_ge(qw.h, nqw)
            e.wait_ge(qg.h, nqg)
            e.wait_ge(qr.h, nqr)
            e.nop().then_inc(sv.h, 1)
        em("vector", vbar)
        bar = sv.inc()
        for eng in ("sync", "gpsimd", "tensor", "scalar"):
            em(eng, lambda e, bar=bar: e.wait_ge(sv.h, bar))

    def ffn_stats_tile(t):
        par = 16 * (t % 2)
        Xt = X[:, D * t:D * (t + 1)]

        def fs_v(e, t=t, Xt=Xt, par=par):
            e.reduce_sum(RT1[:, par:par + 1], Xt, AX.X)
            e.drain()
            e.tensor_scalar(STAT[:, t:t + 1], RT1[:, par:par + 1], 1.0 / D,
                            None, ALU.mult).then_inc(sv.h, 1)
        em("vector", fs_v)
        m_done = sv.inc()

        def fs_s(e, Xt=Xt, par=par, m_done=m_done):
            e.wait_ge(sv.h, m_done)
            e.activation(SGS[:, :], Xt, ACT.Square,
                         accum_out=RT1[:, par + 1:par + 2]).then_inc(ss.h, 1)
        em("scalar", fs_s)
        ssq = ss.inc()

        def fv_v(e, t=t, par=par, ssq=ssq):
            e.wait_ge(ss.h, ssq)
            e.tensor_tensor(RT1[:, par + 2:par + 3], STAT[:, t:t + 1],
                            STAT[:, t:t + 1], ALU.mult)
            e.drain()
            e.tensor_scalar(RT1[:, par + 3:par + 4], RT1[:, par + 1:par + 2],
                            1.0 / D, RT1[:, par + 2:par + 3],
                            ALU.mult, ALU.subtract).then_inc(sv.h, 1)
        em("vector", fv_v)
        var_done = sv.inc()

        def fq_s(e, par=par, var_done=var_done):
            e.wait_ge(sv.h, var_done)
            e.activation(RT1[:, par + 4:par + 5], RT1[:, par + 3:par + 4],
                         ACT.Sqrt, bias=CST[:, 1:2]).then_inc(ss.h, 1)
        em("scalar", fq_s)
        std_done = ss.inc()

        def fr_v(e, t=t, par=par, std_done=std_done):
            e.wait_ge(ss.h, std_done)
            e.reciprocal(STAT[:, 16 + t:17 + t], RT1[:, par + 4:par + 5]).then_inc(sv.h, 1)
        em("vector", fr_v)
        return sv.inc()

    def attn_stats_tile(t, sq2h):
        par = 16 * (t % 2)
        Xt = X[:, D * t:D * (t + 1)]
        xl = GX[:, D * (t % 2):D * (t % 2 + 1)]
        gout = GX[:, 2 * D:3 * D]

        def st_v(e, t=t, Xt=Xt, par=par):
            e.reduce_sum(RT1[:, par:par + 1], Xt, AX.X)
            e.drain()
            e.tensor_scalar(STAT[:, t:t + 1], RT1[:, par:par + 1], 1.0 / D,
                            None, ALU.mult).then_inc(sv.h, 1)
        em("vector", st_v)
        m1d = sv.inc()

        def st_s(e, Xt=Xt, par=par, m1d=m1d, gout=gout):
            e.wait_ge(sv.h, m1d)
            e.activation(gout, Xt, ACT.Square,
                         accum_out=RT1[:, par + 1:par + 2]).then_inc(ss.h, 1)
        em("scalar", st_s)
        sq1 = ss.inc()

        def var1_v(e, t=t, par=par, sq1=sq1):
            e.wait_ge(ss.h, sq1)
            e.tensor_tensor(RT1[:, par + 2:par + 3], STAT[:, t:t + 1],
                            STAT[:, t:t + 1], ALU.mult)
            e.drain()
            e.tensor_scalar(RT1[:, par + 3:par + 4], RT1[:, par + 1:par + 2],
                            1.0 / D, RT1[:, par + 2:par + 3],
                            ALU.mult, ALU.subtract).then_inc(sv.h, 1)
        em("vector", var1_v)
        v1d = sv.inc()

        def sq1_s(e, par=par, v1d=v1d):
            e.wait_ge(sv.h, v1d)
            e.activation(RT1[:, par + 4:par + 5], RT1[:, par + 3:par + 4],
                         ACT.Sqrt, bias=CST[:, 1:2]).then_inc(ss.h, 1)
        em("scalar", sq1_s)
        s1d = ss.inc()

        def xl_v(e, t=t, par=par, Xt=Xt, xl=xl, s1d=s1d,
                 prev_sq2=sq2h.get(t - 2)):
            e.wait_ge(ss.h, s1d)
            if prev_sq2 is not None:
                e.wait_ge(ss.h, prev_sq2)
            e.reciprocal(STAT[:, 16 + t:17 + t], RT1[:, par + 4:par + 5])
            e.drain()
            e.tensor_scalar(xl, Xt, STAT[:, t:t + 1], STAT[:, 16 + t:17 + t],
                            ALU.subtract, ALU.mult)
            e.drain()
            e.reduce_sum(RT1[:, par:par + 1], xl, AX.X)
            e.drain()
            e.tensor_scalar(STAT[:, 32 + t:33 + t], RT1[:, par:par + 1], 1.0 / D,
                            None, ALU.mult).then_inc(sv.h, 1)
        em("vector", xl_v)
        m2d = sv.inc()

        def sq2_s(e, xl=xl, par=par, m2d=m2d, gout=gout):
            e.wait_ge(sv.h, m2d)
            e.activation(gout, xl, ACT.Square,
                         accum_out=RT1[:, par + 1:par + 2]).then_inc(ss.h, 1)
        em("scalar", sq2_s)
        sq2h[t] = ss.inc()

        def var2_v(e, t=t, par=par, sq2=sq2h[t]):
            e.wait_ge(ss.h, sq2)
            e.tensor_tensor(RT1[:, par + 2:par + 3], STAT[:, 32 + t:33 + t],
                            STAT[:, 32 + t:33 + t], ALU.mult)
            e.drain()
            e.tensor_scalar(RT1[:, par + 3:par + 4], RT1[:, par + 1:par + 2],
                            1.0 / D, RT1[:, par + 2:par + 3],
                            ALU.mult, ALU.subtract).then_inc(sv.h, 1)
        em("vector", var2_v)
        v2d = sv.inc()

        def sq2b_s(e, par=par, v2d=v2d):
            e.wait_ge(sv.h, v2d)
            e.activation(RT1[:, par + 5:par + 6], RT1[:, par + 3:par + 4],
                         ACT.Sqrt, bias=CST[:, 1:2]).then_inc(ss.h, 1)
        em("scalar", sq2b_s)
        s2d = ss.inc()

        def r2_v(e, t=t, par=par, s2d=s2d):
            e.wait_ge(ss.h, s2d)
            e.reciprocal(STAT[:, 48 + t:49 + t], RT1[:, par + 5:par + 6]).then_inc(sv.h, 1)
        em("vector", r2_v)
        return sv.inc()

    def ffn_stage(first_call, stats_pre=None, attn_stats_out=None,
                  riders=False, bf16=False):
        bf16_exp = bf16 and _BF16_EXP
        bf16_sh = bf16 and _BF16_SH
        rid_prev = {"ug": None, "dn": None, "sug": None, "sdn": None}
        def vstart(e):
            e.wait_ge(sv.h, vinit_done)
            e.memset(BASE[:, 0:9], 0.0).then_inc(sv.h, 1)
        em("vector", vstart)
        sv.inc()
        if first_call:
            em("gpsimd", lambda e: e.wait_ge(qg.h, xs_zeroed))

        scatter_tgt = {}
        xtr_tgt = {}
        rgmm_tgt = {}

        for t in range(ntile):
            buf = t % 2
            par = 16 * (t % 2)
            xn = XNB[:, D * buf:D * (buf + 1)]
            Xt = X[:, D * t:D * (t + 1)]

            if stats_pre is not None:
                def xn_v(e, t=t, xn=xn, Xt=Xt):
                    if t - 2 in scatter_tgt:
                        e.wait_ge(qg.h, scatter_tgt[t - 2])
                        e.wait_ge(st.h, xtr_tgt[t - 2])
                    e.tensor_scalar(xn, Xt, STAT[:, t:t + 1], STAT[:, 16 + t:17 + t],
                                    ALU.subtract, ALU.mult).then_inc(sv.h, 1)
                em("vector", xn_v)
                xn_done = sv.inc()
            else:
                def stats_v(e, t=t, Xt=Xt, par=par):
                    if t - 2 in scatter_tgt:
                        e.wait_ge(qg.h, scatter_tgt[t - 2])
                        e.wait_ge(st.h, xtr_tgt[t - 2])
                    e.reduce_sum(RT1[:, par:par + 1], Xt, AX.X)
                    e.drain()
                    e.tensor_scalar(STAT[:, t:t + 1], RT1[:, par:par + 1], 1.0 / D,
                                    None, ALU.mult).then_inc(sv.h, 1)
                em("vector", stats_v)
                m_done = sv.inc()

                def stats_s(e, Xt=Xt, par=par, m_done=m_done):
                    e.wait_ge(sv.h, m_done)
                    e.activation(SGS[:, :], Xt, ACT.Square,
                                 accum_out=RT1[:, par + 1:par + 2]).then_inc(ss.h, 1)
                em("scalar", stats_s)
                ssq_done = ss.inc()

                def var_v(e, t=t, par=par, ssq_done=ssq_done):
                    e.wait_ge(ss.h, ssq_done)
                    e.tensor_tensor(RT1[:, par + 2:par + 3], STAT[:, t:t + 1],
                                    STAT[:, t:t + 1], ALU.mult)
                    e.drain()
                    e.tensor_scalar(RT1[:, par + 3:par + 4], RT1[:, par + 1:par + 2],
                                    1.0 / D, RT1[:, par + 2:par + 3],
                                    ALU.mult, ALU.subtract).then_inc(sv.h, 1)
                em("vector", var_v)
                var_done = sv.inc()

                def sqrt_s(e, par=par, var_done=var_done):
                    e.wait_ge(sv.h, var_done)
                    e.activation(RT1[:, par + 4:par + 5], RT1[:, par + 3:par + 4],
                                 ACT.Sqrt, bias=CST[:, 1:2]).then_inc(ss.h, 1)
                em("scalar", sqrt_s)
                std_done = ss.inc()

                def rstd_v(e, t=t, par=par, xn=xn, Xt=Xt, std_done=std_done):
                    e.wait_ge(ss.h, std_done)
                    e.reciprocal(STAT[:, 16 + t:17 + t], RT1[:, par + 4:par + 5])
                    e.drain()
                    e.tensor_scalar(xn, Xt, STAT[:, t:t + 1], STAT[:, 16 + t:17 + t],
                                    ALU.subtract, ALU.mult).then_inc(sv.h, 1)
                em("vector", rstd_v)
                xn_done = sv.inc()

            b0, b1 = (PB[0], PB[1]) if t % 2 == 0 else (PB[6], PB[7])

            def xtr(e, xn=xn, b0=b0, b1=b1, xn_done=xn_done):
                e.wait_ge(sv.h, xn_done)
                for c in range(8):
                    pb = b0 if c < 4 else b1
                    last = e.transpose(pb[:, 128 * (c % 4):128 * (c % 4 + 1)],
                                       xn[:, 128 * c:128 * (c + 1)], IDENT[:, :])
                last.then_inc(st.h, 1)
            em("tensor", xtr)
            xtr_tgt[t] = st.inc()

            def xtr_evac(e, t=t, b0=b0, b1=b1, xd=xtr_tgt[t]):
                e.wait_ge(st.h, xd)
                sl = 128 * (t % 4)
                for c in range(8):
                    pb = b0 if c < 4 else b1
                    last = e.tensor_copy(XNT[:, 512 * c + sl:512 * c + sl + 128],
                                         pb[:, 128 * (c % 4):128 * (c % 4 + 1)])
                last.then_inc(sv.h, 1)
            em("vector", xtr_evac)
            xnt_done = sv.inc()

            def logit_mm(e, t=t, xnt_done=xnt_done):
                e.wait_ge(sv.h, xnt_done)
                sl = 128 * (t % 4)
                for c in range(8):
                    last = e.matmul(PB[5][:, 0:8], XNT[:, 512 * c + sl:512 * c + sl + 128],
                                    ROUTT[:, 8 * c:8 * (c + 1)],
                                    start=(c == 0), stop=(c == 7))
                last.then_inc(st.h, 1)
            em("tensor", logit_mm)
            lg_done = st.inc()

            def route_v(e, t=t, lg_done=lg_done):
                e.wait_ge(st.h, lg_done)
                L = LG[:, 0:8]
                e.tensor_copy(L, PB[5][:, 0:8])
                e.drain()
                e.reduce_max(RT2[:, 0:1], L, AX.X)
                e.drain()
                e.tensor_scalar(RT2[:, 8:16], L, RT2[:, 0:1], None, ALU.is_equal)
                e.drain()
                e.tensor_tensor(RT2[:, 16:24], RT2[:, 8:16], IOTAE[:, :], ALU.mult)
                e.tensor_scalar(RT2[:, 24:32], RT2[:, 8:16], -9.0, 9.0, ALU.mult, ALU.add)
                e.drain()
                e.tensor_tensor(RT2[:, 16:24], RT2[:, 16:24], RT2[:, 24:32], ALU.add)
                e.drain()
                e.tensor_reduce(IDXF[:, t:t + 1], RT2[:, 16:24], AX.X, ALU.min)
                e.drain()
                e.tensor_scalar(RT2[:, 8:16], IOTAE[:, :], IDXF[:, t:t + 1], None,
                                ALU.is_equal)
                e.drain()
                e.scalar_tensor_tensor(RT2[:, 16:24], RT2[:, 8:16], 1e9, L,
                                       ALU.mult, ALU.subtract)
                e.drain()
                e.tensor_reduce(RT2[:, 1:2], RT2[:, 16:24], AX.X, ALU.min)
                e.drain()
                e.tensor_tensor(RT2[:, 2:3], RT2[:, 0:1], RT2[:, 1:2], ALU.add)
                e.drain()
                e.tensor_single_scalar(RT2[:, 16:17], RT2[:, 2:3], THETA, ALU.is_lt)
                e.drain()
                e.tensor_scalar(AMBC[:, t:t + 1], RT2[:, 16:17], -1.0, 1.0,
                                ALU.mult, ALU.add).then_inc(sv.h, 1)
            em("vector", route_v)
            oh_done = sv.inc()

            def cum_mm(e, t=t, oh_done=oh_done):
                e.wait_ge(sv.h, oh_done)
                e.matmul(PB[4][:, 0:9], UT[:, :], RT2[:, 8:17], start=True, stop=False)
                e.matmul(PB[4][:, 0:9], ONESR[0:1, :], BASE[:, 0:9],
                         start=False, stop=True)
                e.matmul(PB[3][0:1, 0:9], ONESC[:, 0:1], RT2[:, 8:17],
                         start=True, stop=False)
                e.matmul(PB[3][0:1, 0:9], ONESR[0:1, 0:1], BASE[:, 0:9],
                         start=False, stop=True).then_inc(st.h, 1)
            em("tensor", cum_mm)
            cum_done = st.inc()

            def rank_v(e, t=t, cum_done=cum_done):
                e.wait_ge(st.h, cum_done)
                e.tensor_copy(BASE[:, 0:9], PB[3][0:1, 0:9])
                e.tensor_tensor(RT2[:, 24:32], PB[4][:, 0:8], RT2[:, 8:16], ALU.mult)
                e.drain()
                e.reduce_sum(RT2[:, 3:4], RT2[:, 24:32], AX.X)
                e.drain()
                e.tensor_scalar(RT2[:, 4:5], RT2[:, 3:4], 1.0, float(CAP - 1),
                                ALU.subtract, ALU.min)
                e.drain()
                e.tensor_scalar(RT2[:, 5:6], IDXF[:, t:t + 1], float(CAP), RT2[:, 4:5],
                                ALU.mult, ALU.add)
                e.drain()
                e.tensor_copy(DESTI[:, t:t + 1], RT2[:, 5:6])
                e.tensor_scalar(RT2[:, 6:7], PB[4][:, 8:9], 1.0, float(NR - 1),
                                ALU.subtract, ALU.min)
                e.drain()
                e.tensor_scalar(GR[:, 16 * t:16 * (t + 1)], IOTA16[:, :], RT2[:, 6:7],
                                RT2[:, 16:17], ALU.is_equal, ALU.mult).then_inc(sv.h, 1)
            em("vector", rank_v)
            rank_done = sv.inc()

            def gt_tr(e, t=t, rank_done=rank_done):
                e.wait_ge(sv.h, rank_done)
                e.transpose(PB[2][0:16, 128 * (t % 4):128 * (t % 4) + 128],
                            GR[:, 16 * t:16 * (t + 1)], IDENT[:, :]).then_inc(st.h, 1)
            em("tensor", gt_tr)
            gt_done = st.inc()
            if t % 4 == 3:
                def gt_evac(e, t=t, gt_done=gt_done):
                    e.wait_ge(st.h, gt_done)
                    e.tensor_copy(GRT[:, 512 * (t // 4):512 * (t // 4 + 1)],
                                  PB[2][0:16, 0:512]).then_inc(sv.h, 1)
                em("vector", gt_evac)
                sv.inc()

            def scat(e, t=t, xn=xn, rank_done=rank_done):
                e.wait_ge(sv.h, rank_done)
                e.indirect_dma_start(
                    out=xs_h[:, :], out_offset=IndirectOffsetOnAxis(DESTI[:, t:t + 1], 0),
                    in_=xn, in_offset=None).then_inc(qg.h, 16)
            em("gpsimd", scat)
            scatter_tgt[t] = qg.inc(16)

        all_scattered = qg.n

        for t in range(ntile):
            buf = t % 2
            xn = XNB[:, D * buf:D * (buf + 1)]
            Xt = X[:, D * t:D * (t + 1)]

            def rgb(e, t=t, xn=xn, Xt=Xt):
                e.wait_ge(qg.h, all_scattered)
                if t - 2 in rgmm_tgt:
                    e.wait_ge(st.h, rgmm_tgt[t - 2])
                e.tensor_scalar(xn, Xt, STAT[:, t:t + 1], STAT[:, 16 + t:17 + t],
                                ALU.subtract, ALU.mult).then_inc(sv.h, 1)
            em("vector", rgb)
            rb_done = sv.inc()

            def rg_mm(e, t=t, xn=xn, rb_done=rb_done):
                e.wait_ge(sv.h, rb_done)
                e.matmul(PB[5][0:16, 0:512], GR[:, 16 * t:16 * (t + 1)], xn[:, 0:512],
                         start=(t == 0), stop=(t == ntile - 1))
                e.matmul(PB[6][0:16, 0:512], GR[:, 16 * t:16 * (t + 1)], xn[:, 512:1024],
                         start=(t == 0), stop=(t == ntile - 1))
                e.matmul(PB[7][0:16, 0:1], GR[:, 16 * t:16 * (t + 1)], IDXF[:, t:t + 1],
                         start=(t == 0), stop=(t == ntile - 1)).then_inc(st.h, 1)
            em("tensor", rg_mm)
            rgmm_tgt[t] = st.inc()
        rg_all = rgmm_tgt[ntile - 1]

        def rg_evac(e, rg_all=rg_all):
            e.wait_ge(st.h, rg_all)
            e.tensor_copy(XR[:, 0:512], PB[5][0:16, 0:512])
            e.tensor_copy(XR[:, 512:1024], PB[6][0:16, 0:512])
            e.tensor_scalar(OHR[:, :], IOTAE[0:16, :], PB[7][0:16, 0:1],
                            None, ALU.is_equal)
            e.memset(YRSEL[:, :], 0.0).then_inc(sv.h, 1)
        em("vector", rg_evac)
        xr_done = sv.inc()

        def xr_tr(e, xr_done=xr_done):
            e.wait_ge(sv.h, xr_done)
            for c in range(8):
                pb = PB[5] if c < 4 else PB[6]
                last = e.transpose(pb[0:128, 128 * (c % 4):128 * (c % 4) + 16],
                                   XR[:, 128 * c:128 * (c + 1)], IDENT[0:16, 0:16])
            last.then_inc(st.h, 1)
        em("tensor", xr_tr)
        xrt_mm = st.inc()

        def xrt_evac(e, xrt_mm=xrt_mm):
            e.wait_ge(st.h, xrt_mm)
            for c in range(8):
                pb = PB[5] if c < 4 else PB[6]
                last = e.tensor_copy(XRT[:, 16 * c:16 * (c + 1)],
                                     pb[0:128, 128 * (c % 4):128 * (c % 4) + 16])
            last.then_inc(sv.h, 1)
        em("vector", xrt_evac)
        xrt_done = sv.inc()

        gx_tgt = {}

        def gather_expert(eidx, wait_st=None):
            for i, (r0, rn) in enumerate(((0, 128), (128, 128), (256, 112))):
                def gdma(e, eidx=eidx, r0=r0, rn=rn, i=i, wait_st=wait_st):
                    if i == 0:
                        if wait_st is not None:
                            e.wait_ge(st.h, wait_st)
                        if eidx == 0:
                            e.wait_ge(qg.h, all_scattered)
                    e.dma_start(out=GX[0:rn, D * i:D * i + D],
                                in_=xs_h[eidx * CAP + r0:eidx * CAP + r0 + rn, :],
                                ).then_inc(qg.h, 16)
                em("gpsimd", gdma)
                tgt = qg.inc(16)
            return tgt

        xst_last_tr = {}
        gx_tgt[0] = gather_expert(0)
        prev_dn_evac = 0
        prev_b45 = 0
        ys_store_tgt = {}

        for eidx in range(E):
            ug_wt = {}
            for Ht in range(16):
                k = len(ug_seq)
                wbuf = (Ht % 2) * 2048
                first_gate = (k == 0 and Ht == 0)

                def wdma(e, eidx=eidx, Ht=Ht, wbuf=wbuf, k=k, fg=first_gate):
                    if k >= 2:
                        e.wait_ge(st.h, ug_seq[k - 2])
                        if rid_ug[k - 2] is not None:
                            e.wait_ge(sv.h, rid_ug[k - 2])
                    elif fg:
                        e.wait_ge(qg.h, xs_zeroed)
                    if bf16_exp:
                        e.dma_start(out=WUGb[:, 2 * wbuf:2 * wbuf + 2048],
                                    in_=eug_scr[eidx, Ht, :, 0:2048]).then_inc(qw.h, 16)
                        e.dma_start(out=WUGb[:, 2 * wbuf + 2048:2 * wbuf + 4096],
                                    in_=eug_scr[eidx, Ht, :, 2048:4096]).then_inc(qw.h, 16)
                    else:
                        e.dma_start(out=WUG[:, wbuf:wbuf + 1024].rearrange("p (c q) -> p c q", c=8),
                                    in_=upT_h[eidx, Ht].rearrange("c p q -> p c q")).then_inc(qw.h, 16)
                        e.dma_start(out=WUG[:, wbuf + 1024:wbuf + 2048].rearrange("p (c q) -> p c q", c=8),
                                    in_=gateT_h[eidx, Ht].rearrange("c p q -> p c q")).then_inc(qw.h, 16)
                em("sync", wdma)
                qw.inc(16)
                ug_wt[Ht] = qw.inc(16)
                ug_seq.append(None)
                rid_ug.append(None)
            dn_wt = {}
            for Ht in range(16):
                k = len(dn_seq)

                def ddma(e, eidx=eidx, Ht=Ht, k=k):
                    if k >= 2:
                        e.wait_ge(st.h, dn_seq[k - 2])
                        if rid_dn[k - 2] is not None:
                            e.wait_ge(sv.h, rid_dn[k - 2])
                    if bf16_exp:
                        e.dma_start(out=WDNb[:, 2048 * (Ht % 2):2048 * (Ht % 2) + 2048],
                                    in_=edn_scr[eidx, Ht]).then_inc(qw.h, 16)
                    else:
                        e.dma_start(out=WDN[:, D * (Ht % 2):D * (Ht % 2) + D],
                                    in_=downT_h[eidx, Ht]).then_inc(qw.h, 16)
                em("sync", ddma)
                dn_wt[Ht] = qw.inc(16)
                dn_seq.append(None)
                rid_dn.append(None)

            glt = gx_tgt[eidx]
            xst_mm = {}
            xst_ev = {}
            for c in range(8):
                def xst_tr(e, c=c, glt=glt, xrt_done=xrt_done, prev=prev_b45,
                           evp=xst_ev.get(c - 2)):
                    pb = PB[4] if c % 2 == 0 else PB[5]
                    if c == 0:
                        e.wait_ge(qg.h, glt)
                        e.wait_ge(sv.h, max(xrt_done, prev))
                    if evp is not None:
                        e.wait_ge(sv.h, evp)
                    e.transpose(pb[:, 0:128], GX[0:128, 128 * c:128 * c + 128],
                                IDENT[:, :])
                    e.transpose(pb[:, 128:256], GX[0:128, D + 128 * c:D + 128 * c + 128],
                                IDENT[:, :])
                    e.transpose(pb[:, 256:368], GX[0:112, 2 * D + 128 * c:2 * D + 128 * c + 128],
                                IDENT[0:112, 0:112]).then_inc(st.h, 1)
                em("tensor", xst_tr)
                xst_mm[c] = st.inc()

                def xst_evac(e, c=c, w=xst_mm[c], xrt_done=xrt_done):
                    pb = PB[4] if c % 2 == 0 else PB[5]
                    e.wait_ge(st.h, w)
                    if c == 0:
                        e.wait_ge(sv.h, xrt_done)
                    if bf16_exp:
                        s0, s1, s2 = SLOTS * c, SLOTS * c + 368, SLOTS * (c + 1)
                        e.tensor_copy(XSTH[:, s0:s1], pb[:, 0:368])
                        e.tensor_copy(XSTH[:, s1:s2], XRT[:, 16 * c:16 * (c + 1)])
                        e.drain()
                        e.tensor_tensor(XSTL[:, s0:s1], pb[:, 0:368],
                                        XSTH[:, s0:s1], ALU.subtract)
                        e.tensor_tensor(XSTL[:, s1:s2], XRT[:, 16 * c:16 * (c + 1)],
                                        XSTH[:, s1:s2], ALU.subtract).then_inc(sv.h, 1)
                    else:
                        e.tensor_copy(XST[:, SLOTS * c:SLOTS * c + 368], pb[:, 0:368])
                        e.tensor_copy(XST[:, SLOTS * c + 368:SLOTS * (c + 1)],
                                      XRT[:, 16 * c:16 * (c + 1)]).then_inc(sv.h, 1)
                em("vector", xst_evac)
                xst_ev[c] = sv.inc()
            xst_last_tr[eidx] = xst_mm[7]
            xst_ready = xst_ev[7]

            if eidx + 1 < E:
                gx_tgt[eidx + 1] = gather_expert(eidx + 1, wait_st=xst_mm[7])

            silu_done = {}
            hmul_done = {}
            hi_done = {}
            lo_done = {}
            for Ht in range(16):
                ub = PB[0] if Ht % 2 == 0 else PB[1]
                gb = PB[2] if Ht % 2 == 0 else PB[3]
                kk = len(ug_seq) - 16 + Ht

                def ug_mm(e, Ht=Ht, ub=ub, gb=gb, wt=ug_wt[Ht], xst_ready=xst_ready,
                          sd=silu_done.get(Ht - 2), hd=hmul_done.get(Ht - 2),
                          prev=prev_dn_evac):
                    e.wait_ge(qw.h, wt)
                    if Ht == 0:
                        e.wait_ge(sv.h, xst_ready)
                        e.wait_ge(sv.h, prev)
                    if sd is not None:
                        e.wait_ge(ss.h, sd)
                        e.wait_ge(sv.h, hd)
                    wbuf = (Ht % 2) * 2048
                    if bf16_exp:
                        base = 4096 * (Ht % 2)
                        for bank, gofs in ((ub, 0), (gb, 1024)):
                            for p, (wo, xsrc) in enumerate(((gofs, XSTH), (gofs, XSTL),
                                                            (2048 + gofs, XSTH))):
                                for c in range(8):
                                    last = e.matmul(
                                        bank[:, 0:SLOTS],
                                        WUGb[:, base + wo + 128 * c:base + wo + 128 * (c + 1)],
                                        xsrc[:, SLOTS * c:SLOTS * (c + 1)],
                                        start=(p == 0 and c == 0), stop=(p == 2 and c == 7))
                    else:
                        for c in range(8):
                            e.matmul(ub[:, 0:SLOTS], WUG[:, wbuf + 128 * c:wbuf + 128 * (c + 1)],
                                     XST[:, SLOTS * c:SLOTS * (c + 1)],
                                     start=(c == 0), stop=(c == 7))
                        for c in range(8):
                            last = e.matmul(gb[:, 0:SLOTS],
                                            WUG[:, wbuf + 1024 + 128 * c:wbuf + 1024 + 128 * (c + 1)],
                                            XST[:, SLOTS * c:SLOTS * (c + 1)],
                                            start=(c == 0), stop=(c == 7))
                    last.then_inc(st.h, 1)
                em("tensor", ug_mm)
                ug_done = st.inc()
                ug_seq[kk] = ug_done

                def silu_s(e, Ht=Ht, gb=gb, ug_done=ug_done):
                    e.wait_ge(st.h, ug_done)
                    so = 512 * (Ht % 2)
                    e.activation(SGS[:, so:so + SLOTS], gb[:, 0:SLOTS],
                                 ACT.Silu).then_inc(ss.h, 1)
                em("scalar", silu_s)
                silu_done[Ht] = ss.inc()

                if bf16_exp:
                    def hmul_v(e, Ht=Ht, ub=ub, sd=silu_done[Ht],
                               ph=hi_done.get(Ht - 2)):
                        e.wait_ge(ss.h, sd)
                        if ph is not None:
                            e.wait_ge(ss.h, ph)
                        so = 512 * (Ht % 2)
                        e.tensor_tensor(XNB[:, 1024 * (Ht % 2):1024 * (Ht % 2) + SLOTS],
                                        SGS[:, so:so + SLOTS], ub[:, 0:SLOTS],
                                        ALU.mult).then_inc(sv.h, 1)
                    em("vector", hmul_v)
                    hmul_done[Ht] = sv.inc()

                    def hi_s(e, Ht=Ht, hm=hmul_done[Ht]):
                        e.wait_ge(sv.h, hm)
                        e.activation(HALLH[:, SLOTS * Ht:SLOTS * (Ht + 1)],
                                     XNB[:, 1024 * (Ht % 2):1024 * (Ht % 2) + SLOTS],
                                     ACT.Copy).then_inc(ss.h, 1)
                    em("scalar", hi_s)
                    hi_done[Ht] = ss.inc()

                    def lo_v(e, Ht=Ht, hd2=hi_done[Ht]):
                        e.wait_ge(ss.h, hd2)
                        e.tensor_tensor(HALLL[:, SLOTS * Ht:SLOTS * (Ht + 1)],
                                        XNB[:, 1024 * (Ht % 2):1024 * (Ht % 2) + SLOTS],
                                        HALLH[:, SLOTS * Ht:SLOTS * (Ht + 1)],
                                        ALU.subtract).then_inc(sv.h, 1)
                    em("vector", lo_v)
                    lo_done[Ht] = sv.inc()
                else:
                    def hmul_v(e, Ht=Ht, ub=ub, sd=silu_done[Ht]):
                        e.wait_ge(ss.h, sd)
                        so = 512 * (Ht % 2)
                        e.tensor_tensor(HALL[:, SLOTS * Ht:SLOTS * (Ht + 1)],
                                        SGS[:, so:so + SLOTS], ub[:, 0:SLOTS],
                                        ALU.mult).then_inc(sv.h, 1)
                    em("vector", hmul_v)
                    hmul_done[Ht] = sv.inc()

                if riders:
                    def rug_hi(e, Ht=Ht, wt=ug_wt[Ht], prev=rid_prev["ug"],
                               xrt_done=xrt_done):
                        e.wait_ge(qw.h, wt)
                        if prev is not None:
                            e.wait_ge(qr.h, prev)
                        else:
                            e.wait_ge(sv.h, xrt_done)
                        wbuf = (Ht % 2) * 2048
                        e.activation(TSCb[:, 0:2048], WUG[:, wbuf:wbuf + 2048],
                                     ACT.Copy).then_inc(ss.h, 1)
                    em("scalar", rug_hi)
                    rh = ss.inc()

                    def rug_lo(e, Ht=Ht, rh=rh):
                        e.wait_ge(ss.h, rh)
                        wbuf = (Ht % 2) * 2048
                        e.tensor_tensor(XNBb[:, 0:2048], WUG[:, wbuf:wbuf + 2048],
                                        TSCb[:, 0:2048], ALU.subtract).then_inc(sv.h, 1)
                    em("vector", rug_lo)
                    rid_ug[kk] = sv.inc()

                    def rug_out(e, eidx=eidx, Ht=Ht, rl=rid_ug[kk]):
                        e.wait_ge(sv.h, rl)
                        e.dma_start(out=eug_scr[eidx, Ht, :, 0:2048],
                                    in_=TSCb[:, 0:2048]).then_inc(qr.h, 16)
                        e.dma_start(out=eug_scr[eidx, Ht, :, 2048:4096],
                                    in_=XNBb[:, 0:2048]).then_inc(qr.h, 16)
                    em("gpsimd", rug_out)
                    qr.inc(16)
                    rid_prev["ug"] = qr.inc(16)

            dn_ht = {}
            hall_ready = lo_done[15] if bf16_exp else hmul_done[15]
            for Ht in range(16):
                kk = len(dn_seq) - 16 + Ht

                def dn_mm(e, Ht=Ht, wt=dn_wt[Ht], hall_ready=hall_ready,
                          prev=prev_dn_evac):
                    e.wait_ge(qw.h, wt)
                    if Ht == 0:
                        e.wait_ge(sv.h, hall_ready)
                        e.wait_ge(sv.h, prev)
                    hofs = SLOTS * Ht
                    wofs = D * (Ht % 2)
                    if bf16_exp:
                        wb = 2048 * (Ht % 2)
                        for p, (hsrc, wo) in enumerate(((HALLH, 0), (HALLH, 1024),
                                                        (HALLL, 0))):
                            st_ = (Ht == 0 and p == 0)
                            sp_ = (Ht == 15 and p == 2)
                            for m in range(3):
                                for n in range(2):
                                    e.matmul(PB[m * 2 + n][0:128, 0:512],
                                             hsrc[:, hofs + 128 * m:hofs + 128 * (m + 1)],
                                             WDNb[:, wb + wo + 512 * n:wb + wo + 512 * (n + 1)],
                                             start=st_, stop=sp_)
                            for n in range(2):
                                last = e.matmul(PB[6 + n][0:16, 0:512],
                                                hsrc[:, hofs + 368:hofs + SLOTS],
                                                WDNb[:, wb + wo + 512 * n:wb + wo + 512 * (n + 1)],
                                                start=st_, stop=sp_)
                    else:
                        for m in range(3):
                            for n in range(2):
                                e.matmul(PB[m * 2 + n][0:128, 0:512],
                                         HALL[:, hofs + 128 * m:hofs + 128 * (m + 1)],
                                         WDN[:, wofs + 512 * n:wofs + 512 * (n + 1)],
                                         start=(Ht == 0), stop=(Ht == 15))
                        for n in range(2):
                            last = e.matmul(PB[6 + n][0:16, 0:512],
                                            HALL[:, hofs + 368:hofs + SLOTS],
                                            WDN[:, wofs + 512 * n:wofs + 512 * (n + 1)],
                                            start=(Ht == 0), stop=(Ht == 15))
                    last.then_inc(st.h, 1)
                em("tensor", dn_mm)
                dn_ht[Ht] = st.inc()
                dn_seq[kk] = dn_ht[Ht]

                if riders:
                    def rdn_hi(e, Ht=Ht, wt=dn_wt[Ht], prev=rid_prev["dn"],
                               xrt_done=xrt_done):
                        e.wait_ge(qw.h, wt)
                        if prev is not None:
                            e.wait_ge(qr.h, prev)
                        else:
                            e.wait_ge(sv.h, xrt_done)
                        wofs = D * (Ht % 2)
                        e.activation(XNBb[:, 2048:3072], WDN[:, wofs:wofs + D],
                                     ACT.Copy).then_inc(ss.h, 1)
                    em("scalar", rdn_hi)
                    rh = ss.inc()

                    def rdn_lo(e, Ht=Ht, rh=rh):
                        e.wait_ge(ss.h, rh)
                        wofs = D * (Ht % 2)
                        e.tensor_tensor(XNBb[:, 3072:4096], WDN[:, wofs:wofs + D],
                                        XNBb[:, 2048:3072], ALU.subtract).then_inc(sv.h, 1)
                    em("vector", rdn_lo)
                    rid_dn[kk] = sv.inc()

                    def rdn_out(e, eidx=eidx, Ht=Ht, rl=rid_dn[kk]):
                        e.wait_ge(sv.h, rl)
                        e.dma_start(out=edn_scr[eidx, Ht, :, 0:1024],
                                    in_=XNBb[:, 2048:3072]).then_inc(qr.h, 16)
                        e.dma_start(out=edn_scr[eidx, Ht, :, 1024:2048],
                                    in_=XNBb[:, 3072:4096]).then_inc(qr.h, 16)
                    em("gpsimd", rdn_out)
                    qr.inc(16)
                    rid_prev["dn"] = qr.inc(16)
            dn_done = dn_ht[15]

            def dn_evac(e, eidx=eidx, dn_done=dn_done,
                        prev_store=ys_store_tgt.get(eidx - 1)):
                e.wait_ge(st.h, dn_done)
                if prev_store is not None:
                    e.wait_ge(qw.h, prev_store)
                for m in (2, 0, 1):
                    rows = 128 if m < 2 else 112
                    e.tensor_copy(YSS[0:rows, 1024 * m:1024 * m + 512], PB[m * 2][0:rows, 0:512])
                    e.tensor_copy(YSS[0:rows, 1024 * m + 512:1024 * (m + 1)],
                                  PB[m * 2 + 1][0:rows, 0:512]).then_inc(sv.h, 1)
                e.scalar_tensor_tensor(YRSEL[:, 0:512], PB[6][0:16, 0:512],
                                       OHR[:, eidx:eidx + 1],
                                       YRSEL[:, 0:512], ALU.mult, ALU.add)
                e.drain()
                e.scalar_tensor_tensor(YRSEL[:, 512:1024], PB[7][0:16, 0:512],
                                       OHR[:, eidx:eidx + 1],
                                       YRSEL[:, 512:1024], ALU.mult, ALU.add).then_inc(sv.h, 1)
            em("vector", dn_evac)
            b45_free = sv.inc()
            yss_m = {2: b45_free, 0: sv.inc(), 1: sv.inc()}
            prev_b45 = b45_free
            prev_dn_evac = sv.inc()

            if debug and first_call and eidx == 0:
                def dbgdump0(e, w=hmul_done[15], w2=prev_dn_evac):
                    e.wait_ge(sv.h, w)
                    e.dma_start(out=dbg_hall_h[:, :], in_=HALL[:, :]).then_inc(qw.h, 16)
                    e.dma_start(out=dbg_xst_h[:, :], in_=XST[:, :]).then_inc(qw.h, 16)
                    e.wait_ge(sv.h, w2)
                    e.dma_start(out=dbg_wdn_h[:, :], in_=WDN[:, :]).then_inc(qw.h, 16)
                    e.dma_start(out=dbg_yss_h[0:112, :], in_=YSS[0:112, :]).then_inc(qw.h, 16)
                em("sync", dbgdump0)
                for _ in range(4):
                    qw.inc(16)

            for m in range(3):
                rows = 128 if m < 2 else 112

                def ysdma(e, eidx=eidx, m=m, rows=rows, w=yss_m[m]):
                    e.wait_ge(sv.h, w)
                    e.dma_start(out=ys_h[eidx * CAP + 128 * m:eidx * CAP + 128 * m + rows, :],
                                in_=YSS[0:rows, 1024 * m:1024 * (m + 1)]).then_inc(qw.h, 16)
                em("sync", ysdma)
                tgt = qw.inc(16)
            ys_store_tgt[eidx] = tgt

        experts_done_v = prev_dn_evac
        ys_all_stored = ys_store_tgt[E - 1]

        stage_dst = [YSS[:, 0:1024], YSS[:, 1024:2048], YSS[:, 2048:3072], TSC[:, 0:1024]]
        sq2h = {}
        xtr2_last = {}
        pend = None

        def flush_comb(pend):
            t0p, psev_d, ysg_p, base_idx, gx_drain = pend
            for m in range(4):
                t = t0p + m

                def comb_v(e, t=t, m=m, psev_d=psev_d, ysg_t=ysg_p[m],
                           gx_drain=gx_drain):
                    if m == 0:
                        e.wait_ge(sv.h, psev_d)
                        if gx_drain is not None:
                            e.wait_ge(qr.h, gx_drain)
                    e.wait_ge(qg.h, ysg_t)
                    stg = stage_dst[m]
                    for n in range(2):
                        e.scalar_tensor_tensor(stg[:, 512 * n:512 * (n + 1)],
                                               YSG[:, 512 * n:512 * (n + 1)],
                                               AMBC[:, t:t + 1],
                                               stg[:, 512 * n:512 * (n + 1)],
                                               ALU.mult, ALU.add)
                    e.drain()
                    for n in range(2):
                        Xsl = X[:, D * t + 512 * n:D * t + 512 * (n + 1)]
                        last = e.scalar_tensor_tensor(Xsl, stg[:, 512 * n:512 * (n + 1)],
                                                      0.5, Xsl, ALU.mult, ALU.add)
                    last.then_inc(sv.h, 1)
                em("vector", comb_v)
                comb_seq[base_idx + m] = sv.inc()
                if attn_stats_out is not None:
                    attn_stats_out[t] = attn_stats_tile(t, sq2h)

        for sI2 in range(T // 512):
            ubf = bf16_sh or (_F1SH_BF16 and riders and sI2 >= 1)
            scr_rdy_ug = rid_prev["sdn"] if (riders and sI2 >= 1) else None
            scr_rdy_dn = rid_prev["sdn"] if (riders and sI2 >= 1) else None
            t0 = sI2 * 4
            trd_last = {}
            last_ev = None
            for i in range(4):
                t = t0 + i
                xn = XNB[:, D * (i % 2):D * (i % 2 + 1)]
                Xt = X[:, D * t:D * (t + 1)]

                def rb2(e, xn=xn, Xt=Xt, t=t, sI2=sI2, i=i,
                        prevtr=xtr2_last.get(sI2 - 1), slottr=trd_last.get(i - 2),
                        rdrain=(rid_prev["dn"] if riders else None)):
                    if sI2 == 0 and i == 0:
                        e.wait_ge(st.h, rg_all)
                        if rdrain is not None:
                            e.wait_ge(qr.h, rdrain)
                    if i == 0 and prevtr is not None:
                        e.wait_ge(st.h, prevtr)
                    if slottr is not None:
                        e.wait_ge(st.h, slottr)
                    e.tensor_scalar(xn, Xt, STAT[:, t:t + 1], STAT[:, 16 + t:17 + t],
                                    ALU.subtract, ALU.mult).then_inc(sv.h, 1)
                em("vector", rb2)
                rb2_d = sv.inc()

                trd = {}
                ev2 = {}
                for c in range(8):
                    def xtr2(e, xn=xn, i=i, c=c, sI2=sI2, rb2_d=rb2_d,
                             prev=prev_b45, evp=ev2.get(c - 2)):
                        pb = PB[4] if c % 2 == 0 else PB[5]
                        if c == 0:
                            e.wait_ge(sv.h, rb2_d)
                            if sI2 == 0 and i == 0:
                                e.wait_ge(sv.h, prev)
                        if evp is not None:
                            e.wait_ge(sv.h, evp)
                        e.transpose(pb[:, 128 * i:128 * i + 128],
                                    xn[:, 128 * c:128 * (c + 1)],
                                    IDENT[:, :]).then_inc(st.h, 1)
                    em("tensor", xtr2)
                    trd[c] = st.inc()

                    def xtr2_ev(e, i=i, c=c, w=trd[c], ubf=ubf):
                        pb = PB[4] if c % 2 == 0 else PB[5]
                        e.wait_ge(st.h, w)
                        col = 512 * c + 128 * i
                        if ubf:
                            e.tensor_copy(XNTH[:, col:col + 128], pb[:, 128 * i:128 * i + 128])
                            e.drain()
                            e.tensor_tensor(XNTL[:, col:col + 128], pb[:, 128 * i:128 * i + 128],
                                            XNTH[:, col:col + 128], ALU.subtract).then_inc(sv.h, 1)
                        else:
                            e.tensor_copy(XNT[:, col:col + 128],
                                          pb[:, 128 * i:128 * i + 128]).then_inc(sv.h, 1)
                    em("vector", xtr2_ev)
                    ev2[c] = sv.inc()
                trd_last[i] = trd[7]
                last_ev = ev2[7]
            xtr2_last[sI2] = trd_last[3]
            xnt2_ready = last_ev

            sh_ug_wt = {}
            for Ht in range(16):
                k = len(ug_seq)
                wbuf = (Ht % 2) * 2048

                def wdma2(e, Ht=Ht, wbuf=wbuf, k=k, scr_rdy=scr_rdy_ug, ubf=ubf):
                    if k >= 2:
                        e.wait_ge(st.h, ug_seq[k - 2])
                        if rid_ug[k - 2] is not None:
                            e.wait_ge(sv.h, rid_ug[k - 2])
                    if scr_rdy is not None:
                        e.wait_ge(qr.h, scr_rdy)
                    if ubf:
                        e.dma_start(out=WUGb[:, 2 * wbuf:2 * wbuf + 2048],
                                    in_=sug_scr[Ht, :, 0:2048]).then_inc(qw.h, 16)
                        e.dma_start(out=WUGb[:, 2 * wbuf + 2048:2 * wbuf + 4096],
                                    in_=sug_scr[Ht, :, 2048:4096]).then_inc(qw.h, 16)
                    else:
                        e.dma_start(out=WUG[:, wbuf:wbuf + 1024].rearrange("p (c q) -> p c q", c=8),
                                    in_=wupT_h[Ht].rearrange("c p q -> p c q")).then_inc(qw.h, 16)
                        e.dma_start(out=WUG[:, wbuf + 1024:wbuf + 2048].rearrange("p (c q) -> p c q", c=8),
                                    in_=wgateT_h[Ht].rearrange("c p q -> p c q")).then_inc(qw.h, 16)
                em("sync", wdma2)
                qw.inc(16)
                sh_ug_wt[Ht] = qw.inc(16)
                ug_seq.append(None)
                rid_ug.append(None)
            sh_dn_wt = {}
            for Ht in range(16):
                k = len(dn_seq)

                def ddma2(e, Ht=Ht, k=k, scr_rdy=scr_rdy_dn, ubf=ubf):
                    if k >= 2:
                        e.wait_ge(st.h, dn_seq[k - 2])
                        if rid_dn[k - 2] is not None:
                            e.wait_ge(sv.h, rid_dn[k - 2])
                    if scr_rdy is not None:
                        e.wait_ge(qr.h, scr_rdy)
                    if ubf:
                        e.dma_start(out=WDNb[:, 2048 * (Ht % 2):2048 * (Ht % 2) + 2048],
                                    in_=sdn_scr[Ht]).then_inc(qw.h, 16)
                    else:
                        e.dma_start(out=WDN[:, D * (Ht % 2):D * (Ht % 2) + D],
                                    in_=wdownT_h[Ht]).then_inc(qw.h, 16)
                em("sync", ddma2)
                sh_dn_wt[Ht] = qw.inc(16)
                dn_seq.append(None)
                rid_dn.append(None)

            silu2_done = {}
            hmul2_done = {}
            hi2_done = {}
            lo2_done = {}
            for Ht in range(16):
                ub = PB[0] if Ht % 2 == 0 else PB[1]
                gb = PB[2] if Ht % 2 == 0 else PB[3]
                kk = len(ug_seq) - 16 + Ht

                def ug2_mm(e, Ht=Ht, ub=ub, gb=gb, wt=sh_ug_wt[Ht],
                           xnt2_ready=xnt2_ready, sd=silu2_done.get(Ht - 2),
                           hd=hmul2_done.get(Ht - 2), ubf=ubf):
                    e.wait_ge(qw.h, wt)
                    if Ht == 0:
                        e.wait_ge(sv.h, xnt2_ready)
                    if sd is not None:
                        e.wait_ge(ss.h, sd)
                        e.wait_ge(sv.h, hd)
                    wbuf = (Ht % 2) * 2048
                    if ubf:
                        base = 4096 * (Ht % 2)
                        for bank, gofs in ((ub, 0), (gb, 1024)):
                            for p, (wo, xsrc) in enumerate(((gofs, XNTH), (gofs, XNTL),
                                                            (2048 + gofs, XNTH))):
                                for c in range(8):
                                    last = e.matmul(
                                        bank[:, 0:512],
                                        WUGb[:, base + wo + 128 * c:base + wo + 128 * (c + 1)],
                                        xsrc[:, 512 * c:512 * (c + 1)],
                                        start=(p == 0 and c == 0), stop=(p == 2 and c == 7))
                    else:
                        for c in range(8):
                            e.matmul(ub[:, 0:512], WUG[:, wbuf + 128 * c:wbuf + 128 * (c + 1)],
                                     XNT[:, 512 * c:512 * (c + 1)], start=(c == 0), stop=(c == 7))
                        for c in range(8):
                            last = e.matmul(gb[:, 0:512],
                                            WUG[:, wbuf + 1024 + 128 * c:wbuf + 1024 + 128 * (c + 1)],
                                            XNT[:, 512 * c:512 * (c + 1)], start=(c == 0), stop=(c == 7))
                    last.then_inc(st.h, 1)
                em("tensor", ug2_mm)
                ug2_done = st.inc()
                ug_seq[kk] = ug2_done

                def silu2_s(e, Ht=Ht, gb=gb, ug2_done=ug2_done):
                    e.wait_ge(st.h, ug2_done)
                    so = 512 * (Ht % 2)
                    e.activation(SGS[:, so:so + 512], gb[:, 0:512], ACT.Silu).then_inc(ss.h, 1)
                em("scalar", silu2_s)
                silu2_done[Ht] = ss.inc()

                if ubf:
                    def hmul2_v(e, Ht=Ht, ub=ub, sd=silu2_done[Ht],
                                ph=hi2_done.get(Ht - 2)):
                        e.wait_ge(ss.h, sd)
                        if ph is not None:
                            e.wait_ge(ss.h, ph)
                        so = 512 * (Ht % 2)
                        e.tensor_tensor(XNB[:, 1024 * (Ht % 2):1024 * (Ht % 2) + 512],
                                        SGS[:, so:so + 512], ub[:, 0:512],
                                        ALU.mult).then_inc(sv.h, 1)
                    em("vector", hmul2_v)
                    hmul2_done[Ht] = sv.inc()

                    def hi2_s(e, Ht=Ht, hm=hmul2_done[Ht]):
                        e.wait_ge(sv.h, hm)
                        e.activation(HS2H[:, 512 * Ht:512 * (Ht + 1)],
                                     XNB[:, 1024 * (Ht % 2):1024 * (Ht % 2) + 512],
                                     ACT.Copy).then_inc(ss.h, 1)
                    em("scalar", hi2_s)
                    hi2_done[Ht] = ss.inc()

                    def lo2_v(e, Ht=Ht, hd2=hi2_done[Ht]):
                        e.wait_ge(ss.h, hd2)
                        e.tensor_tensor(HS2L[:, 512 * Ht:512 * (Ht + 1)],
                                        XNB[:, 1024 * (Ht % 2):1024 * (Ht % 2) + 512],
                                        HS2H[:, 512 * Ht:512 * (Ht + 1)],
                                        ALU.subtract).then_inc(sv.h, 1)
                    em("vector", lo2_v)
                    lo2_done[Ht] = sv.inc()
                else:
                    def hmul2_v(e, Ht=Ht, ub=ub, sd=silu2_done[Ht]):
                        e.wait_ge(ss.h, sd)
                        so = 512 * (Ht % 2)
                        e.tensor_tensor(HS2[:, 512 * Ht:512 * (Ht + 1)], SGS[:, so:so + 512],
                                        ub[:, 0:512], ALU.mult).then_inc(sv.h, 1)
                    em("vector", hmul2_v)
                    hmul2_done[Ht] = sv.inc()

                if riders and sI2 == 0:
                    def rsug_hi(e, Ht=Ht, wt=sh_ug_wt[Ht], prev=rid_prev["sug"],
                                gxfree=xst_last_tr[E - 1]):
                        e.wait_ge(qw.h, wt)
                        if prev is not None:
                            e.wait_ge(qr.h, prev)
                        else:
                            e.wait_ge(st.h, gxfree)
                        wbuf = (Ht % 2) * 2048
                        e.activation(GXb[:, 0:2048], WUG[:, wbuf:wbuf + 2048],
                                     ACT.Copy).then_inc(ss.h, 1)
                    em("scalar", rsug_hi)
                    rh = ss.inc()

                    def rsug_lo(e, Ht=Ht, rh=rh):
                        e.wait_ge(ss.h, rh)
                        wbuf = (Ht % 2) * 2048
                        e.tensor_tensor(GXb[:, 2048:4096], WUG[:, wbuf:wbuf + 2048],
                                        GXb[:, 0:2048], ALU.subtract).then_inc(sv.h, 1)
                    em("vector", rsug_lo)
                    rid_ug[kk] = sv.inc()

                    def rsug_out(e, Ht=Ht, rl=rid_ug[kk]):
                        e.wait_ge(sv.h, rl)
                        e.dma_start(out=sug_scr[Ht, :, 0:2048],
                                    in_=GXb[:, 0:2048]).then_inc(qr.h, 16)
                        e.dma_start(out=sug_scr[Ht, :, 2048:4096],
                                    in_=GXb[:, 2048:4096]).then_inc(qr.h, 16)
                    em("gpsimd", rsug_out)
                    qr.inc(16)
                    rid_prev["sug"] = qr.inc(16)

            if pend is not None:
                flush_comb(pend)

            sh_hall_ready = lo2_done[15] if ubf else hmul2_done[15]
            for Ht in range(16):
                kk = len(dn_seq) - 16 + Ht

                def dn2_ht(e, Ht=Ht, hall_ready=sh_hall_ready, ev=experts_done_v,
                           wt=sh_dn_wt[Ht], ubf=ubf):
                    e.wait_ge(qw.h, wt)
                    if Ht == 0:
                        e.wait_ge(sv.h, hall_ready)
                        e.wait_ge(sv.h, ev)
                    hofs = 512 * Ht
                    wofs = D * (Ht % 2)
                    if ubf:
                        wb = 2048 * (Ht % 2)
                        for p, (hsrc, wo) in enumerate(((HS2H, 0), (HS2H, 1024),
                                                        (HS2L, 0))):
                            st_ = (Ht == 0 and p == 0)
                            for m in range(4):
                                for n in range(2):
                                    last = e.matmul(
                                        PB[m * 2 + n][0:128, 0:512],
                                        hsrc[:, hofs + 128 * m:hofs + 128 * (m + 1)],
                                        WDNb[:, wb + wo + 512 * n:wb + wo + 512 * (n + 1)],
                                        start=st_, stop=False)
                    else:
                        for m in range(4):
                            for n in range(2):
                                last = e.matmul(PB[m * 2 + n][0:128, 0:512],
                                                HS2[:, hofs + 128 * m:hofs + 128 * (m + 1)],
                                                WDN[:, wofs + 512 * n:wofs + 512 * (n + 1)],
                                                start=(Ht == 0), stop=False)
                    last.then_inc(st.h, 1)
                em("tensor", dn2_ht)
                dn_seq[kk] = st.inc()

                if riders and sI2 == 0:
                    def rsdn_hi(e, Ht=Ht, wt=sh_dn_wt[Ht], prev=rid_prev["sdn"],
                                gxfree=xst_last_tr[E - 1]):
                        e.wait_ge(qw.h, wt)
                        if prev is not None:
                            e.wait_ge(qr.h, prev)
                        else:
                            e.wait_ge(st.h, gxfree)
                        wofs = D * (Ht % 2)
                        e.activation(GXb[:, 4096:5120], WDN[:, wofs:wofs + D],
                                     ACT.Copy).then_inc(ss.h, 1)
                    em("scalar", rsdn_hi)
                    rh = ss.inc()

                    def rsdn_lo(e, Ht=Ht, rh=rh):
                        e.wait_ge(ss.h, rh)
                        wofs = D * (Ht % 2)
                        e.tensor_tensor(GXb[:, 5120:6144], WDN[:, wofs:wofs + D],
                                        GXb[:, 4096:5120], ALU.subtract).then_inc(sv.h, 1)
                    em("vector", rsdn_lo)
                    rid_dn[kk] = sv.inc()

                    def rsdn_out(e, Ht=Ht, rl=rid_dn[kk]):
                        e.wait_ge(sv.h, rl)
                        e.dma_start(out=sdn_scr[Ht, :, 0:1024],
                                    in_=GXb[:, 4096:5120]).then_inc(qr.h, 16)
                        e.dma_start(out=sdn_scr[Ht, :, 1024:2048],
                                    in_=GXb[:, 5120:6144]).then_inc(qr.h, 16)
                    em("gpsimd", rsdn_out)
                    qr.inc(16)
                    rid_prev["sdn"] = qr.inc(16)

            def dn2_fin(e, t0=t0):
                for m in range(4):
                    t = t0 + m
                    for n in range(2):
                        last = e.matmul(PB[m * 2 + n][0:128, 0:512],
                                        GRT[:, 128 * t:128 * (t + 1)],
                                        YRSEL[:, 512 * n:512 * (n + 1)],
                                        start=False, stop=True)
                last.then_inc(st.h, 1)
            em("tensor", dn2_fin)
            dn2_done = st.inc()

            ysg_tgts = []
            for i in range(4):
                t = t0 + i
                kcomb = len(comb_seq)

                def ysga(e, t=t, kcomb=kcomb):
                    e.wait_ge(qw.h, ys_all_stored)
                    if kcomb >= 1:
                        e.wait_ge(sv.h, comb_seq[kcomb - 1])
                    e.indirect_dma_start(
                        out=YSG[:, 0:D], out_offset=None,
                        in_=ys_h[:, :], in_offset=IndirectOffsetOnAxis(DESTI[:, t:t + 1], 0),
                    ).then_inc(qg.h, 16)
                em("gpsimd", ysga)
                ysg_tgts.append(qg.inc(16))
                comb_seq.append(None)
            base_idx = len(comb_seq) - 4

            def psev(e, sI2=sI2, dn2_done=dn2_done, yas=ys_all_stored):
                e.wait_ge(st.h, dn2_done)
                if sI2 == 0:
                    e.wait_ge(qw.h, yas)
                for m in range(4):
                    stg = stage_dst[m]
                    for n in range(2):
                        last = e.tensor_copy(stg[:, 512 * n:512 * (n + 1)],
                                             PB[m * 2 + n][0:128, 0:512])
                last.then_inc(sv.h, 1)
            em("vector", psev)
            psev_d = sv.inc()
            pend = (t0, psev_d, ysg_tgts, base_idx,
                    rid_prev["sdn"] if (riders and sI2 == 0) else None)

        flush_comb(pend)
        return sv.n

    def attn_stage(level=5, stats_pre=None, emit_f3=False):
        em("vector", lambda e: e.memset(SST[:, :], 0.0).then_inc(sv.h, 1))
        sst_zeroed = sv.inc()
        dma("sync", GAM, gam_h[:, :], qw)
        gb_ld = dma("sync", BET, bet_h[:, :], qw)
        ba_ld = dma("sync", BA, ba_h[:, :], qw)
        for c in range(4):
            dma("sync", WOS[:, D * c:D * (c + 1)], wo_h[128 * c:128 * (c + 1), :], qw)
        attld = qw.n

        if level < 2:
            return None
        f3_stats = {} if emit_f3 else None
        if stats_pre is not None:
            stats_ready = stats_pre
        else:
            stats_ready = {}
            sq2_tgt = {}
            for t in range(ntile):
                par = 16 * (t % 2)
                Xt = X[:, D * t:D * (t + 1)]
                xl = XNB[:, D * (t % 2):D * (t % 2 + 1)]

                def st_v(e, t=t, Xt=Xt, par=par):
                    e.reduce_sum(RT1[:, par:par + 1], Xt, AX.X)
                    e.drain()
                    e.tensor_scalar(STAT[:, t:t + 1], RT1[:, par:par + 1], 1.0 / D,
                                    None, ALU.mult).then_inc(sv.h, 1)
                em("vector", st_v)
                m1d = sv.inc()

                def st_s(e, Xt=Xt, par=par, m1d=m1d):
                    e.wait_ge(sv.h, m1d)
                    e.activation(SGS[:, :], Xt, ACT.Square,
                                 accum_out=RT1[:, par + 1:par + 2]).then_inc(ss.h, 1)
                em("scalar", st_s)
                sq1 = ss.inc()

                def var1_v(e, t=t, par=par, sq1=sq1):
                    e.wait_ge(ss.h, sq1)
                    e.tensor_tensor(RT1[:, par + 2:par + 3], STAT[:, t:t + 1],
                                    STAT[:, t:t + 1], ALU.mult)
                    e.drain()
                    e.tensor_scalar(RT1[:, par + 3:par + 4], RT1[:, par + 1:par + 2],
                                    1.0 / D, RT1[:, par + 2:par + 3],
                                    ALU.mult, ALU.subtract).then_inc(sv.h, 1)
                em("vector", var1_v)
                v1d = sv.inc()

                def sq1_s(e, par=par, v1d=v1d):
                    e.wait_ge(sv.h, v1d)
                    e.activation(RT1[:, par + 4:par + 5], RT1[:, par + 3:par + 4],
                                 ACT.Sqrt, bias=CST[:, 1:2]).then_inc(ss.h, 1)
                em("scalar", sq1_s)
                s1d = ss.inc()

                def xl_v(e, t=t, par=par, Xt=Xt, xl=xl, s1d=s1d,
                         prev_sq2=sq2_tgt.get(t - 2)):
                    e.wait_ge(ss.h, s1d)
                    if prev_sq2 is not None:
                        e.wait_ge(ss.h, prev_sq2)
                    e.reciprocal(STAT[:, 16 + t:17 + t], RT1[:, par + 4:par + 5])
                    e.drain()
                    e.tensor_scalar(xl, Xt, STAT[:, t:t + 1], STAT[:, 16 + t:17 + t],
                                    ALU.subtract, ALU.mult)
                    e.drain()
                    e.reduce_sum(RT1[:, par:par + 1], xl, AX.X)
                    e.drain()
                    e.tensor_scalar(STAT[:, 32 + t:33 + t], RT1[:, par:par + 1], 1.0 / D,
                                    None, ALU.mult).then_inc(sv.h, 1)
                em("vector", xl_v)
                m2d = sv.inc()

                def sq2_s(e, xl=xl, par=par, m2d=m2d):
                    e.wait_ge(sv.h, m2d)
                    e.activation(SGS[:, :], xl, ACT.Square,
                                 accum_out=RT1[:, par + 1:par + 2]).then_inc(ss.h, 1)
                em("scalar", sq2_s)
                sq2 = ss.inc()
                sq2_tgt[t] = sq2

                def var2_v(e, t=t, par=par, sq2=sq2):
                    e.wait_ge(ss.h, sq2)
                    e.tensor_tensor(RT1[:, par + 2:par + 3], STAT[:, 32 + t:33 + t],
                                    STAT[:, 32 + t:33 + t], ALU.mult)
                    e.drain()
                    e.tensor_scalar(RT1[:, par + 3:par + 4], RT1[:, par + 1:par + 2],
                                    1.0 / D, RT1[:, par + 2:par + 3],
                                    ALU.mult, ALU.subtract).then_inc(sv.h, 1)
                em("vector", var2_v)
                v2d = sv.inc()

                def sq2b_s(e, par=par, v2d=v2d):
                    e.wait_ge(sv.h, v2d)
                    e.activation(RT1[:, par + 5:par + 6], RT1[:, par + 3:par + 4],
                                 ACT.Sqrt, bias=CST[:, 1:2]).then_inc(ss.h, 1)
                em("scalar", sq2b_s)
                s2d = ss.inc()

                def r2_v(e, t=t, par=par, s2d=s2d):
                    e.wait_ge(ss.h, s2d)
                    e.reciprocal(STAT[:, 48 + t:49 + t], RT1[:, par + 5:par + 6]).then_inc(sv.h, 1)
                em("vector", r2_v)
                stats_ready[t] = sv.inc()

        if level < 3:
            return f3_stats
        s_upd_prev = {h: sst_zeroed for h in range(NH)}
        last_comb = None
        pref_tgt = {}
        for sI in range(T // 512):
            for i in range(4):
                t = sI * 4 + i
                Xt = X[:, D * t:D * (t + 1)]
                pre = pref_tgt.get((sI, i))
                if pre is not None:
                    xin_d = pre
                    xsrc = XNB[:, 1024 * i:1024 * (i + 1)]
                else:
                    xsrc = TSC

                def xin_v(e, t=t, Xt=Xt, sr=stats_ready[t], gb_ld=gb_ld):
                    e.wait_ge(sv.h, sr)
                    e.wait_ge(qw.h, gb_ld)
                    e.tensor_scalar(TSC[:, :], Xt, STAT[:, t:t + 1], STAT[:, 16 + t:17 + t],
                                    ALU.subtract, ALU.mult)
                    e.drain()
                    e.tensor_scalar(TSC[:, :], TSC[:, :], STAT[:, 32 + t:33 + t],
                                    STAT[:, 48 + t:49 + t], ALU.subtract, ALU.mult)
                    e.drain()
                    e.tensor_tensor(TSC[:, :], TSC[:, :], GAM, ALU.mult)
                    e.drain()
                    e.tensor_tensor(TSC[:, :], TSC[:, :], BET, ALU.add).then_inc(sv.h, 1)
                if pre is None:
                    em("vector", xin_v)
                else:
                    def xin_nop(e, sr=stats_ready[t]):
                        e.wait_ge(sv.h, sr)
                        e.nop().then_inc(sv.h, 1)
                    em("vector", xin_nop)
                xin_d = sv.inc()

                trd = {}
                evx = {}
                for c in range(8):
                    def xin_tr(e, i=i, c=c, xin_d=xin_d, evp=evx.get(c - 2), xsrc=xsrc):
                        pb = PB[4] if c % 2 == 0 else PB[5]
                        if c == 0:
                            e.wait_ge(sv.h, xin_d)
                        if evp is not None:
                            e.wait_ge(sv.h, evp)
                        e.transpose(pb[:, 128 * i:128 * i + 128], xsrc[:, 128 * c:128 * (c + 1)],
                                    IDENT[:, :]).then_inc(st.h, 1)
                    em("tensor", xin_tr)
                    trd[c] = st.inc()

                    def xin_ev(e, i=i, c=c, w=trd[c]):
                        pb = PB[4] if c % 2 == 0 else PB[5]
                        e.wait_ge(st.h, w)
                        e.tensor_copy(XNT[:, 512 * c + 128 * i:512 * c + 128 * (i + 1)],
                                      pb[:, 128 * i:128 * i + 128]).then_inc(sv.h, 1)
                    em("vector", xin_ev)
                    evx[c] = sv.inc()
                em("vector", lambda e: e.nop().then_inc(sv.h, 1))
                sv.inc()
            xnt_ready = sv.n

            proj_evac = {}
            for wi, (wh, dest) in enumerate(((wq_h, QS), (wk_h, KS), (wv_h, VS), (wa_h, ALA))):
                ctgts = {}
                for c in range(8):
                    k = len(wat_seq)

                    def pwdma(e, wh=wh, c=c, k=k):
                        if k >= 2:
                            e.wait_ge(st.h, wat_seq[k - 2])
                        e.dma_start(out=WAT[:, DI * (c % 2):DI * (c % 2) + DI],
                                    in_=wh[128 * c:128 * (c + 1), :]).then_inc(qw.h, 16)
                    em("sync", pwdma)
                    ctgts[c] = qw.inc(16)
                    wat_seq.append(None)

                pj_done = None
                for c in range(8):
                    kk = len(wat_seq) - 8 + c

                    def proj_mm(e, wi=wi, c=c, wt=ctgts[c], xnt_ready=xnt_ready,
                                pe=proj_evac.get(wi - 1)):
                        e.wait_ge(qw.h, wt)
                        if c == 0:
                            if wi == 0:
                                e.wait_ge(sv.h, xnt_ready)
                            else:
                                e.wait_ge(sv.h, pe)
                        for i in range(4):
                            last = e.matmul(PB[i][:, 0:DI],
                                            XNT[:, 512 * c + 128 * i:512 * c + 128 * (i + 1)],
                                            WAT[:, DI * (c % 2):DI * (c % 2) + DI],
                                            start=(c == 0), stop=(c == 7))
                        last.then_inc(st.h, 1)
                    em("tensor", proj_mm)
                    pj_done = st.inc()
                    wat_seq[kk] = pj_done

                def proj_ev(e, dest=dest, wi=wi, pj_done=pj_done, ba_ld=ba_ld):
                    e.wait_ge(st.h, pj_done)
                    if wi == 3:
                        e.wait_ge(qw.h, ba_ld)
                    for i in range(4):
                        if wi == 3:
                            last = e.scalar_tensor_tensor(dest[:, DI * i:DI * (i + 1)],
                                                          PB[i][:, 0:DI], 1.0, BA,
                                                          ALU.mult, ALU.add)
                        else:
                            last = e.tensor_copy(dest[:, DI * i:DI * (i + 1)], PB[i][:, 0:DI])
                    last.then_inc(sv.h, 1)
                em("vector", proj_ev)
                proj_evac[wi] = sv.inc()
            proj_all = proj_evac[3]

            if level < 4:
                continue

            sig_d = {}
            la_d = {}
            for i2 in range(4):
                def sig_s(e, i2=i2, proj_all=proj_all):
                    if i2 == 0:
                        e.wait_ge(sv.h, proj_all)
                    e.activation(ALA[:, DI * i2:DI * (i2 + 1)], ALA[:, DI * i2:DI * (i2 + 1)],
                                 ACT.Sigmoid).then_inc(ss.h, 1)
                em("scalar", sig_s)
                sig_d[i2] = ss.inc()
            for i2 in range(4):
                def ln2_s(e, i2=i2):
                    e.activation(ALA[:, DI * i2:DI * (i2 + 1)], ALA[:, DI * i2:DI * (i2 + 1)],
                                 ACT.Ln).then_inc(ss.h, 1)
                em("scalar", ln2_s)
                la_d[i2] = ss.inc()

            knorm_d = {}
            ksq_tgt = {}
            for i in range(4):
                kp = 16 * (i % 2)

                def kss_v(e, i=i, kp=kp, proj_all=proj_all, prev_ksq=ksq_tgt.get(i - 2)):
                    e.wait_ge(sv.h, proj_all)
                    if prev_ksq is not None:
                        e.wait_ge(ss.h, prev_ksq)
                    e.tensor_tensor(TSC[:, 0:DI], KS[:, DI * i:DI * (i + 1)],
                                    KS[:, DI * i:DI * (i + 1)], ALU.mult)
                    e.drain()
                    for h in range(NH):
                        last = e.reduce_sum(KNRM[:, kp + h:kp + h + 1],
                                            TSC[:, 64 * h:64 * (h + 1)], AX.X)
                    last.then_inc(sv.h, 1)
                em("vector", kss_v)
                kss_d = sv.inc()

                def ksq_s(e, kp=kp, kss_d=kss_d):
                    e.wait_ge(sv.h, kss_d)
                    e.activation(KNRM[:, kp + 8:kp + 16], KNRM[:, kp:kp + 8],
                                 ACT.Sqrt).then_inc(ss.h, 1)
                em("scalar", ksq_s)
                ksq_d = ss.inc()
                ksq_tgt[i] = ksq_d

                def khat_v(e, i=i, kp=kp, ksq_d=ksq_d):
                    e.wait_ge(ss.h, ksq_d)
                    e.tensor_scalar_max(KNRM[:, kp + 8:kp + 16], KNRM[:, kp + 8:kp + 16], 1e-12)
                    e.drain()
                    e.reciprocal(KNRM[:, kp:kp + 8], KNRM[:, kp + 8:kp + 16])
                    e.drain()
                    for h in range(NH):
                        sl = slice(DI * i + 64 * h, DI * i + 64 * (h + 1))
                        last = e.tensor_scalar(KS[:, sl], KS[:, sl], KNRM[:, kp + h:kp + h + 1],
                                               None, ALU.mult)
                    last.then_inc(sv.h, 1)
                em("vector", khat_v)
                knorm_d[i] = sv.inc()

            if level < 5:
                continue
            chunk_ready = {}
            dec_tgt = {}
            kbs_tgt = {}
            for i in range(4):
                la = ALA[:, DI * i:DI * (i + 1)]
                qb = (i % 2) * NH * 128

                def cs_mm(e, i=i, la=la, la_d=la_d[i], kd=knorm_d[i],
                          lcomb=last_comb, prev_dec=dec_tgt.get(i - 1)):
                    e.wait_ge(ss.h, la_d)
                    e.wait_ge(sv.h, kd)
                    if lcomb is not None:
                        e.wait_ge(sv.h, lcomb)
                    if prev_dec is not None:
                        e.wait_ge(ss.h, prev_dec)
                    pbc = PB[4] if i % 2 == 0 else PB[5]
                    e.matmul(pbc[:, 0:DI], UT[:, :], la, start=True, stop=True)
                    for h in range(NH):
                        last = e.matmul(PB[2][0:64, 8 * i + h:8 * i + h + 1],
                                        la[:, 64 * h:64 * (h + 1)], ONESC[:, 0:1],
                                        start=(h == 0), stop=(h == NH - 1))
                    last.then_inc(st.h, 1)
                em("tensor", cs_mm)
                csd = st.inc()

                def cs_ev(e, i=i, csd=csd):
                    e.wait_ge(st.h, csd)
                    pbc = PB[4] if i % 2 == 0 else PB[5]
                    e.tensor_copy(CSB[:, DI * (i % 2):DI * (i % 2 + 1)], pbc[:, 0:DI]).then_inc(sv.h, 1)
                em("vector", cs_ev)
                cs_done = sv.inc()

                def dec_s(e, i=i, csd=csd):
                    e.wait_ge(st.h, csd)
                    e.activation(SDEC[:, 8 * i:8 * (i + 1)], PB[2][0:64, 8 * i:8 * (i + 1)],
                                 ACT.Exp).then_inc(ss.h, 1)
                em("scalar", dec_s)
                dec_tgt[i] = ss.inc()

                def e12_s(e, i=i, cs_done=cs_done):
                    e.wait_ge(sv.h, cs_done)
                    e.activation(EF[:, 0:DI], CSB[:, DI * (i % 2):DI * (i % 2 + 1)], ACT.Exp)
                    e.drain()
                    e.activation(EF[:, DI:2 * DI], CSB[:, DI * (i % 2):DI * (i % 2 + 1)], ACT.Exp,
                                 scale=-1.0).then_inc(ss.h, 1)
                em("scalar", e12_s)
                e12_d = ss.inc()

                def qk_v(e, i=i, e12_d=e12_d):
                    e.wait_ge(ss.h, e12_d)
                    sl = slice(DI * i, DI * (i + 1))
                    e.tensor_tensor(QS[:, sl], QS[:, sl], EF[:, 0:DI], ALU.mult)
                    e.drain()
                    e.tensor_tensor(KS[:, sl], KS[:, sl], EF[:, DI:2 * DI],
                                    ALU.mult).then_inc(sv.h, 1)
                em("vector", qk_v)
                qk_d = sv.inc()

                def clb_mm(e, i=i, la=la, qk_d=qk_d, prev_kbs=kbs_tgt.get(i - 1)):
                    e.wait_ge(sv.h, qk_d)
                    if prev_kbs is not None:
                        e.wait_ge(ss.h, prev_kbs)
                    e.matmul(PB[7][0:1, 0:DI], ONESC[:, 0:1], la,
                             start=True, stop=True).then_inc(st.h, 1)
                em("tensor", clb_mm)
                clb_d = st.inc()

                def clb_ev(e, clb_d=clb_d):
                    e.wait_ge(st.h, clb_d)
                    e.tensor_copy(EF[0:1, 0:DI], PB[7][0:1, 0:DI]).then_inc(sv.h, 1)
                em("vector", clb_ev)
                clb_e = sv.inc()

                def clb2_mm(e, clb_e=clb_e):
                    e.wait_ge(sv.h, clb_e)
                    e.matmul(PB[7][:, 0:DI], ONESR[0:1, :], EF[0:1, 0:DI],
                             start=True, stop=True).then_inc(st.h, 1)
                em("tensor", clb2_mm)
                clb2_d = st.inc()

                def kb_s(e, i=i, clb2_d=clb2_d):
                    e.wait_ge(st.h, clb2_d)
                    e.activation(KBE[:, DI * (i % 2):DI * (i % 2 + 1)], PB[7][:, 0:DI],
                                 ACT.Exp).then_inc(ss.h, 1)
                em("scalar", kb_s)
                kbs_tgt[i] = ss.inc()

                def kb2_v(e, i=i, kbs_d=kbs_tgt[i]):
                    e.wait_ge(ss.h, kbs_d)
                    sl = slice(DI * i, DI * (i + 1))
                    e.tensor_tensor(KBB[:, sl], KS[:, sl],
                                    KBE[:, DI * (i % 2):DI * (i % 2 + 1)],
                                    ALU.mult).then_inc(sv.h, 1)
                em("vector", kb2_v)
                kb_done = sv.inc()

                def f_tr(e, i=i, kb_done=kb_done):
                    e.wait_ge(sv.h, kb_done)
                    for h in range(NH):
                        pb = PB[4] if h < 4 else PB[5]
                        e.transpose(pb[0:64, 128 * (h % 4):128 * (h % 4) + 128],
                                    QS[:, DI * i + 64 * h:DI * i + 64 * (h + 1)], IDENT[:, :])
                    e.nop().then_inc(st.h, 1)
                    for h in range(NH):
                        pb = PB[6] if h < 4 else PB[7]
                        last = e.transpose(pb[0:64, 128 * (h % 4):128 * (h % 4) + 128],
                                           KS[:, DI * i + 64 * h:DI * i + 64 * (h + 1)], IDENT[:, :])
                    last.then_inc(st.h, 1)
                em("tensor", f_tr)
                st.inc()
                ftr2 = st.inc()

                def f_ev(e, i=i, qb=qb, ftr2=ftr2):
                    e.wait_ge(st.h, ftr2)
                    e.tensor_copy(QTT[:, qb:qb + 512], PB[4][0:64, 0:512])
                    e.tensor_copy(QTT[:, qb + 512:qb + 1024], PB[5][0:64, 0:512])
                    e.tensor_copy(KHT[:, qb:qb + 512], PB[6][0:64, 0:512])
                    last = e.tensor_copy(KHT[:, qb + 512:qb + 1024], PB[7][0:64, 0:512])
                    last.then_inc(sv.h, 1)
                em("vector", f_ev)
                chunk_ready[i] = sv.inc()

                if level < 6:
                    continue
                tglob = sI * 4 + i
                pm_evac = {}
                o_evac = {}
                s_upd = {}
                for h in range(NH):
                    pbP = PB[6] if h % 2 == 0 else PB[7]
                    pbO = PB[4] if h % 2 == 0 else PB[5]
                    pbS = PB[2] if h % 2 == 0 else PB[3]

                    def p_mm(e, h=h, qb=qb, pbP=pbP, cr=chunk_ready[i], pe=pm_evac.get(h - 2)):
                        if h == 0:
                            e.wait_ge(sv.h, cr)
                        if pe is not None:
                            e.wait_ge(sv.h, pe)
                        e.matmul(pbP[:, 0:128], KHT[:, qb + 128 * h:qb + 128 * (h + 1)],
                                 QTT[:, qb + 128 * h:qb + 128 * (h + 1)],
                                 start=True, stop=True).then_inc(st.h, 1)
                    em("tensor", p_mm)
                    p_d = st.inc()

                    def pm_v(e, pbP=pbP, p_d=p_d):
                        e.wait_ge(st.h, p_d)
                        e.tensor_tensor(PMS[:, :], pbP[:, 0:128], UT[:, :],
                                        ALU.mult).then_inc(sv.h, 1)
                    em("vector", pm_v)
                    pm_evac[h] = sv.inc()

                    def o_mm(e, i=i, h=h, qb=qb, pbO=pbO, pmd=pm_evac[h],
                             oe=o_evac.get(h - 2), su=s_upd_prev[h]):
                        e.wait_ge(sv.h, pmd)
                        if oe is not None:
                            e.wait_ge(sv.h, oe)
                        if su is not None:
                            e.wait_ge(sv.h, su)
                        e.matmul(pbO[0:64, 0:128], VS[:, DI * i + 64 * h:DI * i + 64 * (h + 1)],
                                 PMS[:, :], start=True, stop=False)
                        e.matmul(pbO[0:64, 0:128], SST[:, 64 * h:64 * (h + 1)],
                                 QTT[:, qb + 128 * h:qb + 128 * (h + 1)],
                                 start=False, stop=True).then_inc(st.h, 1)
                    em("tensor", o_mm)
                    o_d = st.inc()

                    def o_ev(e, h=h, pbO=pbO, o_d=o_d):
                        e.wait_ge(st.h, o_d)
                        pair = h // 2
                        if h % 2 == 0:
                            e.tensor_copy(OTS[0:64, 128 * pair:128 * (pair + 1)],
                                          pbO[0:64, 0:128]).then_inc(sv.h, 1)
                        else:
                            e.tensor_copy(OTS[64:128, 128 * pair:128 * (pair + 1)],
                                          pbO[0:64, 0:128]).then_inc(sv.h, 1)
                    em("vector", o_ev)
                    o_evac[h] = sv.inc()

                    def s_mm(e, i=i, h=h, pbS=pbS, su=s_upd.get(h - 2), dci=dec_tgt[i]):
                        if h < 2:
                            e.wait_ge(ss.h, dci)
                        if su is not None:
                            e.wait_ge(sv.h, su)
                        e.matmul(pbS[0:64, 0:64], KBB[:, DI * i + 64 * h:DI * i + 64 * (h + 1)],
                                 VS[:, DI * i + 64 * h:DI * i + 64 * (h + 1)],
                                 start=True, stop=True).then_inc(st.h, 1)
                    em("tensor", s_mm)
                    smm_d = st.inc()

                    def s_v(e, i=i, h=h, pbS=pbS, smm_d=smm_d, od=o_evac[h]):
                        e.wait_ge(st.h, smm_d)
                        e.wait_ge(sv.h, od)
                        e.scalar_tensor_tensor(SST[:, 64 * h:64 * (h + 1)],
                                               SST[:, 64 * h:64 * (h + 1)],
                                               SDEC[:, 8 * i + h:8 * i + h + 1],
                                               pbS[0:64, 0:64],
                                               ALU.mult, ALU.add).then_inc(sv.h, 1)
                    em("vector", s_v)
                    s_upd[h] = sv.inc()

                s_upd_prev = dict(s_upd)

                def wo_mm(e, oe=o_evac[NH - 1], attld=attld):
                    e.wait_ge(sv.h, oe)
                    e.wait_ge(qw.h, attld)
                    for n in range(2):
                        for c in range(4):
                            last = e.matmul(PB[n][:, 0:512], OTS[:, 128 * c:128 * (c + 1)],
                                            WOS[:, D * c + 512 * n:D * c + 512 * (n + 1)],
                                            start=(c == 0), stop=(c == 3))
                    last.then_inc(st.h, 1)
                em("tensor", wo_mm)
                wo_d = st.inc()

                def att_comb(e, tglob=tglob, wo_d=wo_d):
                    e.wait_ge(st.h, wo_d)
                    t = tglob
                    Xt = X[:, D * t:D * (t + 1)]
                    e.tensor_scalar(TSC[:, :], Xt, STAT[:, t:t + 1], STAT[:, 16 + t:17 + t],
                                    ALU.subtract, ALU.mult)
                    e.drain()
                    e.tensor_tensor(Xt, Xt, TSC[:, :], ALU.add)
                    e.drain()
                    for n in range(2):
                        sl = slice(D * t + 512 * n, D * t + 512 * (n + 1))
                        last = e.tensor_tensor(X[:, sl], X[:, sl], PB[n][:, 0:512], ALU.add)
                    last.then_inc(sv.h, 1)
                em("vector", att_comb)
                last_comb = sv.inc()

                if emit_f3 and i == 3:
                    for i3 in range(4):
                        tg = sI * 4 + i3
                        f3_stats[tg] = ffn_stats_tile(tg)

                if i == 1 and sI + 1 < T // 512:
                    for j in range(2):
                        t2 = (sI + 1) * 4 + j

                        def pref_v(e, t2=t2, j=j, sr=stats_ready[t2], gb_ld=gb_ld):
                            e.wait_ge(sv.h, sr)
                            e.wait_ge(qw.h, gb_ld)
                            dstb = XNB[:, 1024 * j:1024 * (j + 1)]
                            e.tensor_scalar(dstb, X[:, D * t2:D * (t2 + 1)],
                                            STAT[:, t2:t2 + 1], STAT[:, 16 + t2:17 + t2],
                                            ALU.subtract, ALU.mult)
                            e.drain()
                            e.tensor_scalar(dstb, dstb, STAT[:, 32 + t2:33 + t2],
                                            STAT[:, 48 + t2:49 + t2], ALU.subtract, ALU.mult)
                            e.drain()
                            e.tensor_tensor(dstb, dstb, GAM, ALU.mult)
                            e.drain()
                            e.tensor_tensor(dstb, dstb, BET, ALU.add).then_inc(sv.h, 1)
                        em("vector", pref_v)
                        pref_tgt[(sI + 1, j)] = sv.inc()
        return f3_stats

    a_stats = None
    scr_ready = False
    if "f1" in stages:
        a_stats = {} if ("attn" in stages and attn_level >= 3) else None
        ffn_stage(first_call=True, attn_stats_out=a_stats,
                  riders=("f3" in stages and _BF16))
        scr_ready = "f3" in stages and _BF16
    f3_pre = None
    if "attn" in stages:
        stage_barrier()
        f3_pre = attn_stage(level=attn_level, stats_pre=a_stats,
                            emit_f3=("f3" in stages and attn_level >= 6))
    if "f3" in stages:
        stage_barrier()
        ffn_stage(first_call=False, stats_pre=f3_pre, bf16=scr_ready)

    def stores(e, n=sv.n):
        e.wait_ge(sv.h, n)
        for t in range(ntile):
            e.dma_start(out=y_h[128 * t:128 * (t + 1), :],
                        in_=X[:, D * t:D * (t + 1)]).then_inc(qw.h, 16)
    em("sync", stores)
    if debug:
        def dbg_stores(e, n=sv.n):
            e.wait_ge(sv.h, n)
            e.dma_start(out=dbg_desti_h[:, 0:ntile], in_=DESTI[:, 0:ntile]).then_inc(qw.h, 16)
            e.dma_start(out=dbg_idxf_h[:, 0:ntile], in_=IDXF[:, 0:ntile]).then_inc(qw.h, 16)
            e.dma_start(out=dbg_ambc_h[:, 0:ntile], in_=AMBC[:, 0:ntile]).then_inc(qw.h, 16)
            e.dma_start(out=dbg_yrsel_h[:, :], in_=YRSEL[:, :]).then_inc(qw.h, 16)
        em("gpsimd", lambda e, n=sv.n: e.wait_ge(sv.h, n))
        for i in range((E * CAP) // 128):
            em("gpsimd", lambda e, i=i: e.dma_start(out=dbg_xs_h[128 * i:128 * (i + 1), :],
                                                    in_=xs_h[128 * i:128 * (i + 1), :]).then_inc(qg.h, 16))
            qg.inc(16)
            em("gpsimd", lambda e, i=i: e.dma_start(out=dbg_ys_h[128 * i:128 * (i + 1), :],
                                                    in_=ys_h[128 * i:128 * (i + 1), :]).then_inc(qg.h, 16))
            qg.inc(16)
        em("sync", dbg_stores)
        for _ in range(4):
            qw.inc(16)
        em("sync", lambda e, n=qw.n: e.wait_ge(qw.h, n))
        em("gpsimd", lambda e, n=qg.n: e.wait_ge(qg.h, n))
    final_store = None
    for t in range(ntile):
        final_store = qw.inc(16)
    em("sync", lambda e, fs=final_store: e.wait_ge(qw.h, fs))

    with nc.Block() as block:
        for eng_name, deco in (("sync", block.sync), ("tensor", block.tensor),
                               ("vector", block.vector), ("scalar", block.scalar),
                               ("gpsimd", block.gpsimd)):
            fns = streams[eng_name]

            def body(e, fns=fns):
                for f in fns:
                    f(e)
            deco(body)
    return nc



def _host_layouts(inputs, bslice):
    up = np.asarray(inputs["L0_ffn_up"], np.float32)
    gate = np.asarray(inputs["L0_ffn_gate"], np.float32)
    down = np.asarray(inputs["L0_ffn_down"], np.float32)
    router = np.asarray(inputs["L0_ffn_router"], np.float32)
    wus = np.asarray(inputs["L0_ffn_w_up_s"], np.float32)
    wgs = np.asarray(inputs["L0_ffn_w_gate_s"], np.float32)
    wds = np.asarray(inputs["L0_ffn_w_down_s"], np.float32)

    def tile_ug(w):
        wt = w.transpose(0, 2, 1)
        return np.ascontiguousarray(wt.reshape(E, 8, 128, 16, 128).transpose(0, 3, 1, 2, 4))

    def tile_ug_s(w):
        return np.ascontiguousarray(w.T.reshape(8, 128, 16, 128).transpose(2, 0, 1, 3))

    base = {
        "upTt": tile_ug(up), "gateTt": tile_ug(gate),
        "downTt": np.ascontiguousarray(down.transpose(0, 2, 1).reshape(E, 16, 128, D)),
        "wupTt": tile_ug_s(wus), "wgateTt": tile_ug_s(wgs),
        "wdownTt": np.ascontiguousarray(wds.T.reshape(16, 128, D)),
        "routerT": np.ascontiguousarray(
            router.T.reshape(8, 128, 8).transpose(1, 0, 2).reshape(128, 64)),
        "Wq": np.ascontiguousarray(np.asarray(inputs["L0_attn_Wq"], np.float32)),
        "Wk": np.ascontiguousarray(np.asarray(inputs["L0_attn_Wk"], np.float32)),
        "Wv": np.ascontiguousarray(np.asarray(inputs["L0_attn_Wv"], np.float32)),
        "Wa": np.ascontiguousarray(np.asarray(inputs["L0_attn_Wa"], np.float32)),
        "Wo": np.ascontiguousarray(np.asarray(inputs["L0_attn_Wo"], np.float32)),
        "ba_tile": np.ascontiguousarray(
            np.tile(np.asarray(inputs["L0_attn_ba"], np.float32)[None, :], (128, 1))),
        "gamma_tile": np.ascontiguousarray(
            np.tile(np.asarray(inputs["L0_attn_gamma"], np.float32)[None, :], (128, 1))),
        "beta_tile": np.ascontiguousarray(
            np.tile(np.asarray(inputs["L0_attn_beta"], np.float32)[None, :], (128, 1))),
        "ident": np.eye(128, dtype=np.float32),
        "ut": np.triu(np.ones((128, 128), np.float32)),
        "ones_row": np.ones((1, 128), np.float32),
        "ones_col": np.ones((128, 1), np.float32),
        "iotaE": np.tile(np.arange(8, dtype=np.float32)[None, :], (128, 1)),
        "iota16": np.tile(np.arange(16, dtype=np.float32)[None, :], (128, 1)),
        "actconsts": np.tile(np.array([-1.0, LN_EPS, 0.5, 0, 0, 0, 0, 0],
                                      np.float32)[None, :], (128, 1)),
    }
    x = np.asarray(inputs["L0_attn_x"], np.float32)
    return [dict(base, x=np.ascontiguousarray(x[bidx])) for bidx in bslice]


def kernel(**inputs):
    x = np.asarray(inputs["L0_attn_x"], np.float32)
    Bn = x.shape[0]
    nc = build_core_program(T=x.shape[1])
    in_maps = _host_layouts(inputs, list(range(Bn)))
    res = run_bass_kernel_spmd(nc, in_maps, core_ids=list(range(Bn)))
    return np.stack([np.asarray(r["y"], np.float32) for r in res.results], axis=0)


# version 15
# speedup vs baseline: 1.0185x; 1.0185x over previous
import numpy as np
import concourse.bass as bass
import concourse.mybir as mybir
from concourse.bass_utils import run_bass_kernel_spmd
from concourse.bass import IndirectOffsetOnAxis

F32 = mybir.dt.float32
I32 = mybir.dt.int32
BF16 = mybir.dt.bfloat16
AX = mybir.AxisListType
ALU = mybir.AluOpType
ACT = mybir.ActivationFunctionType

D = 1024
DI = 512
NH = 8
E = 8
CAP = 368
NR = 16
SLOTS = CAP + NR
THETA = 1e-4
LN_EPS = 1e-5
ARENA_COLS = 28000
_BF16 = True
_BF16_EXP = True
_F1SH_BF16 = True
_BF16_SH = True


class Sem:
    def __init__(self, nc, name):
        self.h = nc.alloc_semaphore(name)
        self.n = 0

    def inc(self, k=1):
        self.n += k
        return self.n


def build_core_program(T=2048, stages=("f1", "attn", "f3"), debug=False, attn_level=6):
    ntile = T // 128
    nc = bass.Bass("TRN2", target_bir_lowering=False, debug=False,
                   detect_race_conditions=False)
    streams = {k: [] for k in ("tensor", "vector", "scalar", "gpsimd", "sync")}

    def em(eng, fn):
        streams[eng].append(fn)

    st, sv, ss, sg, sy = (Sem(nc, "st"), Sem(nc, "sv"), Sem(nc, "ss"),
                          Sem(nc, "sg"), Sem(nc, "sy"))
    qw, qg, qr = Sem(nc, "qw"), Sem(nc, "qg"), Sem(nc, "qr")

    x_h = nc.dram_tensor("x", [T, D], F32, kind="ExternalInput").ap()
    y_h = nc.dram_tensor("y", [T, D], F32, kind="ExternalOutput").ap()
    upT_h = nc.dram_tensor("upTt", [E, 16, 8, 128, 128], F32, kind="ExternalInput").ap()
    gateT_h = nc.dram_tensor("gateTt", [E, 16, 8, 128, 128], F32, kind="ExternalInput").ap()
    downT_h = nc.dram_tensor("downTt", [E, 16, 128, D], F32, kind="ExternalInput").ap()
    wupT_h = nc.dram_tensor("wupTt", [16, 8, 128, 128], F32, kind="ExternalInput").ap()
    wgateT_h = nc.dram_tensor("wgateTt", [16, 8, 128, 128], F32, kind="ExternalInput").ap()
    wdownT_h = nc.dram_tensor("wdownTt", [16, 128, D], F32, kind="ExternalInput").ap()
    routerT_h = nc.dram_tensor("routerT", [128, 64], F32, kind="ExternalInput").ap()
    wq_h = nc.dram_tensor("Wq", [D, DI], F32, kind="ExternalInput").ap()
    wk_h = nc.dram_tensor("Wk", [D, DI], F32, kind="ExternalInput").ap()
    wv_h = nc.dram_tensor("Wv", [D, DI], F32, kind="ExternalInput").ap()
    wa_h = nc.dram_tensor("Wa", [D, DI], F32, kind="ExternalInput").ap()
    wo_h = nc.dram_tensor("Wo", [DI, D], F32, kind="ExternalInput").ap()
    ba_h = nc.dram_tensor("ba_tile", [128, DI], F32, kind="ExternalInput").ap()
    gam_h = nc.dram_tensor("gamma_tile", [128, D], F32, kind="ExternalInput").ap()
    bet_h = nc.dram_tensor("beta_tile", [128, D], F32, kind="ExternalInput").ap()
    ident_h = nc.dram_tensor("ident", [128, 128], F32, kind="ExternalInput").ap()
    ut_h = nc.dram_tensor("ut", [128, 128], F32, kind="ExternalInput").ap()
    onesr_h = nc.dram_tensor("ones_row", [1, 128], F32, kind="ExternalInput").ap()
    onesc_h = nc.dram_tensor("ones_col", [128, 1], F32, kind="ExternalInput").ap()
    iotaE_h = nc.dram_tensor("iotaE", [128, 8], F32, kind="ExternalInput").ap()
    iota16_h = nc.dram_tensor("iota16", [128, 16], F32, kind="ExternalInput").ap()
    cst_h = nc.dram_tensor("actconsts", [128, 8], F32, kind="ExternalInput").ap()
    xs_h = nc.dram_tensor("xs_scratch", [E * CAP, D], F32, kind="Internal").ap()
    ys_h = nc.dram_tensor("ys_scratch", [E * CAP, D], F32, kind="Internal").ap()
    eug_scr = nc.dram_tensor("eug_scr", [E, 16, 128, 4096], BF16, kind="Internal").ap()
    edn_scr = nc.dram_tensor("edn_scr", [E, 16, 128, 2048], BF16, kind="Internal").ap()
    sug_scr = nc.dram_tensor("sug_scr", [16, 128, 4096], BF16, kind="Internal").ap()
    sdn_scr = nc.dram_tensor("sdn_scr", [16, 128, 2048], BF16, kind="Internal").ap()
    if debug:
        dbg_desti_h = nc.dram_tensor("dbg_desti", [128, 16], I32, kind="ExternalOutput").ap()
        dbg_idxf_h = nc.dram_tensor("dbg_idxf", [128, 16], F32, kind="ExternalOutput").ap()
        dbg_ambc_h = nc.dram_tensor("dbg_ambc", [128, 16], F32, kind="ExternalOutput").ap()
        dbg_xs_h = nc.dram_tensor("dbg_xs", [E * CAP, D], F32, kind="ExternalOutput").ap()
        dbg_ys_h = nc.dram_tensor("dbg_ys", [E * CAP, D], F32, kind="ExternalOutput").ap()
        dbg_yrsel_h = nc.dram_tensor("dbg_yrsel", [16, D], F32, kind="ExternalOutput").ap()
        dbg_hall_h = nc.dram_tensor("dbg_hall", [128, 16 * SLOTS], F32, kind="ExternalOutput").ap()
        dbg_xst_h = nc.dram_tensor("dbg_xst", [128, 8 * SLOTS], F32, kind="ExternalOutput").ap()
        dbg_wdn_h = nc.dram_tensor("dbg_wdn", [128, 2 * D], F32, kind="ExternalOutput").ap()
        dbg_qs_h = nc.dram_tensor("dbg_qs", [128, 4 * DI], F32, kind="ExternalOutput").ap()
        dbg_ks_h = nc.dram_tensor("dbg_ks", [128, 4 * DI], F32, kind="ExternalOutput").ap()
        dbg_vs_h = nc.dram_tensor("dbg_vs", [128, 4 * DI], F32, kind="ExternalOutput").ap()
        dbg_kbb_h = nc.dram_tensor("dbg_kbb", [128, 4 * DI], F32, kind="ExternalOutput").ap()
        dbg_csb_h = nc.dram_tensor("dbg_csb", [128, 4 * DI], F32, kind="ExternalOutput").ap()
        dbg_ots_h = nc.dram_tensor("dbg_ots", [128, 4 * 128], F32, kind="ExternalOutput").ap()
        dbg_sst_h = nc.dram_tensor("dbg_sst", [64, NH * 64], F32, kind="ExternalOutput").ap()
        dbg_sdec_h = nc.dram_tensor("dbg_sdec", [64, 32], F32, kind="ExternalOutput").ap()
        dbg_yss_h = nc.dram_tensor("dbg_yss", [128, 3 * D], F32, kind="ExternalOutput").ap()

    A = lambda name, shape, dt=F32: nc.alloc_sbuf_tensor(name, shape, dt).ap()
    X = A("X", [128, ntile * D])
    XNT = A("XNT", [128, 8 * 512])
    XNB = A("XNB", [128, 2 * D])
    STAT = A("STAT", [128, 4 * 16])
    RT1 = A("RT1", [128, 32])
    KNRM = A("KNRM", [128, 32])
    TSC = A("TSC", [128, D])
    SGS = A("SGS", [128, D])
    DESTI = A("DESTI", [128, 16], I32)
    IDENT = A("IDENT", [128, 128])
    UT = A("UT", [128, 128])
    ONESR = A("ONESR", [1, 128])
    ONESC = A("ONESC", [128, 1])
    IOTAE = A("IOTAE", [128, 8])
    IOTA16 = A("IOTA16", [128, 16])
    CST = A("CST", [128, 8])
    ROUTT = A("ROUTT", [128, 64])

    ARENA = A("ARENA", [128, ARENA_COLS])
    _off = [0]

    def carve(ncols, parts=128):
        a = ARENA[0:parts, _off[0]:_off[0] + ncols]
        _off[0] += ncols
        return a

    HALL = carve(16 * SLOTS)
    XST = carve(8 * SLOTS)
    GX = carve(3 * D)
    WUG = carve(4096)
    WDN = carve(2 * D)
    YSG = carve(D)
    YSS = carve(3 * D)
    GR = carve(256)
    GRT = carve(2048, parts=16)
    XR = carve(D, parts=16)
    YRSEL = carve(D, parts=16)
    XRT = carve(128)
    OHR = carve(8, parts=16)
    LG = carve(16)
    BASE = carve(16, parts=1)
    IDXF = carve(16)
    AMBC = carve(16)
    RT2 = carve(40)
    ZEROT = YSG[:, 0:D]
    HS2 = ARENA[0:128, 0:16 * 512]
    assert _off[0] <= ARENA_COLS, _off[0]

    WUGb = WUG.bitcast(BF16)
    WDNb = WDN.bitcast(BF16)
    XSTb = XST.bitcast(BF16)
    XSTH = XSTb[:, 0:8 * SLOTS]
    XSTL = XSTb[:, 8 * SLOTS:16 * SLOTS]
    HALLb = HALL.bitcast(BF16)
    HALLH = HALLb[:, 0:16 * SLOTS]
    HALLL = HALLb[:, 16 * SLOTS:32 * SLOTS]
    HS2b = HS2.bitcast(BF16)
    HS2H = HS2b[:, 0:8192]
    HS2L = HS2b[:, 8192:16384]
    GXb = GX.bitcast(BF16)
    TSCb = TSC.bitcast(BF16)
    XNBb = XNB.bitcast(BF16)
    XNTb = XNT.bitcast(BF16)
    XNTH = XNTb[:, 0:4096]
    XNTL = XNTb[:, 4096:8192]

    _off[0] = 0
    QS = carve(4 * DI)
    KS = carve(4 * DI)
    VS = carve(4 * DI)
    ALA = carve(4 * DI)
    CSB = carve(2 * DI)
    KBB = carve(4 * DI)
    KBE = carve(2 * DI)
    QTT = carve(2 * NH * 128, parts=64)
    KHT = carve(2 * NH * 128, parts=64)
    WOS = carve(4 * D)
    EF = carve(2 * DI)
    OTS = carve(4 * 128)
    SST = carve(NH * 64, parts=64)
    SDEC = carve(32, parts=64)
    PMS = carve(128)
    WAT = carve(2 * DI)
    BA = carve(DI)
    GAM = carve(D)
    BET = carve(D)
    assert _off[0] <= ARENA_COLS, _off[0]

    PB = [nc.alloc_psum_tensor(f"pb{i}", [128, 512], F32).ap() for i in range(8)]

    def dma(engq, out, in_, sem):
        tgt = sem.inc(16)
        em(engq, lambda e, out=out, in_=in_: e.dma_start(out=out, in_=in_).then_inc(sem.h, 16))
        return tgt

    for pair in ((IDENT, ident_h), (UT, ut_h), (ONESR, onesr_h), (ONESC, onesc_h),
                 (IOTAE, iotaE_h), (IOTA16, iota16_h), (CST, cst_h), (ROUTT, routerT_h)):
        dma("sync", pair[0][:, :], pair[1][:, :], qw)
    for t in range(ntile):
        dma("sync", X[:, D * t:D * (t + 1)], x_h[128 * t:128 * (t + 1), :], qw)
    consts_loaded = qw.n

    nc.const_aps.register(F32, -1.0, CST[:, 0:1])
    nc.const_aps.register(F32, LN_EPS, CST[:, 1:2])
    nc.const_aps.register(F32, 0.5, CST[:, 2:3])

    def gz(e):
        e.wait_ge(qw.h, consts_loaded)
        e.memset(ZEROT, 0.0)
        e.drain()
    em("gpsimd", gz)
    for i in range((E * CAP) // 128):
        dma("gpsimd", xs_h[128 * i:128 * (i + 1), :], ZEROT, qg)
    xs_zeroed = qg.n

    def vinit(e):
        e.wait_ge(qw.h, consts_loaded)
        e.memset(SST[:, :], 0.0).then_inc(sv.h, 1)
    em("vector", vinit)
    vinit_done = sv.inc()

    ug_seq = []
    dn_seq = []
    wat_seq = []
    comb_seq = []
    rid_ug = []
    rid_dn = []

    def stage_barrier():
        def vbar(e, nst=st.n, nss=ss.n, nqw=qw.n, nqg=qg.n, nqr=qr.n):
            e.wait_ge(st.h, nst)
            e.wait_ge(ss.h, nss)
            e.wait_ge(qw.h, nqw)
            e.wait_ge(qg.h, nqg)
            e.wait_ge(qr.h, nqr)
            e.nop().then_inc(sv.h, 1)
        em("vector", vbar)
        bar = sv.inc()
        for eng in ("sync", "gpsimd", "tensor", "scalar"):
            em(eng, lambda e, bar=bar: e.wait_ge(sv.h, bar))

    def ffn_stats_tile(t):
        par = 16 * (t % 2)
        Xt = X[:, D * t:D * (t + 1)]

        def fs_v(e, t=t, Xt=Xt, par=par):
            e.reduce_sum(RT1[:, par:par + 1], Xt, AX.X)
            e.drain()
            e.tensor_scalar(STAT[:, t:t + 1], RT1[:, par:par + 1], 1.0 / D,
                            None, ALU.mult).then_inc(sv.h, 1)
        em("vector", fs_v)
        m_done = sv.inc()

        def fs_s(e, Xt=Xt, par=par, m_done=m_done):
            e.wait_ge(sv.h, m_done)
            e.activation(SGS[:, :], Xt, ACT.Square,
                         accum_out=RT1[:, par + 1:par + 2]).then_inc(ss.h, 1)
        em("scalar", fs_s)
        ssq = ss.inc()

        def fv_v(e, t=t, par=par, ssq=ssq):
            e.wait_ge(ss.h, ssq)
            e.tensor_tensor(RT1[:, par + 2:par + 3], STAT[:, t:t + 1],
                            STAT[:, t:t + 1], ALU.mult)
            e.drain()
            e.tensor_scalar(RT1[:, par + 3:par + 4], RT1[:, par + 1:par + 2],
                            1.0 / D, RT1[:, par + 2:par + 3],
                            ALU.mult, ALU.subtract).then_inc(sv.h, 1)
        em("vector", fv_v)
        var_done = sv.inc()

        def fq_s(e, par=par, var_done=var_done):
            e.wait_ge(sv.h, var_done)
            e.activation(RT1[:, par + 4:par + 5], RT1[:, par + 3:par + 4],
                         ACT.Sqrt, bias=CST[:, 1:2]).then_inc(ss.h, 1)
        em("scalar", fq_s)
        std_done = ss.inc()

        def fr_v(e, t=t, par=par, std_done=std_done):
            e.wait_ge(ss.h, std_done)
            e.reciprocal(STAT[:, 16 + t:17 + t], RT1[:, par + 4:par + 5]).then_inc(sv.h, 1)
        em("vector", fr_v)
        return sv.inc()

    def attn_stats_tile(t, sq2h):
        par = 16 * (t % 2)
        Xt = X[:, D * t:D * (t + 1)]
        xl = GX[:, D * (t % 2):D * (t % 2 + 1)]
        gout = GX[:, 2 * D:3 * D]

        def st_v(e, t=t, Xt=Xt, par=par):
            e.reduce_sum(RT1[:, par:par + 1], Xt, AX.X)
            e.drain()
            e.tensor_scalar(STAT[:, t:t + 1], RT1[:, par:par + 1], 1.0 / D,
                            None, ALU.mult).then_inc(sv.h, 1)
        em("vector", st_v)
        m1d = sv.inc()

        def st_s(e, Xt=Xt, par=par, m1d=m1d, gout=gout):
            e.wait_ge(sv.h, m1d)
            e.activation(gout, Xt, ACT.Square,
                         accum_out=RT1[:, par + 1:par + 2]).then_inc(ss.h, 1)
        em("scalar", st_s)
        sq1 = ss.inc()

        def var1_v(e, t=t, par=par, sq1=sq1):
            e.wait_ge(ss.h, sq1)
            e.tensor_tensor(RT1[:, par + 2:par + 3], STAT[:, t:t + 1],
                            STAT[:, t:t + 1], ALU.mult)
            e.drain()
            e.tensor_scalar(RT1[:, par + 3:par + 4], RT1[:, par + 1:par + 2],
                            1.0 / D, RT1[:, par + 2:par + 3],
                            ALU.mult, ALU.subtract).then_inc(sv.h, 1)
        em("vector", var1_v)
        v1d = sv.inc()

        def sq1_s(e, par=par, v1d=v1d):
            e.wait_ge(sv.h, v1d)
            e.activation(RT1[:, par + 4:par + 5], RT1[:, par + 3:par + 4],
                         ACT.Sqrt, bias=CST[:, 1:2]).then_inc(ss.h, 1)
        em("scalar", sq1_s)
        s1d = ss.inc()

        def xl_v(e, t=t, par=par, Xt=Xt, xl=xl, s1d=s1d,
                 prev_sq2=sq2h.get(t - 2)):
            e.wait_ge(ss.h, s1d)
            if prev_sq2 is not None:
                e.wait_ge(ss.h, prev_sq2)
            e.reciprocal(STAT[:, 16 + t:17 + t], RT1[:, par + 4:par + 5])
            e.drain()
            e.tensor_scalar(xl, Xt, STAT[:, t:t + 1], STAT[:, 16 + t:17 + t],
                            ALU.subtract, ALU.mult)
            e.drain()
            e.reduce_sum(RT1[:, par:par + 1], xl, AX.X)
            e.drain()
            e.tensor_scalar(STAT[:, 32 + t:33 + t], RT1[:, par:par + 1], 1.0 / D,
                            None, ALU.mult).then_inc(sv.h, 1)
        em("vector", xl_v)
        m2d = sv.inc()

        def sq2_s(e, xl=xl, par=par, m2d=m2d, gout=gout):
            e.wait_ge(sv.h, m2d)
            e.activation(gout, xl, ACT.Square,
                         accum_out=RT1[:, par + 1:par + 2]).then_inc(ss.h, 1)
        em("scalar", sq2_s)
        sq2h[t] = ss.inc()

        def var2_v(e, t=t, par=par, sq2=sq2h[t]):
            e.wait_ge(ss.h, sq2)
            e.tensor_tensor(RT1[:, par + 2:par + 3], STAT[:, 32 + t:33 + t],
                            STAT[:, 32 + t:33 + t], ALU.mult)
            e.drain()
            e.tensor_scalar(RT1[:, par + 3:par + 4], RT1[:, par + 1:par + 2],
                            1.0 / D, RT1[:, par + 2:par + 3],
                            ALU.mult, ALU.subtract).then_inc(sv.h, 1)
        em("vector", var2_v)
        v2d = sv.inc()

        def sq2b_s(e, par=par, v2d=v2d):
            e.wait_ge(sv.h, v2d)
            e.activation(RT1[:, par + 5:par + 6], RT1[:, par + 3:par + 4],
                         ACT.Sqrt, bias=CST[:, 1:2]).then_inc(ss.h, 1)
        em("scalar", sq2b_s)
        s2d = ss.inc()

        def r2_v(e, t=t, par=par, s2d=s2d):
            e.wait_ge(ss.h, s2d)
            e.reciprocal(STAT[:, 48 + t:49 + t], RT1[:, par + 5:par + 6]).then_inc(sv.h, 1)
        em("vector", r2_v)
        return sv.inc()

    def ffn_stage(first_call, stats_pre=None, attn_stats_out=None,
                  riders=False, bf16=False):
        bf16_exp = bf16 and _BF16_EXP
        bf16_sh = bf16 and _BF16_SH
        rid_prev = {"ug": None, "dn": None, "sug": None, "sdn": None}
        def vstart(e):
            e.wait_ge(sv.h, vinit_done)
            e.memset(BASE[:, 0:9], 0.0).then_inc(sv.h, 1)
        em("vector", vstart)
        sv.inc()
        if first_call:
            em("gpsimd", lambda e: e.wait_ge(qg.h, xs_zeroed))

        scatter_tgt = {}
        xtr_tgt = {}
        rgmm_tgt = {}

        for t in range(ntile):
            buf = t % 2
            par = 16 * (t % 2)
            xn = XNB[:, D * buf:D * (buf + 1)]
            Xt = X[:, D * t:D * (t + 1)]

            if stats_pre is not None:
                def xn_v(e, t=t, xn=xn, Xt=Xt):
                    if t - 2 in scatter_tgt:
                        e.wait_ge(qg.h, scatter_tgt[t - 2])
                        e.wait_ge(st.h, xtr_tgt[t - 2])
                    e.tensor_scalar(xn, Xt, STAT[:, t:t + 1], STAT[:, 16 + t:17 + t],
                                    ALU.subtract, ALU.mult).then_inc(sv.h, 1)
                em("vector", xn_v)
                xn_done = sv.inc()
            else:
                def stats_v(e, t=t, Xt=Xt, par=par):
                    if t - 2 in scatter_tgt:
                        e.wait_ge(qg.h, scatter_tgt[t - 2])
                        e.wait_ge(st.h, xtr_tgt[t - 2])
                    e.reduce_sum(RT1[:, par:par + 1], Xt, AX.X)
                    e.drain()
                    e.tensor_scalar(STAT[:, t:t + 1], RT1[:, par:par + 1], 1.0 / D,
                                    None, ALU.mult).then_inc(sv.h, 1)
                em("vector", stats_v)
                m_done = sv.inc()

                def stats_s(e, Xt=Xt, par=par, m_done=m_done):
                    e.wait_ge(sv.h, m_done)
                    e.activation(SGS[:, :], Xt, ACT.Square,
                                 accum_out=RT1[:, par + 1:par + 2]).then_inc(ss.h, 1)
                em("scalar", stats_s)
                ssq_done = ss.inc()

                def var_v(e, t=t, par=par, ssq_done=ssq_done):
                    e.wait_ge(ss.h, ssq_done)
                    e.tensor_tensor(RT1[:, par + 2:par + 3], STAT[:, t:t + 1],
                                    STAT[:, t:t + 1], ALU.mult)
                    e.drain()
                    e.tensor_scalar(RT1[:, par + 3:par + 4], RT1[:, par + 1:par + 2],
                                    1.0 / D, RT1[:, par + 2:par + 3],
                                    ALU.mult, ALU.subtract).then_inc(sv.h, 1)
                em("vector", var_v)
                var_done = sv.inc()

                def sqrt_s(e, par=par, var_done=var_done):
                    e.wait_ge(sv.h, var_done)
                    e.activation(RT1[:, par + 4:par + 5], RT1[:, par + 3:par + 4],
                                 ACT.Sqrt, bias=CST[:, 1:2]).then_inc(ss.h, 1)
                em("scalar", sqrt_s)
                std_done = ss.inc()

                def rstd_v(e, t=t, par=par, xn=xn, Xt=Xt, std_done=std_done):
                    e.wait_ge(ss.h, std_done)
                    e.reciprocal(STAT[:, 16 + t:17 + t], RT1[:, par + 4:par + 5])
                    e.drain()
                    e.tensor_scalar(xn, Xt, STAT[:, t:t + 1], STAT[:, 16 + t:17 + t],
                                    ALU.subtract, ALU.mult).then_inc(sv.h, 1)
                em("vector", rstd_v)
                xn_done = sv.inc()

            b0, b1 = (PB[0], PB[1]) if t % 2 == 0 else (PB[6], PB[7])

            def xtr(e, xn=xn, b0=b0, b1=b1, xn_done=xn_done):
                e.wait_ge(sv.h, xn_done)
                for c in range(8):
                    pb = b0 if c < 4 else b1
                    last = e.transpose(pb[:, 128 * (c % 4):128 * (c % 4 + 1)],
                                       xn[:, 128 * c:128 * (c + 1)], IDENT[:, :])
                last.then_inc(st.h, 1)
            em("tensor", xtr)
            xtr_tgt[t] = st.inc()

            def xtr_evac(e, t=t, b0=b0, b1=b1, xd=xtr_tgt[t]):
                e.wait_ge(st.h, xd)
                sl = 128 * (t % 4)
                for c in range(8):
                    pb = b0 if c < 4 else b1
                    last = e.tensor_copy(XNT[:, 512 * c + sl:512 * c + sl + 128],
                                         pb[:, 128 * (c % 4):128 * (c % 4 + 1)])
                last.then_inc(sv.h, 1)
            em("vector", xtr_evac)
            xnt_done = sv.inc()

            def logit_mm(e, t=t, xnt_done=xnt_done):
                e.wait_ge(sv.h, xnt_done)
                sl = 128 * (t % 4)
                for c in range(8):
                    last = e.matmul(PB[5][:, 0:8], XNT[:, 512 * c + sl:512 * c + sl + 128],
                                    ROUTT[:, 8 * c:8 * (c + 1)],
                                    start=(c == 0), stop=(c == 7))
                last.then_inc(st.h, 1)
            em("tensor", logit_mm)
            lg_done = st.inc()

            def route_v(e, t=t, lg_done=lg_done):
                e.wait_ge(st.h, lg_done)
                L = LG[:, 0:8]
                e.tensor_copy(L, PB[5][:, 0:8])
                e.drain()
                e.reduce_max(RT2[:, 0:1], L, AX.X)
                e.drain()
                e.tensor_scalar(RT2[:, 8:16], L, RT2[:, 0:1], None, ALU.is_equal)
                e.drain()
                e.tensor_tensor(RT2[:, 16:24], RT2[:, 8:16], IOTAE[:, :], ALU.mult)
                e.tensor_scalar(RT2[:, 24:32], RT2[:, 8:16], -9.0, 9.0, ALU.mult, ALU.add)
                e.drain()
                e.tensor_tensor(RT2[:, 16:24], RT2[:, 16:24], RT2[:, 24:32], ALU.add)
                e.drain()
                e.tensor_reduce(IDXF[:, t:t + 1], RT2[:, 16:24], AX.X, ALU.min)
                e.drain()
                e.tensor_scalar(RT2[:, 8:16], IOTAE[:, :], IDXF[:, t:t + 1], None,
                                ALU.is_equal)
                e.drain()
                e.scalar_tensor_tensor(RT2[:, 16:24], RT2[:, 8:16], 1e9, L,
                                       ALU.mult, ALU.subtract)
                e.drain()
                e.tensor_reduce(RT2[:, 1:2], RT2[:, 16:24], AX.X, ALU.min)
                e.drain()
                e.tensor_tensor(RT2[:, 2:3], RT2[:, 0:1], RT2[:, 1:2], ALU.add)
                e.drain()
                e.tensor_single_scalar(RT2[:, 16:17], RT2[:, 2:3], THETA, ALU.is_lt)
                e.drain()
                e.tensor_scalar(AMBC[:, t:t + 1], RT2[:, 16:17], -1.0, 1.0,
                                ALU.mult, ALU.add).then_inc(sv.h, 1)
            em("vector", route_v)
            oh_done = sv.inc()

            def cum_mm(e, t=t, oh_done=oh_done):
                e.wait_ge(sv.h, oh_done)
                e.matmul(PB[4][:, 0:9], UT[:, :], RT2[:, 8:17], start=True, stop=False)
                e.matmul(PB[4][:, 0:9], ONESR[0:1, :], BASE[:, 0:9],
                         start=False, stop=True)
                e.matmul(PB[3][0:1, 0:9], ONESC[:, 0:1], RT2[:, 8:17],
                         start=True, stop=False)
                e.matmul(PB[3][0:1, 0:9], ONESR[0:1, 0:1], BASE[:, 0:9],
                         start=False, stop=True).then_inc(st.h, 1)
            em("tensor", cum_mm)
            cum_done = st.inc()

            def rank_v(e, t=t, cum_done=cum_done):
                e.wait_ge(st.h, cum_done)
                e.tensor_copy(BASE[:, 0:9], PB[3][0:1, 0:9])
                e.tensor_tensor(RT2[:, 24:32], PB[4][:, 0:8], RT2[:, 8:16], ALU.mult)
                e.drain()
                e.reduce_sum(RT2[:, 3:4], RT2[:, 24:32], AX.X)
                e.drain()
                e.tensor_scalar(RT2[:, 4:5], RT2[:, 3:4], 1.0, float(CAP - 1),
                                ALU.subtract, ALU.min)
                e.drain()
                e.tensor_scalar(RT2[:, 5:6], IDXF[:, t:t + 1], float(CAP), RT2[:, 4:5],
                                ALU.mult, ALU.add)
                e.drain()
                e.tensor_copy(DESTI[:, t:t + 1], RT2[:, 5:6])
                e.tensor_scalar(RT2[:, 6:7], PB[4][:, 8:9], 1.0, float(NR - 1),
                                ALU.subtract, ALU.min)
                e.drain()
                e.tensor_scalar(GR[:, 16 * t:16 * (t + 1)], IOTA16[:, :], RT2[:, 6:7],
                                RT2[:, 16:17], ALU.is_equal, ALU.mult).then_inc(sv.h, 1)
            em("vector", rank_v)
            rank_done = sv.inc()

            def gt_tr(e, t=t, rank_done=rank_done):
                e.wait_ge(sv.h, rank_done)
                e.transpose(PB[2][0:16, 128 * (t % 4):128 * (t % 4) + 128],
                            GR[:, 16 * t:16 * (t + 1)], IDENT[:, :]).then_inc(st.h, 1)
            em("tensor", gt_tr)
            gt_done = st.inc()
            if t % 4 == 3:
                def gt_evac(e, t=t, gt_done=gt_done):
                    e.wait_ge(st.h, gt_done)
                    e.tensor_copy(GRT[:, 512 * (t // 4):512 * (t // 4 + 1)],
                                  PB[2][0:16, 0:512]).then_inc(sv.h, 1)
                em("vector", gt_evac)
                sv.inc()

            def scat(e, t=t, xn=xn, rank_done=rank_done):
                e.wait_ge(sv.h, rank_done)
                e.indirect_dma_start(
                    out=xs_h[:, :], out_offset=IndirectOffsetOnAxis(DESTI[:, t:t + 1], 0),
                    in_=xn, in_offset=None).then_inc(qg.h, 16)
            em("gpsimd", scat)
            scatter_tgt[t] = qg.inc(16)

        all_scattered = qg.n

        for t in range(ntile):
            buf = t % 2
            xn = XNB[:, D * buf:D * (buf + 1)]
            Xt = X[:, D * t:D * (t + 1)]

            def rgb(e, t=t, xn=xn, Xt=Xt):
                e.wait_ge(qg.h, all_scattered)
                if t - 2 in rgmm_tgt:
                    e.wait_ge(st.h, rgmm_tgt[t - 2])
                e.tensor_scalar(xn, Xt, STAT[:, t:t + 1], STAT[:, 16 + t:17 + t],
                                ALU.subtract, ALU.mult).then_inc(sv.h, 1)
            em("vector", rgb)
            rb_done = sv.inc()

            def rg_mm(e, t=t, xn=xn, rb_done=rb_done):
                e.wait_ge(sv.h, rb_done)
                e.matmul(PB[5][0:16, 0:512], GR[:, 16 * t:16 * (t + 1)], xn[:, 0:512],
                         start=(t == 0), stop=(t == ntile - 1))
                e.matmul(PB[6][0:16, 0:512], GR[:, 16 * t:16 * (t + 1)], xn[:, 512:1024],
                         start=(t == 0), stop=(t == ntile - 1))
                e.matmul(PB[7][0:16, 0:1], GR[:, 16 * t:16 * (t + 1)], IDXF[:, t:t + 1],
                         start=(t == 0), stop=(t == ntile - 1)).then_inc(st.h, 1)
            em("tensor", rg_mm)
            rgmm_tgt[t] = st.inc()
        rg_all = rgmm_tgt[ntile - 1]

        def rg_evac(e, rg_all=rg_all):
            e.wait_ge(st.h, rg_all)
            e.tensor_copy(XR[:, 0:512], PB[5][0:16, 0:512])
            e.tensor_copy(XR[:, 512:1024], PB[6][0:16, 0:512])
            e.tensor_scalar(OHR[:, :], IOTAE[0:16, :], PB[7][0:16, 0:1],
                            None, ALU.is_equal)
            e.memset(YRSEL[:, :], 0.0).then_inc(sv.h, 1)
        em("vector", rg_evac)
        xr_done = sv.inc()

        def xr_tr(e, xr_done=xr_done):
            e.wait_ge(sv.h, xr_done)
            for c in range(8):
                pb = PB[5] if c < 4 else PB[6]
                last = e.transpose(pb[0:128, 128 * (c % 4):128 * (c % 4) + 16],
                                   XR[:, 128 * c:128 * (c + 1)], IDENT[0:16, 0:16])
            last.then_inc(st.h, 1)
        em("tensor", xr_tr)
        xrt_mm = st.inc()

        def xrt_evac(e, xrt_mm=xrt_mm):
            e.wait_ge(st.h, xrt_mm)
            for c in range(8):
                pb = PB[5] if c < 4 else PB[6]
                last = e.tensor_copy(XRT[:, 16 * c:16 * (c + 1)],
                                     pb[0:128, 128 * (c % 4):128 * (c % 4) + 16])
            last.then_inc(sv.h, 1)
        em("vector", xrt_evac)
        xrt_done = sv.inc()

        gx_tgt = {}

        def gather_expert(eidx, wait_st=None):
            for i, (r0, rn) in enumerate(((0, 128), (128, 128), (256, 112))):
                def gdma(e, eidx=eidx, r0=r0, rn=rn, i=i, wait_st=wait_st):
                    if i == 0:
                        if wait_st is not None:
                            e.wait_ge(st.h, wait_st)
                        if eidx == 0:
                            e.wait_ge(qg.h, all_scattered)
                    e.dma_start(out=GX[0:rn, D * i:D * i + D],
                                in_=xs_h[eidx * CAP + r0:eidx * CAP + r0 + rn, :],
                                ).then_inc(qg.h, 16)
                em("gpsimd", gdma)
                tgt = qg.inc(16)
            return tgt

        xst_last_tr = {}
        gx_tgt[0] = gather_expert(0)
        prev_dn_evac = 0
        prev_b45 = 0
        ys_store_tgt = {}

        for eidx in range(E):
            ug_wt = {}
            for Ht in range(16):
                k = len(ug_seq)
                wbuf = (Ht % 2) * 2048
                first_gate = (k == 0 and Ht == 0)

                def wdma(e, eidx=eidx, Ht=Ht, wbuf=wbuf, k=k, fg=first_gate):
                    if k >= 2:
                        e.wait_ge(st.h, ug_seq[k - 2])
                        if rid_ug[k - 2] is not None:
                            e.wait_ge(sv.h, rid_ug[k - 2])
                    elif fg:
                        e.wait_ge(qg.h, xs_zeroed)
                    if bf16_exp:
                        e.dma_start(out=WUGb[:, 2 * wbuf:2 * wbuf + 2048],
                                    in_=eug_scr[eidx, Ht, :, 0:2048]).then_inc(qw.h, 16)
                        e.dma_start(out=WUGb[:, 2 * wbuf + 2048:2 * wbuf + 4096],
                                    in_=eug_scr[eidx, Ht, :, 2048:4096]).then_inc(qw.h, 16)
                    else:
                        e.dma_start(out=WUG[:, wbuf:wbuf + 1024].rearrange("p (c q) -> p c q", c=8),
                                    in_=upT_h[eidx, Ht].rearrange("c p q -> p c q")).then_inc(qw.h, 16)
                        e.dma_start(out=WUG[:, wbuf + 1024:wbuf + 2048].rearrange("p (c q) -> p c q", c=8),
                                    in_=gateT_h[eidx, Ht].rearrange("c p q -> p c q")).then_inc(qw.h, 16)
                em("sync", wdma)
                qw.inc(16)
                ug_wt[Ht] = qw.inc(16)
                ug_seq.append(None)
                rid_ug.append(None)
            dn_wt = {}
            for Ht in range(16):
                k = len(dn_seq)

                def ddma(e, eidx=eidx, Ht=Ht, k=k):
                    if k >= 2:
                        e.wait_ge(st.h, dn_seq[k - 2])
                        if rid_dn[k - 2] is not None:
                            e.wait_ge(sv.h, rid_dn[k - 2])
                    if bf16_exp:
                        e.dma_start(out=WDNb[:, 2048 * (Ht % 2):2048 * (Ht % 2) + 2048],
                                    in_=edn_scr[eidx, Ht]).then_inc(qw.h, 16)
                    else:
                        e.dma_start(out=WDN[:, D * (Ht % 2):D * (Ht % 2) + D],
                                    in_=downT_h[eidx, Ht]).then_inc(qw.h, 16)
                em("sync", ddma)
                dn_wt[Ht] = qw.inc(16)
                dn_seq.append(None)
                rid_dn.append(None)

            glt = gx_tgt[eidx]
            xst_mm = {}
            xst_ev = {}
            for c in range(8):
                def xst_tr(e, c=c, glt=glt, xrt_done=xrt_done, prev=prev_b45,
                           evp=xst_ev.get(c - 2)):
                    pb = PB[4] if c % 2 == 0 else PB[5]
                    if c == 0:
                        e.wait_ge(qg.h, glt)
                        e.wait_ge(sv.h, max(xrt_done, prev))
                    if evp is not None:
                        e.wait_ge(sv.h, evp)
                    e.transpose(pb[:, 0:128], GX[0:128, 128 * c:128 * c + 128],
                                IDENT[:, :])
                    e.transpose(pb[:, 128:256], GX[0:128, D + 128 * c:D + 128 * c + 128],
                                IDENT[:, :])
                    e.transpose(pb[:, 256:368], GX[0:112, 2 * D + 128 * c:2 * D + 128 * c + 128],
                                IDENT[0:112, 0:112]).then_inc(st.h, 1)
                em("tensor", xst_tr)
                xst_mm[c] = st.inc()

                def xst_evac(e, c=c, w=xst_mm[c], xrt_done=xrt_done):
                    pb = PB[4] if c % 2 == 0 else PB[5]
                    e.wait_ge(st.h, w)
                    if c == 0:
                        e.wait_ge(sv.h, xrt_done)
                    if bf16_exp:
                        s0, s1, s2 = SLOTS * c, SLOTS * c + 368, SLOTS * (c + 1)
                        e.tensor_copy(XSTH[:, s0:s1], pb[:, 0:368])
                        e.tensor_copy(XSTH[:, s1:s2], XRT[:, 16 * c:16 * (c + 1)])
                        e.drain()
                        e.tensor_tensor(XSTL[:, s0:s1], pb[:, 0:368],
                                        XSTH[:, s0:s1], ALU.subtract)
                        e.tensor_tensor(XSTL[:, s1:s2], XRT[:, 16 * c:16 * (c + 1)],
                                        XSTH[:, s1:s2], ALU.subtract).then_inc(sv.h, 1)
                    else:
                        e.tensor_copy(XST[:, SLOTS * c:SLOTS * c + 368], pb[:, 0:368])
                        e.tensor_copy(XST[:, SLOTS * c + 368:SLOTS * (c + 1)],
                                      XRT[:, 16 * c:16 * (c + 1)]).then_inc(sv.h, 1)
                em("vector", xst_evac)
                xst_ev[c] = sv.inc()
            xst_last_tr[eidx] = xst_mm[7]
            xst_ready = xst_ev[7]

            if eidx + 1 < E:
                gx_tgt[eidx + 1] = gather_expert(eidx + 1, wait_st=xst_mm[7])

            silu_done = {}
            hmul_done = {}
            hi_done = {}
            lo_done = {}
            for Ht in range(16):
                ub = PB[0] if Ht % 2 == 0 else PB[1]
                gb = PB[2] if Ht % 2 == 0 else PB[3]
                kk = len(ug_seq) - 16 + Ht

                def ug_mm(e, Ht=Ht, ub=ub, gb=gb, wt=ug_wt[Ht], xst_ready=xst_ready,
                          sd=silu_done.get(Ht - 2), hd=hmul_done.get(Ht - 2),
                          prev=prev_dn_evac):
                    e.wait_ge(qw.h, wt)
                    if Ht == 0:
                        e.wait_ge(sv.h, xst_ready)
                        e.wait_ge(sv.h, prev)
                    if sd is not None:
                        e.wait_ge(ss.h, sd)
                        e.wait_ge(sv.h, hd)
                    wbuf = (Ht % 2) * 2048
                    if bf16_exp:
                        base = 4096 * (Ht % 2)
                        for bank, gofs in ((ub, 0), (gb, 1024)):
                            for p, (wo, xsrc) in enumerate(((gofs, XSTH), (gofs, XSTL),
                                                            (2048 + gofs, XSTH))):
                                for c in range(8):
                                    last = e.matmul(
                                        bank[:, 0:SLOTS],
                                        WUGb[:, base + wo + 128 * c:base + wo + 128 * (c + 1)],
                                        xsrc[:, SLOTS * c:SLOTS * (c + 1)],
                                        start=(p == 0 and c == 0), stop=(p == 2 and c == 7))
                    else:
                        for c in range(8):
                            e.matmul(ub[:, 0:SLOTS], WUG[:, wbuf + 128 * c:wbuf + 128 * (c + 1)],
                                     XST[:, SLOTS * c:SLOTS * (c + 1)],
                                     start=(c == 0), stop=(c == 7))
                        for c in range(8):
                            last = e.matmul(gb[:, 0:SLOTS],
                                            WUG[:, wbuf + 1024 + 128 * c:wbuf + 1024 + 128 * (c + 1)],
                                            XST[:, SLOTS * c:SLOTS * (c + 1)],
                                            start=(c == 0), stop=(c == 7))
                    last.then_inc(st.h, 1)
                em("tensor", ug_mm)
                ug_done = st.inc()
                ug_seq[kk] = ug_done

                def silu_s(e, Ht=Ht, gb=gb, ug_done=ug_done):
                    e.wait_ge(st.h, ug_done)
                    so = 512 * (Ht % 2)
                    e.activation(SGS[:, so:so + SLOTS], gb[:, 0:SLOTS],
                                 ACT.Silu).then_inc(ss.h, 1)
                em("scalar", silu_s)
                silu_done[Ht] = ss.inc()

                if bf16_exp:
                    def hmul_v(e, Ht=Ht, ub=ub, sd=silu_done[Ht],
                               ph=hi_done.get(Ht - 2)):
                        e.wait_ge(ss.h, sd)
                        if ph is not None:
                            e.wait_ge(ss.h, ph)
                        so = 512 * (Ht % 2)
                        e.tensor_tensor(XNB[:, 1024 * (Ht % 2):1024 * (Ht % 2) + SLOTS],
                                        SGS[:, so:so + SLOTS], ub[:, 0:SLOTS],
                                        ALU.mult).then_inc(sv.h, 1)
                    em("vector", hmul_v)
                    hmul_done[Ht] = sv.inc()

                    def hi_s(e, Ht=Ht, hm=hmul_done[Ht]):
                        e.wait_ge(sv.h, hm)
                        e.activation(HALLH[:, SLOTS * Ht:SLOTS * (Ht + 1)],
                                     XNB[:, 1024 * (Ht % 2):1024 * (Ht % 2) + SLOTS],
                                     ACT.Copy).then_inc(ss.h, 1)
                    em("scalar", hi_s)
                    hi_done[Ht] = ss.inc()

                    def lo_v(e, Ht=Ht, hd2=hi_done[Ht]):
                        e.wait_ge(ss.h, hd2)
                        e.tensor_tensor(HALLL[:, SLOTS * Ht:SLOTS * (Ht + 1)],
                                        XNB[:, 1024 * (Ht % 2):1024 * (Ht % 2) + SLOTS],
                                        HALLH[:, SLOTS * Ht:SLOTS * (Ht + 1)],
                                        ALU.subtract).then_inc(sv.h, 1)
                    em("vector", lo_v)
                    lo_done[Ht] = sv.inc()
                else:
                    def hmul_v(e, Ht=Ht, ub=ub, sd=silu_done[Ht]):
                        e.wait_ge(ss.h, sd)
                        so = 512 * (Ht % 2)
                        e.tensor_tensor(HALL[:, SLOTS * Ht:SLOTS * (Ht + 1)],
                                        SGS[:, so:so + SLOTS], ub[:, 0:SLOTS],
                                        ALU.mult).then_inc(sv.h, 1)
                    em("vector", hmul_v)
                    hmul_done[Ht] = sv.inc()

                if riders:
                    def rug_hi(e, Ht=Ht, wt=ug_wt[Ht], prev=rid_prev["ug"],
                               xrt_done=xrt_done):
                        e.wait_ge(qw.h, wt)
                        if prev is not None:
                            e.wait_ge(qr.h, prev)
                        else:
                            e.wait_ge(sv.h, xrt_done)
                        wbuf = (Ht % 2) * 2048
                        e.activation(TSCb[:, 0:2048], WUG[:, wbuf:wbuf + 2048],
                                     ACT.Copy).then_inc(ss.h, 1)
                    em("scalar", rug_hi)
                    rh = ss.inc()

                    def rug_lo(e, Ht=Ht, rh=rh):
                        e.wait_ge(ss.h, rh)
                        wbuf = (Ht % 2) * 2048
                        e.tensor_tensor(XNBb[:, 0:2048], WUG[:, wbuf:wbuf + 2048],
                                        TSCb[:, 0:2048], ALU.subtract).then_inc(sv.h, 1)
                    em("vector", rug_lo)
                    rid_ug[kk] = sv.inc()

                    def rug_out(e, eidx=eidx, Ht=Ht, rl=rid_ug[kk]):
                        e.wait_ge(sv.h, rl)
                        e.dma_start(out=eug_scr[eidx, Ht, :, 0:2048],
                                    in_=TSCb[:, 0:2048]).then_inc(qr.h, 16)
                        e.dma_start(out=eug_scr[eidx, Ht, :, 2048:4096],
                                    in_=XNBb[:, 0:2048]).then_inc(qr.h, 16)
                    em("gpsimd", rug_out)
                    qr.inc(16)
                    rid_prev["ug"] = qr.inc(16)

            dn_ht = {}
            hall_ready = lo_done[15] if bf16_exp else hmul_done[15]
            for Ht in range(16):
                kk = len(dn_seq) - 16 + Ht

                def dn_mm(e, Ht=Ht, wt=dn_wt[Ht], hall_ready=hall_ready,
                          prev=prev_dn_evac):
                    e.wait_ge(qw.h, wt)
                    if Ht == 0:
                        e.wait_ge(sv.h, hall_ready)
                        e.wait_ge(sv.h, prev)
                    hofs = SLOTS * Ht
                    wofs = D * (Ht % 2)
                    if bf16_exp:
                        wb = 2048 * (Ht % 2)
                        for p, (hsrc, wo) in enumerate(((HALLH, 0), (HALLH, 1024),
                                                        (HALLL, 0))):
                            st_ = (Ht == 0 and p == 0)
                            sp_ = (Ht == 15 and p == 2)
                            for m in range(3):
                                for n in range(2):
                                    e.matmul(PB[m * 2 + n][0:128, 0:512],
                                             hsrc[:, hofs + 128 * m:hofs + 128 * (m + 1)],
                                             WDNb[:, wb + wo + 512 * n:wb + wo + 512 * (n + 1)],
                                             start=st_, stop=sp_)
                            for n in range(2):
                                last = e.matmul(PB[6 + n][0:16, 0:512],
                                                hsrc[:, hofs + 368:hofs + SLOTS],
                                                WDNb[:, wb + wo + 512 * n:wb + wo + 512 * (n + 1)],
                                                start=st_, stop=sp_)
                    else:
                        for m in range(3):
                            for n in range(2):
                                e.matmul(PB[m * 2 + n][0:128, 0:512],
                                         HALL[:, hofs + 128 * m:hofs + 128 * (m + 1)],
                                         WDN[:, wofs + 512 * n:wofs + 512 * (n + 1)],
                                         start=(Ht == 0), stop=(Ht == 15))
                        for n in range(2):
                            last = e.matmul(PB[6 + n][0:16, 0:512],
                                            HALL[:, hofs + 368:hofs + SLOTS],
                                            WDN[:, wofs + 512 * n:wofs + 512 * (n + 1)],
                                            start=(Ht == 0), stop=(Ht == 15))
                    last.then_inc(st.h, 1)
                em("tensor", dn_mm)
                dn_ht[Ht] = st.inc()
                dn_seq[kk] = dn_ht[Ht]

                if riders:
                    def rdn_hi(e, Ht=Ht, wt=dn_wt[Ht], prev=rid_prev["dn"],
                               xrt_done=xrt_done):
                        e.wait_ge(qw.h, wt)
                        if prev is not None:
                            e.wait_ge(qr.h, prev)
                        else:
                            e.wait_ge(sv.h, xrt_done)
                        wofs = D * (Ht % 2)
                        e.activation(XNBb[:, 2048:3072], WDN[:, wofs:wofs + D],
                                     ACT.Copy).then_inc(ss.h, 1)
                    em("scalar", rdn_hi)
                    rh = ss.inc()

                    def rdn_lo(e, Ht=Ht, rh=rh):
                        e.wait_ge(ss.h, rh)
                        wofs = D * (Ht % 2)
                        e.tensor_tensor(XNBb[:, 3072:4096], WDN[:, wofs:wofs + D],
                                        XNBb[:, 2048:3072], ALU.subtract).then_inc(sv.h, 1)
                    em("vector", rdn_lo)
                    rid_dn[kk] = sv.inc()

                    def rdn_out(e, eidx=eidx, Ht=Ht, rl=rid_dn[kk]):
                        e.wait_ge(sv.h, rl)
                        e.dma_start(out=edn_scr[eidx, Ht, :, 0:1024],
                                    in_=XNBb[:, 2048:3072]).then_inc(qr.h, 16)
                        e.dma_start(out=edn_scr[eidx, Ht, :, 1024:2048],
                                    in_=XNBb[:, 3072:4096]).then_inc(qr.h, 16)
                    em("gpsimd", rdn_out)
                    qr.inc(16)
                    rid_prev["dn"] = qr.inc(16)
            dn_done = dn_ht[15]

            def dn_evac(e, eidx=eidx, dn_done=dn_done,
                        prev_store=ys_store_tgt.get(eidx - 1)):
                e.wait_ge(st.h, dn_done)
                if prev_store is not None:
                    e.wait_ge(qw.h, prev_store)
                for m in (2, 0, 1):
                    rows = 128 if m < 2 else 112
                    e.tensor_copy(YSS[0:rows, 1024 * m:1024 * m + 512], PB[m * 2][0:rows, 0:512])
                    e.tensor_copy(YSS[0:rows, 1024 * m + 512:1024 * (m + 1)],
                                  PB[m * 2 + 1][0:rows, 0:512]).then_inc(sv.h, 1)
                e.scalar_tensor_tensor(YRSEL[:, 0:512], PB[6][0:16, 0:512],
                                       OHR[:, eidx:eidx + 1],
                                       YRSEL[:, 0:512], ALU.mult, ALU.add)
                e.drain()
                e.scalar_tensor_tensor(YRSEL[:, 512:1024], PB[7][0:16, 0:512],
                                       OHR[:, eidx:eidx + 1],
                                       YRSEL[:, 512:1024], ALU.mult, ALU.add).then_inc(sv.h, 1)
            em("vector", dn_evac)
            b45_free = sv.inc()
            yss_m = {2: b45_free, 0: sv.inc(), 1: sv.inc()}
            prev_b45 = b45_free
            prev_dn_evac = sv.inc()

            if debug and first_call and eidx == 0:
                def dbgdump0(e, w=hmul_done[15], w2=prev_dn_evac):
                    e.wait_ge(sv.h, w)
                    e.dma_start(out=dbg_hall_h[:, :], in_=HALL[:, :]).then_inc(qw.h, 16)
                    e.dma_start(out=dbg_xst_h[:, :], in_=XST[:, :]).then_inc(qw.h, 16)
                    e.wait_ge(sv.h, w2)
                    e.dma_start(out=dbg_wdn_h[:, :], in_=WDN[:, :]).then_inc(qw.h, 16)
                    e.dma_start(out=dbg_yss_h[0:112, :], in_=YSS[0:112, :]).then_inc(qw.h, 16)
                em("sync", dbgdump0)
                for _ in range(4):
                    qw.inc(16)

            for m in range(3):
                rows = 128 if m < 2 else 112

                def ysdma(e, eidx=eidx, m=m, rows=rows, w=yss_m[m]):
                    e.wait_ge(sv.h, w)
                    e.dma_start(out=ys_h[eidx * CAP + 128 * m:eidx * CAP + 128 * m + rows, :],
                                in_=YSS[0:rows, 1024 * m:1024 * (m + 1)]).then_inc(qw.h, 16)
                em("sync", ysdma)
                tgt = qw.inc(16)
            ys_store_tgt[eidx] = tgt

        experts_done_v = prev_dn_evac
        ys_all_stored = ys_store_tgt[E - 1]

        stage_dst = [YSS[:, 0:1024], YSS[:, 1024:2048], YSS[:, 2048:3072], TSC[:, 0:1024]]
        sq2h = {}
        xtr2_last = {}
        pend = None

        def flush_comb(pend):
            t0p, psev_d, ysg_p, base_idx, gx_drain = pend
            for m in range(4):
                t = t0p + m

                def comb_v(e, t=t, m=m, psev_d=psev_d, ysg_t=ysg_p[m],
                           gx_drain=gx_drain):
                    if m == 0:
                        e.wait_ge(sv.h, psev_d)
                        if gx_drain is not None:
                            e.wait_ge(qr.h, gx_drain)
                    e.wait_ge(qg.h, ysg_t)
                    stg = stage_dst[m]
                    for n in range(2):
                        e.scalar_tensor_tensor(stg[:, 512 * n:512 * (n + 1)],
                                               YSG[:, 512 * n:512 * (n + 1)],
                                               AMBC[:, t:t + 1],
                                               stg[:, 512 * n:512 * (n + 1)],
                                               ALU.mult, ALU.add)
                    e.drain()
                    for n in range(2):
                        Xsl = X[:, D * t + 512 * n:D * t + 512 * (n + 1)]
                        last = e.scalar_tensor_tensor(Xsl, stg[:, 512 * n:512 * (n + 1)],
                                                      0.5, Xsl, ALU.mult, ALU.add)
                    last.then_inc(sv.h, 1)
                em("vector", comb_v)
                comb_seq[base_idx + m] = sv.inc()
                if attn_stats_out is not None:
                    attn_stats_out[t] = attn_stats_tile(t, sq2h)

        for sI2 in range(T // 512):
            ubf = bf16_sh or (_F1SH_BF16 and riders and sI2 >= 1)
            scr_rdy_ug = rid_prev["sdn"] if (riders and sI2 >= 1) else None
            scr_rdy_dn = rid_prev["sdn"] if (riders and sI2 >= 1) else None
            t0 = sI2 * 4
            trd_last = {}
            last_ev = None
            for i in range(4):
                t = t0 + i
                xn = XNB[:, D * (i % 2):D * (i % 2 + 1)]
                Xt = X[:, D * t:D * (t + 1)]

                def rb2(e, xn=xn, Xt=Xt, t=t, sI2=sI2, i=i,
                        prevtr=xtr2_last.get(sI2 - 1), slottr=trd_last.get(i - 2),
                        rdrain=(rid_prev["dn"] if riders else None)):
                    if sI2 == 0 and i == 0:
                        e.wait_ge(st.h, rg_all)
                        if rdrain is not None:
                            e.wait_ge(qr.h, rdrain)
                    if i == 0 and prevtr is not None:
                        e.wait_ge(st.h, prevtr)
                    if slottr is not None:
                        e.wait_ge(st.h, slottr)
                    e.tensor_scalar(xn, Xt, STAT[:, t:t + 1], STAT[:, 16 + t:17 + t],
                                    ALU.subtract, ALU.mult).then_inc(sv.h, 1)
                em("vector", rb2)
                rb2_d = sv.inc()

                trd = {}
                ev2 = {}
                for c in range(8):
                    def xtr2(e, xn=xn, i=i, c=c, sI2=sI2, rb2_d=rb2_d,
                             prev=prev_b45, evp=ev2.get(c - 2)):
                        pb = PB[4] if c % 2 == 0 else PB[5]
                        if c == 0:
                            e.wait_ge(sv.h, rb2_d)
                            if sI2 == 0 and i == 0:
                                e.wait_ge(sv.h, prev)
                        if evp is not None:
                            e.wait_ge(sv.h, evp)
                        e.transpose(pb[:, 128 * i:128 * i + 128],
                                    xn[:, 128 * c:128 * (c + 1)],
                                    IDENT[:, :]).then_inc(st.h, 1)
                    em("tensor", xtr2)
                    trd[c] = st.inc()

                    def xtr2_ev(e, i=i, c=c, w=trd[c], ubf=ubf):
                        pb = PB[4] if c % 2 == 0 else PB[5]
                        e.wait_ge(st.h, w)
                        col = 512 * c + 128 * i
                        if ubf:
                            e.tensor_copy(XNTH[:, col:col + 128], pb[:, 128 * i:128 * i + 128])
                            e.drain()
                            e.tensor_tensor(XNTL[:, col:col + 128], pb[:, 128 * i:128 * i + 128],
                                            XNTH[:, col:col + 128], ALU.subtract).then_inc(sv.h, 1)
                        else:
                            e.tensor_copy(XNT[:, col:col + 128],
                                          pb[:, 128 * i:128 * i + 128]).then_inc(sv.h, 1)
                    em("vector", xtr2_ev)
                    ev2[c] = sv.inc()
                trd_last[i] = trd[7]
                last_ev = ev2[7]
            xtr2_last[sI2] = trd_last[3]
            xnt2_ready = last_ev

            sh_ug_wt = {}
            for Ht in range(16):
                k = len(ug_seq)
                wbuf = (Ht % 2) * 2048

                def wdma2(e, Ht=Ht, wbuf=wbuf, k=k, scr_rdy=scr_rdy_ug, ubf=ubf):
                    if k >= 2:
                        e.wait_ge(st.h, ug_seq[k - 2])
                        if rid_ug[k - 2] is not None:
                            e.wait_ge(sv.h, rid_ug[k - 2])
                    if scr_rdy is not None:
                        e.wait_ge(qr.h, scr_rdy)
                    if ubf:
                        e.dma_start(out=WUGb[:, 2 * wbuf:2 * wbuf + 2048],
                                    in_=sug_scr[Ht, :, 0:2048]).then_inc(qw.h, 16)
                        e.dma_start(out=WUGb[:, 2 * wbuf + 2048:2 * wbuf + 4096],
                                    in_=sug_scr[Ht, :, 2048:4096]).then_inc(qw.h, 16)
                    else:
                        e.dma_start(out=WUG[:, wbuf:wbuf + 1024].rearrange("p (c q) -> p c q", c=8),
                                    in_=wupT_h[Ht].rearrange("c p q -> p c q")).then_inc(qw.h, 16)
                        e.dma_start(out=WUG[:, wbuf + 1024:wbuf + 2048].rearrange("p (c q) -> p c q", c=8),
                                    in_=wgateT_h[Ht].rearrange("c p q -> p c q")).then_inc(qw.h, 16)
                em("sync", wdma2)
                qw.inc(16)
                sh_ug_wt[Ht] = qw.inc(16)
                ug_seq.append(None)
                rid_ug.append(None)
            sh_dn_wt = {}
            for Ht in range(16):
                k = len(dn_seq)

                def ddma2(e, Ht=Ht, k=k, scr_rdy=scr_rdy_dn, ubf=ubf):
                    if k >= 2:
                        e.wait_ge(st.h, dn_seq[k - 2])
                        if rid_dn[k - 2] is not None:
                            e.wait_ge(sv.h, rid_dn[k - 2])
                    if scr_rdy is not None:
                        e.wait_ge(qr.h, scr_rdy)
                    if ubf:
                        e.dma_start(out=WDNb[:, 2048 * (Ht % 2):2048 * (Ht % 2) + 2048],
                                    in_=sdn_scr[Ht]).then_inc(qw.h, 16)
                    else:
                        e.dma_start(out=WDN[:, D * (Ht % 2):D * (Ht % 2) + D],
                                    in_=wdownT_h[Ht]).then_inc(qw.h, 16)
                em("sync", ddma2)
                sh_dn_wt[Ht] = qw.inc(16)
                dn_seq.append(None)
                rid_dn.append(None)

            silu2_done = {}
            hmul2_done = {}
            hi2_done = {}
            lo2_done = {}
            for Ht in range(16):
                ub = PB[0] if Ht % 2 == 0 else PB[1]
                gb = PB[2] if Ht % 2 == 0 else PB[3]
                kk = len(ug_seq) - 16 + Ht

                def ug2_mm(e, Ht=Ht, ub=ub, gb=gb, wt=sh_ug_wt[Ht],
                           xnt2_ready=xnt2_ready, sd=silu2_done.get(Ht - 2),
                           hd=hmul2_done.get(Ht - 2), ubf=ubf):
                    e.wait_ge(qw.h, wt)
                    if Ht == 0:
                        e.wait_ge(sv.h, xnt2_ready)
                    if sd is not None:
                        e.wait_ge(ss.h, sd)
                        e.wait_ge(sv.h, hd)
                    wbuf = (Ht % 2) * 2048
                    if ubf:
                        base = 4096 * (Ht % 2)
                        for bank, gofs in ((ub, 0), (gb, 1024)):
                            for p, (wo, xsrc) in enumerate(((gofs, XNTH), (gofs, XNTL),
                                                            (2048 + gofs, XNTH))):
                                for c in range(8):
                                    last = e.matmul(
                                        bank[:, 0:512],
                                        WUGb[:, base + wo + 128 * c:base + wo + 128 * (c + 1)],
                                        xsrc[:, 512 * c:512 * (c + 1)],
                                        start=(p == 0 and c == 0), stop=(p == 2 and c == 7))
                    else:
                        for c in range(8):
                            e.matmul(ub[:, 0:512], WUG[:, wbuf + 128 * c:wbuf + 128 * (c + 1)],
                                     XNT[:, 512 * c:512 * (c + 1)], start=(c == 0), stop=(c == 7))
                        for c in range(8):
                            last = e.matmul(gb[:, 0:512],
                                            WUG[:, wbuf + 1024 + 128 * c:wbuf + 1024 + 128 * (c + 1)],
                                            XNT[:, 512 * c:512 * (c + 1)], start=(c == 0), stop=(c == 7))
                    last.then_inc(st.h, 1)
                em("tensor", ug2_mm)
                ug2_done = st.inc()
                ug_seq[kk] = ug2_done

                def silu2_s(e, Ht=Ht, gb=gb, ug2_done=ug2_done):
                    e.wait_ge(st.h, ug2_done)
                    so = 512 * (Ht % 2)
                    e.activation(SGS[:, so:so + 512], gb[:, 0:512], ACT.Silu).then_inc(ss.h, 1)
                em("scalar", silu2_s)
                silu2_done[Ht] = ss.inc()

                if ubf:
                    def hmul2_v(e, Ht=Ht, ub=ub, sd=silu2_done[Ht],
                                ph=hi2_done.get(Ht - 2)):
                        e.wait_ge(ss.h, sd)
                        if ph is not None:
                            e.wait_ge(ss.h, ph)
                        so = 512 * (Ht % 2)
                        e.tensor_tensor(XNB[:, 1024 * (Ht % 2):1024 * (Ht % 2) + 512],
                                        SGS[:, so:so + 512], ub[:, 0:512],
                                        ALU.mult).then_inc(sv.h, 1)
                    em("vector", hmul2_v)
                    hmul2_done[Ht] = sv.inc()

                    def hi2_s(e, Ht=Ht, hm=hmul2_done[Ht]):
                        e.wait_ge(sv.h, hm)
                        e.activation(HS2H[:, 512 * Ht:512 * (Ht + 1)],
                                     XNB[:, 1024 * (Ht % 2):1024 * (Ht % 2) + 512],
                                     ACT.Copy).then_inc(ss.h, 1)
                    em("scalar", hi2_s)
                    hi2_done[Ht] = ss.inc()

                    def lo2_v(e, Ht=Ht, hd2=hi2_done[Ht]):
                        e.wait_ge(ss.h, hd2)
                        e.tensor_tensor(HS2L[:, 512 * Ht:512 * (Ht + 1)],
                                        XNB[:, 1024 * (Ht % 2):1024 * (Ht % 2) + 512],
                                        HS2H[:, 512 * Ht:512 * (Ht + 1)],
                                        ALU.subtract).then_inc(sv.h, 1)
                    em("vector", lo2_v)
                    lo2_done[Ht] = sv.inc()
                else:
                    def hmul2_v(e, Ht=Ht, ub=ub, sd=silu2_done[Ht]):
                        e.wait_ge(ss.h, sd)
                        so = 512 * (Ht % 2)
                        e.tensor_tensor(HS2[:, 512 * Ht:512 * (Ht + 1)], SGS[:, so:so + 512],
                                        ub[:, 0:512], ALU.mult).then_inc(sv.h, 1)
                    em("vector", hmul2_v)
                    hmul2_done[Ht] = sv.inc()

                if riders and sI2 == 0:
                    def rsug_hi(e, Ht=Ht, wt=sh_ug_wt[Ht], prev=rid_prev["sug"],
                                gxfree=xst_last_tr[E - 1]):
                        e.wait_ge(qw.h, wt)
                        if prev is not None:
                            e.wait_ge(qr.h, prev)
                        else:
                            e.wait_ge(st.h, gxfree)
                        wbuf = (Ht % 2) * 2048
                        e.activation(GXb[:, 0:2048], WUG[:, wbuf:wbuf + 2048],
                                     ACT.Copy).then_inc(ss.h, 1)
                    em("scalar", rsug_hi)
                    rh = ss.inc()

                    def rsug_lo(e, Ht=Ht, rh=rh):
                        e.wait_ge(ss.h, rh)
                        wbuf = (Ht % 2) * 2048
                        e.tensor_tensor(GXb[:, 2048:4096], WUG[:, wbuf:wbuf + 2048],
                                        GXb[:, 0:2048], ALU.subtract).then_inc(sv.h, 1)
                    em("vector", rsug_lo)
                    rid_ug[kk] = sv.inc()

                    def rsug_out(e, Ht=Ht, rl=rid_ug[kk]):
                        e.wait_ge(sv.h, rl)
                        e.dma_start(out=sug_scr[Ht, :, 0:2048],
                                    in_=GXb[:, 0:2048]).then_inc(qr.h, 16)
                        e.dma_start(out=sug_scr[Ht, :, 2048:4096],
                                    in_=GXb[:, 2048:4096]).then_inc(qr.h, 16)
                    em("gpsimd", rsug_out)
                    qr.inc(16)
                    rid_prev["sug"] = qr.inc(16)

            if pend is not None:
                flush_comb(pend)

            sh_hall_ready = lo2_done[15] if ubf else hmul2_done[15]
            for Ht in range(16):
                kk = len(dn_seq) - 16 + Ht

                def dn2_ht(e, Ht=Ht, hall_ready=sh_hall_ready, ev=experts_done_v,
                           wt=sh_dn_wt[Ht], ubf=ubf):
                    e.wait_ge(qw.h, wt)
                    if Ht == 0:
                        e.wait_ge(sv.h, hall_ready)
                        e.wait_ge(sv.h, ev)
                    hofs = 512 * Ht
                    wofs = D * (Ht % 2)
                    if ubf:
                        wb = 2048 * (Ht % 2)
                        for p, (hsrc, wo) in enumerate(((HS2H, 0), (HS2H, 1024),
                                                        (HS2L, 0))):
                            st_ = (Ht == 0 and p == 0)
                            for m in range(4):
                                for n in range(2):
                                    last = e.matmul(
                                        PB[m * 2 + n][0:128, 0:512],
                                        hsrc[:, hofs + 128 * m:hofs + 128 * (m + 1)],
                                        WDNb[:, wb + wo + 512 * n:wb + wo + 512 * (n + 1)],
                                        start=st_, stop=False)
                    else:
                        for m in range(4):
                            for n in range(2):
                                last = e.matmul(PB[m * 2 + n][0:128, 0:512],
                                                HS2[:, hofs + 128 * m:hofs + 128 * (m + 1)],
                                                WDN[:, wofs + 512 * n:wofs + 512 * (n + 1)],
                                                start=(Ht == 0), stop=False)
                    last.then_inc(st.h, 1)
                em("tensor", dn2_ht)
                dn_seq[kk] = st.inc()

                if riders and sI2 == 0:
                    def rsdn_hi(e, Ht=Ht, wt=sh_dn_wt[Ht], prev=rid_prev["sdn"],
                                gxfree=xst_last_tr[E - 1]):
                        e.wait_ge(qw.h, wt)
                        if prev is not None:
                            e.wait_ge(qr.h, prev)
                        else:
                            e.wait_ge(st.h, gxfree)
                        wofs = D * (Ht % 2)
                        e.activation(GXb[:, 4096:5120], WDN[:, wofs:wofs + D],
                                     ACT.Copy).then_inc(ss.h, 1)
                    em("scalar", rsdn_hi)
                    rh = ss.inc()

                    def rsdn_lo(e, Ht=Ht, rh=rh):
                        e.wait_ge(ss.h, rh)
                        wofs = D * (Ht % 2)
                        e.tensor_tensor(GXb[:, 5120:6144], WDN[:, wofs:wofs + D],
                                        GXb[:, 4096:5120], ALU.subtract).then_inc(sv.h, 1)
                    em("vector", rsdn_lo)
                    rid_dn[kk] = sv.inc()

                    def rsdn_out(e, Ht=Ht, rl=rid_dn[kk]):
                        e.wait_ge(sv.h, rl)
                        e.dma_start(out=sdn_scr[Ht, :, 0:1024],
                                    in_=GXb[:, 4096:5120]).then_inc(qr.h, 16)
                        e.dma_start(out=sdn_scr[Ht, :, 1024:2048],
                                    in_=GXb[:, 5120:6144]).then_inc(qr.h, 16)
                    em("gpsimd", rsdn_out)
                    qr.inc(16)
                    rid_prev["sdn"] = qr.inc(16)

            def dn2_fin(e, t0=t0):
                for m in range(4):
                    t = t0 + m
                    for n in range(2):
                        last = e.matmul(PB[m * 2 + n][0:128, 0:512],
                                        GRT[:, 128 * t:128 * (t + 1)],
                                        YRSEL[:, 512 * n:512 * (n + 1)],
                                        start=False, stop=True)
                last.then_inc(st.h, 1)
            em("tensor", dn2_fin)
            dn2_done = st.inc()

            ysg_tgts = []
            for i in range(4):
                t = t0 + i
                kcomb = len(comb_seq)

                def ysga(e, t=t, kcomb=kcomb):
                    e.wait_ge(qw.h, ys_all_stored)
                    if kcomb >= 1:
                        e.wait_ge(sv.h, comb_seq[kcomb - 1])
                    e.indirect_dma_start(
                        out=YSG[:, 0:D], out_offset=None,
                        in_=ys_h[:, :], in_offset=IndirectOffsetOnAxis(DESTI[:, t:t + 1], 0),
                    ).then_inc(qg.h, 16)
                em("gpsimd", ysga)
                ysg_tgts.append(qg.inc(16))
                comb_seq.append(None)
            base_idx = len(comb_seq) - 4

            def psev(e, sI2=sI2, dn2_done=dn2_done, yas=ys_all_stored):
                e.wait_ge(st.h, dn2_done)
                if sI2 == 0:
                    e.wait_ge(qw.h, yas)
                for m in range(4):
                    stg = stage_dst[m]
                    for n in range(2):
                        last = e.tensor_copy(stg[:, 512 * n:512 * (n + 1)],
                                             PB[m * 2 + n][0:128, 0:512])
                last.then_inc(sv.h, 1)
            em("vector", psev)
            psev_d = sv.inc()
            pend = (t0, psev_d, ysg_tgts, base_idx,
                    rid_prev["sdn"] if (riders and sI2 == 0) else None)

        flush_comb(pend)
        return sv.n

    def attn_stage(level=5, stats_pre=None, emit_f3=False):
        em("vector", lambda e: e.memset(SST[:, :], 0.0).then_inc(sv.h, 1))
        sst_zeroed = sv.inc()
        dma("sync", GAM, gam_h[:, :], qw)
        gb_ld = dma("sync", BET, bet_h[:, :], qw)
        ba_ld = dma("sync", BA, ba_h[:, :], qw)
        for c in range(4):
            dma("sync", WOS[:, D * c:D * (c + 1)], wo_h[128 * c:128 * (c + 1), :], qw)
        attld = qw.n

        if level < 2:
            return None
        f3_stats = {} if emit_f3 else None
        if stats_pre is not None:
            stats_ready = stats_pre
        else:
            stats_ready = {}
            sq2_tgt = {}
            for t in range(ntile):
                par = 16 * (t % 2)
                Xt = X[:, D * t:D * (t + 1)]
                xl = XNB[:, D * (t % 2):D * (t % 2 + 1)]

                def st_v(e, t=t, Xt=Xt, par=par):
                    e.reduce_sum(RT1[:, par:par + 1], Xt, AX.X)
                    e.drain()
                    e.tensor_scalar(STAT[:, t:t + 1], RT1[:, par:par + 1], 1.0 / D,
                                    None, ALU.mult).then_inc(sv.h, 1)
                em("vector", st_v)
                m1d = sv.inc()

                def st_s(e, Xt=Xt, par=par, m1d=m1d):
                    e.wait_ge(sv.h, m1d)
                    e.activation(SGS[:, :], Xt, ACT.Square,
                                 accum_out=RT1[:, par + 1:par + 2]).then_inc(ss.h, 1)
                em("scalar", st_s)
                sq1 = ss.inc()

                def var1_v(e, t=t, par=par, sq1=sq1):
                    e.wait_ge(ss.h, sq1)
                    e.tensor_tensor(RT1[:, par + 2:par + 3], STAT[:, t:t + 1],
                                    STAT[:, t:t + 1], ALU.mult)
                    e.drain()
                    e.tensor_scalar(RT1[:, par + 3:par + 4], RT1[:, par + 1:par + 2],
                                    1.0 / D, RT1[:, par + 2:par + 3],
                                    ALU.mult, ALU.subtract).then_inc(sv.h, 1)
                em("vector", var1_v)
                v1d = sv.inc()

                def sq1_s(e, par=par, v1d=v1d):
                    e.wait_ge(sv.h, v1d)
                    e.activation(RT1[:, par + 4:par + 5], RT1[:, par + 3:par + 4],
                                 ACT.Sqrt, bias=CST[:, 1:2]).then_inc(ss.h, 1)
                em("scalar", sq1_s)
                s1d = ss.inc()

                def xl_v(e, t=t, par=par, Xt=Xt, xl=xl, s1d=s1d,
                         prev_sq2=sq2_tgt.get(t - 2)):
                    e.wait_ge(ss.h, s1d)
                    if prev_sq2 is not None:
                        e.wait_ge(ss.h, prev_sq2)
                    e.reciprocal(STAT[:, 16 + t:17 + t], RT1[:, par + 4:par + 5])
                    e.drain()
                    e.tensor_scalar(xl, Xt, STAT[:, t:t + 1], STAT[:, 16 + t:17 + t],
                                    ALU.subtract, ALU.mult)
                    e.drain()
                    e.reduce_sum(RT1[:, par:par + 1], xl, AX.X)
                    e.drain()
                    e.tensor_scalar(STAT[:, 32 + t:33 + t], RT1[:, par:par + 1], 1.0 / D,
                                    None, ALU.mult).then_inc(sv.h, 1)
                em("vector", xl_v)
                m2d = sv.inc()

                def sq2_s(e, xl=xl, par=par, m2d=m2d):
                    e.wait_ge(sv.h, m2d)
                    e.activation(SGS[:, :], xl, ACT.Square,
                                 accum_out=RT1[:, par + 1:par + 2]).then_inc(ss.h, 1)
                em("scalar", sq2_s)
                sq2 = ss.inc()
                sq2_tgt[t] = sq2

                def var2_v(e, t=t, par=par, sq2=sq2):
                    e.wait_ge(ss.h, sq2)
                    e.tensor_tensor(RT1[:, par + 2:par + 3], STAT[:, 32 + t:33 + t],
                                    STAT[:, 32 + t:33 + t], ALU.mult)
                    e.drain()
                    e.tensor_scalar(RT1[:, par + 3:par + 4], RT1[:, par + 1:par + 2],
                                    1.0 / D, RT1[:, par + 2:par + 3],
                                    ALU.mult, ALU.subtract).then_inc(sv.h, 1)
                em("vector", var2_v)
                v2d = sv.inc()

                def sq2b_s(e, par=par, v2d=v2d):
                    e.wait_ge(sv.h, v2d)
                    e.activation(RT1[:, par + 5:par + 6], RT1[:, par + 3:par + 4],
                                 ACT.Sqrt, bias=CST[:, 1:2]).then_inc(ss.h, 1)
                em("scalar", sq2b_s)
                s2d = ss.inc()

                def r2_v(e, t=t, par=par, s2d=s2d):
                    e.wait_ge(ss.h, s2d)
                    e.reciprocal(STAT[:, 48 + t:49 + t], RT1[:, par + 5:par + 6]).then_inc(sv.h, 1)
                em("vector", r2_v)
                stats_ready[t] = sv.inc()

        if level < 3:
            return f3_stats
        s_upd_prev = {h: sst_zeroed for h in range(NH)}
        pref_tgt = {}
        pend_att = [None]

        def flush_att():
            if pend_att[0] is None:
                return
            fn, sI_of, is3 = pend_att[0]
            pend_att[0] = None
            em("vector", fn)
            sv.inc()
            if is3 and emit_f3:
                for i3 in range(4):
                    tg = sI_of * 4 + i3
                    f3_stats[tg] = ffn_stats_tile(tg)
        for sI in range(T // 512):
            for i in range(4):
                t = sI * 4 + i
                Xt = X[:, D * t:D * (t + 1)]
                if i == 2:
                    flush_att()
                pre = pref_tgt.get((sI, i))
                if pre is not None:
                    xin_d = pre
                    xsrc = XNB[:, 1024 * i:1024 * (i + 1)]
                else:
                    xsrc = TSC

                def xin_v(e, t=t, Xt=Xt, sr=stats_ready[t], gb_ld=gb_ld):
                    e.wait_ge(sv.h, sr)
                    e.wait_ge(qw.h, gb_ld)
                    e.tensor_scalar(TSC[:, :], Xt, STAT[:, t:t + 1], STAT[:, 16 + t:17 + t],
                                    ALU.subtract, ALU.mult)
                    e.drain()
                    e.tensor_scalar(TSC[:, :], TSC[:, :], STAT[:, 32 + t:33 + t],
                                    STAT[:, 48 + t:49 + t], ALU.subtract, ALU.mult)
                    e.drain()
                    e.tensor_tensor(TSC[:, :], TSC[:, :], GAM, ALU.mult)
                    e.drain()
                    e.tensor_tensor(TSC[:, :], TSC[:, :], BET, ALU.add).then_inc(sv.h, 1)
                if pre is None:
                    em("vector", xin_v)
                else:
                    def xin_nop(e, sr=stats_ready[t]):
                        e.wait_ge(sv.h, sr)
                        e.nop().then_inc(sv.h, 1)
                    em("vector", xin_nop)
                xin_d = sv.inc()

                trd = {}
                evx = {}
                for c in range(8):
                    def xin_tr(e, i=i, c=c, xin_d=xin_d, evp=evx.get(c - 2), xsrc=xsrc):
                        pb = PB[4] if c % 2 == 0 else PB[5]
                        if c == 0:
                            e.wait_ge(sv.h, xin_d)
                        if evp is not None:
                            e.wait_ge(sv.h, evp)
                        e.transpose(pb[:, 128 * i:128 * i + 128], xsrc[:, 128 * c:128 * (c + 1)],
                                    IDENT[:, :]).then_inc(st.h, 1)
                    em("tensor", xin_tr)
                    trd[c] = st.inc()

                    def xin_ev(e, i=i, c=c, w=trd[c]):
                        pb = PB[4] if c % 2 == 0 else PB[5]
                        e.wait_ge(st.h, w)
                        e.tensor_copy(XNT[:, 512 * c + 128 * i:512 * c + 128 * (i + 1)],
                                      pb[:, 128 * i:128 * i + 128]).then_inc(sv.h, 1)
                    em("vector", xin_ev)
                    evx[c] = sv.inc()
                em("vector", lambda e: e.nop().then_inc(sv.h, 1))
                sv.inc()
            xnt_ready = sv.n

            proj_evac = {}
            for wi, (wh, dest) in enumerate(((wq_h, QS), (wk_h, KS), (wv_h, VS), (wa_h, ALA))):
                ctgts = {}
                for c in range(8):
                    k = len(wat_seq)

                    def pwdma(e, wh=wh, c=c, k=k):
                        if k >= 2:
                            e.wait_ge(st.h, wat_seq[k - 2])
                        e.dma_start(out=WAT[:, DI * (c % 2):DI * (c % 2) + DI],
                                    in_=wh[128 * c:128 * (c + 1), :]).then_inc(qw.h, 16)
                    em("sync", pwdma)
                    ctgts[c] = qw.inc(16)
                    wat_seq.append(None)

                pj_done = None
                for c in range(8):
                    kk = len(wat_seq) - 8 + c

                    def proj_mm(e, wi=wi, c=c, wt=ctgts[c], xnt_ready=xnt_ready,
                                pe=proj_evac.get(wi - 1)):
                        e.wait_ge(qw.h, wt)
                        if c == 0:
                            if wi == 0:
                                e.wait_ge(sv.h, xnt_ready)
                            else:
                                e.wait_ge(sv.h, pe)
                        for i in range(4):
                            last = e.matmul(PB[i][:, 0:DI],
                                            XNT[:, 512 * c + 128 * i:512 * c + 128 * (i + 1)],
                                            WAT[:, DI * (c % 2):DI * (c % 2) + DI],
                                            start=(c == 0), stop=(c == 7))
                        last.then_inc(st.h, 1)
                    em("tensor", proj_mm)
                    pj_done = st.inc()
                    wat_seq[kk] = pj_done

                def proj_ev(e, dest=dest, wi=wi, pj_done=pj_done, ba_ld=ba_ld):
                    e.wait_ge(st.h, pj_done)
                    if wi == 3:
                        e.wait_ge(qw.h, ba_ld)
                    for i in range(4):
                        if wi == 3:
                            last = e.scalar_tensor_tensor(dest[:, DI * i:DI * (i + 1)],
                                                          PB[i][:, 0:DI], 1.0, BA,
                                                          ALU.mult, ALU.add)
                        else:
                            last = e.tensor_copy(dest[:, DI * i:DI * (i + 1)], PB[i][:, 0:DI])
                    last.then_inc(sv.h, 1)
                em("vector", proj_ev)
                proj_evac[wi] = sv.inc()
            proj_all = proj_evac[3]

            if level < 4:
                continue

            sig_d = {}
            la_d = {}
            for i2 in range(4):
                def sig_s(e, i2=i2, proj_all=proj_all):
                    if i2 == 0:
                        e.wait_ge(sv.h, proj_all)
                    e.activation(ALA[:, DI * i2:DI * (i2 + 1)], ALA[:, DI * i2:DI * (i2 + 1)],
                                 ACT.Sigmoid).then_inc(ss.h, 1)
                em("scalar", sig_s)
                sig_d[i2] = ss.inc()
            for i2 in range(4):
                def ln2_s(e, i2=i2):
                    e.activation(ALA[:, DI * i2:DI * (i2 + 1)], ALA[:, DI * i2:DI * (i2 + 1)],
                                 ACT.Ln).then_inc(ss.h, 1)
                em("scalar", ln2_s)
                la_d[i2] = ss.inc()

            knorm_d = {}
            ksq_tgt = {}
            for i in range(4):
                kp = 16 * (i % 2)

                def kss_v(e, i=i, kp=kp, proj_all=proj_all, prev_ksq=ksq_tgt.get(i - 2)):
                    e.wait_ge(sv.h, proj_all)
                    if prev_ksq is not None:
                        e.wait_ge(ss.h, prev_ksq)
                    e.tensor_tensor(TSC[:, 0:DI], KS[:, DI * i:DI * (i + 1)],
                                    KS[:, DI * i:DI * (i + 1)], ALU.mult)
                    e.drain()
                    for h in range(NH):
                        last = e.reduce_sum(KNRM[:, kp + h:kp + h + 1],
                                            TSC[:, 64 * h:64 * (h + 1)], AX.X)
                    last.then_inc(sv.h, 1)
                em("vector", kss_v)
                kss_d = sv.inc()

                def ksq_s(e, kp=kp, kss_d=kss_d):
                    e.wait_ge(sv.h, kss_d)
                    e.activation(KNRM[:, kp + 8:kp + 16], KNRM[:, kp:kp + 8],
                                 ACT.Sqrt).then_inc(ss.h, 1)
                em("scalar", ksq_s)
                ksq_d = ss.inc()
                ksq_tgt[i] = ksq_d

                def khat_v(e, i=i, kp=kp, ksq_d=ksq_d):
                    e.wait_ge(ss.h, ksq_d)
                    e.tensor_scalar_max(KNRM[:, kp + 8:kp + 16], KNRM[:, kp + 8:kp + 16], 1e-12)
                    e.drain()
                    e.reciprocal(KNRM[:, kp:kp + 8], KNRM[:, kp + 8:kp + 16])
                    e.drain()
                    for h in range(NH):
                        sl = slice(DI * i + 64 * h, DI * i + 64 * (h + 1))
                        last = e.tensor_scalar(KS[:, sl], KS[:, sl], KNRM[:, kp + h:kp + h + 1],
                                               None, ALU.mult)
                    last.then_inc(sv.h, 1)
                em("vector", khat_v)
                knorm_d[i] = sv.inc()

            if level < 5:
                continue
            chunk_ready = {}
            dec_tgt = {}
            kbs_tgt = {}
            for i in range(4):
                la = ALA[:, DI * i:DI * (i + 1)]
                qb = (i % 2) * NH * 128

                def cs_mm(e, i=i, la=la, la_d=la_d[i], kd=knorm_d[i],
                          lcomb=s_upd_prev[7], prev_dec=dec_tgt.get(i - 1)):
                    e.wait_ge(ss.h, la_d)
                    e.wait_ge(sv.h, kd)
                    if lcomb is not None:
                        e.wait_ge(sv.h, lcomb)
                    if prev_dec is not None:
                        e.wait_ge(ss.h, prev_dec)
                    pbc = PB[4] if i % 2 == 0 else PB[5]
                    e.matmul(pbc[:, 0:DI], UT[:, :], la, start=True, stop=True)
                    for h in range(NH):
                        last = e.matmul(PB[2][0:64, 8 * i + h:8 * i + h + 1],
                                        la[:, 64 * h:64 * (h + 1)], ONESC[:, 0:1],
                                        start=(h == 0), stop=(h == NH - 1))
                    last.then_inc(st.h, 1)
                em("tensor", cs_mm)
                csd = st.inc()

                def cs_ev(e, i=i, csd=csd):
                    e.wait_ge(st.h, csd)
                    pbc = PB[4] if i % 2 == 0 else PB[5]
                    e.tensor_copy(CSB[:, DI * (i % 2):DI * (i % 2 + 1)], pbc[:, 0:DI]).then_inc(sv.h, 1)
                em("vector", cs_ev)
                cs_done = sv.inc()

                def dec_s(e, i=i, csd=csd):
                    e.wait_ge(st.h, csd)
                    e.activation(SDEC[:, 8 * i:8 * (i + 1)], PB[2][0:64, 8 * i:8 * (i + 1)],
                                 ACT.Exp).then_inc(ss.h, 1)
                em("scalar", dec_s)
                dec_tgt[i] = ss.inc()

                def e12_s(e, i=i, cs_done=cs_done):
                    e.wait_ge(sv.h, cs_done)
                    e.activation(EF[:, 0:DI], CSB[:, DI * (i % 2):DI * (i % 2 + 1)], ACT.Exp)
                    e.drain()
                    e.activation(EF[:, DI:2 * DI], CSB[:, DI * (i % 2):DI * (i % 2 + 1)], ACT.Exp,
                                 scale=-1.0).then_inc(ss.h, 1)
                em("scalar", e12_s)
                e12_d = ss.inc()

                def qk_v(e, i=i, e12_d=e12_d):
                    e.wait_ge(ss.h, e12_d)
                    sl = slice(DI * i, DI * (i + 1))
                    e.tensor_tensor(QS[:, sl], QS[:, sl], EF[:, 0:DI], ALU.mult)
                    e.drain()
                    e.tensor_tensor(KS[:, sl], KS[:, sl], EF[:, DI:2 * DI],
                                    ALU.mult).then_inc(sv.h, 1)
                em("vector", qk_v)
                qk_d = sv.inc()

                def clb_mm(e, i=i, la=la, qk_d=qk_d, prev_kbs=kbs_tgt.get(i - 1)):
                    e.wait_ge(sv.h, qk_d)
                    if prev_kbs is not None:
                        e.wait_ge(ss.h, prev_kbs)
                    e.matmul(PB[7][0:1, 0:DI], ONESC[:, 0:1], la,
                             start=True, stop=True).then_inc(st.h, 1)
                em("tensor", clb_mm)
                clb_d = st.inc()

                def clb_ev(e, clb_d=clb_d):
                    e.wait_ge(st.h, clb_d)
                    e.tensor_copy(EF[0:1, 0:DI], PB[7][0:1, 0:DI]).then_inc(sv.h, 1)
                em("vector", clb_ev)
                clb_e = sv.inc()

                def clb2_mm(e, clb_e=clb_e):
                    e.wait_ge(sv.h, clb_e)
                    e.matmul(PB[7][:, 0:DI], ONESR[0:1, :], EF[0:1, 0:DI],
                             start=True, stop=True).then_inc(st.h, 1)
                em("tensor", clb2_mm)
                clb2_d = st.inc()

                def kb_s(e, i=i, clb2_d=clb2_d):
                    e.wait_ge(st.h, clb2_d)
                    e.activation(KBE[:, DI * (i % 2):DI * (i % 2 + 1)], PB[7][:, 0:DI],
                                 ACT.Exp).then_inc(ss.h, 1)
                em("scalar", kb_s)
                kbs_tgt[i] = ss.inc()

                def kb2_v(e, i=i, kbs_d=kbs_tgt[i]):
                    e.wait_ge(ss.h, kbs_d)
                    sl = slice(DI * i, DI * (i + 1))
                    e.tensor_tensor(KBB[:, sl], KS[:, sl],
                                    KBE[:, DI * (i % 2):DI * (i % 2 + 1)],
                                    ALU.mult).then_inc(sv.h, 1)
                em("vector", kb2_v)
                kb_done = sv.inc()

                def f_tr(e, i=i, kb_done=kb_done):
                    e.wait_ge(sv.h, kb_done)
                    for h in range(NH):
                        pb = PB[4] if h < 4 else PB[5]
                        e.transpose(pb[0:64, 128 * (h % 4):128 * (h % 4) + 128],
                                    QS[:, DI * i + 64 * h:DI * i + 64 * (h + 1)], IDENT[:, :])
                    e.nop().then_inc(st.h, 1)
                    for h in range(NH):
                        pb = PB[6] if h < 4 else PB[7]
                        last = e.transpose(pb[0:64, 128 * (h % 4):128 * (h % 4) + 128],
                                           KS[:, DI * i + 64 * h:DI * i + 64 * (h + 1)], IDENT[:, :])
                    last.then_inc(st.h, 1)
                em("tensor", f_tr)
                st.inc()
                ftr2 = st.inc()

                def f_ev(e, i=i, qb=qb, ftr2=ftr2):
                    e.wait_ge(st.h, ftr2)
                    e.tensor_copy(QTT[:, qb:qb + 512], PB[4][0:64, 0:512])
                    e.tensor_copy(QTT[:, qb + 512:qb + 1024], PB[5][0:64, 0:512])
                    e.tensor_copy(KHT[:, qb:qb + 512], PB[6][0:64, 0:512])
                    last = e.tensor_copy(KHT[:, qb + 512:qb + 1024], PB[7][0:64, 0:512])
                    last.then_inc(sv.h, 1)
                em("vector", f_ev)
                chunk_ready[i] = sv.inc()
                flush_att()

                if level < 6:
                    continue
                tglob = sI * 4 + i
                pm_evac = {}
                o_evac = {}
                s_upd = {}
                for h in range(NH):
                    pbP = PB[6] if h % 2 == 0 else PB[7]
                    pbO = PB[4] if h % 2 == 0 else PB[5]
                    pbS = PB[2] if h % 2 == 0 else PB[3]

                    def p_mm(e, h=h, qb=qb, pbP=pbP, cr=chunk_ready[i], pe=pm_evac.get(h - 2)):
                        if h == 0:
                            e.wait_ge(sv.h, cr)
                        if pe is not None:
                            e.wait_ge(sv.h, pe)
                        e.matmul(pbP[:, 0:128], KHT[:, qb + 128 * h:qb + 128 * (h + 1)],
                                 QTT[:, qb + 128 * h:qb + 128 * (h + 1)],
                                 start=True, stop=True).then_inc(st.h, 1)
                    em("tensor", p_mm)
                    p_d = st.inc()

                    def pm_v(e, pbP=pbP, p_d=p_d):
                        e.wait_ge(st.h, p_d)
                        e.tensor_tensor(PMS[:, :], pbP[:, 0:128], UT[:, :],
                                        ALU.mult).then_inc(sv.h, 1)
                    em("vector", pm_v)
                    pm_evac[h] = sv.inc()

                    def o_mm(e, i=i, h=h, qb=qb, pbO=pbO, pmd=pm_evac[h],
                             oe=o_evac.get(h - 2), su=s_upd_prev[h]):
                        e.wait_ge(sv.h, pmd)
                        if oe is not None:
                            e.wait_ge(sv.h, oe)
                        if su is not None:
                            e.wait_ge(sv.h, su)
                        e.matmul(pbO[0:64, 0:128], VS[:, DI * i + 64 * h:DI * i + 64 * (h + 1)],
                                 PMS[:, :], start=True, stop=False)
                        e.matmul(pbO[0:64, 0:128], SST[:, 64 * h:64 * (h + 1)],
                                 QTT[:, qb + 128 * h:qb + 128 * (h + 1)],
                                 start=False, stop=True).then_inc(st.h, 1)
                    em("tensor", o_mm)
                    o_d = st.inc()

                    def o_ev(e, h=h, pbO=pbO, o_d=o_d):
                        e.wait_ge(st.h, o_d)
                        pair = h // 2
                        if h % 2 == 0:
                            e.tensor_copy(OTS[0:64, 128 * pair:128 * (pair + 1)],
                                          pbO[0:64, 0:128]).then_inc(sv.h, 1)
                        else:
                            e.tensor_copy(OTS[64:128, 128 * pair:128 * (pair + 1)],
                                          pbO[0:64, 0:128]).then_inc(sv.h, 1)
                    em("vector", o_ev)
                    o_evac[h] = sv.inc()

                    def s_mm(e, i=i, h=h, pbS=pbS, su=s_upd.get(h - 2), dci=dec_tgt[i]):
                        if h < 2:
                            e.wait_ge(ss.h, dci)
                        if su is not None:
                            e.wait_ge(sv.h, su)
                        e.matmul(pbS[0:64, 0:64], KBB[:, DI * i + 64 * h:DI * i + 64 * (h + 1)],
                                 VS[:, DI * i + 64 * h:DI * i + 64 * (h + 1)],
                                 start=True, stop=True).then_inc(st.h, 1)
                    em("tensor", s_mm)
                    smm_d = st.inc()

                    def s_v(e, i=i, h=h, pbS=pbS, smm_d=smm_d, od=o_evac[h]):
                        e.wait_ge(st.h, smm_d)
                        e.wait_ge(sv.h, od)
                        e.scalar_tensor_tensor(SST[:, 64 * h:64 * (h + 1)],
                                               SST[:, 64 * h:64 * (h + 1)],
                                               SDEC[:, 8 * i + h:8 * i + h + 1],
                                               pbS[0:64, 0:64],
                                               ALU.mult, ALU.add).then_inc(sv.h, 1)
                    em("vector", s_v)
                    s_upd[h] = sv.inc()

                s_upd_prev = dict(s_upd)

                def wo_mm(e, oe=o_evac[NH - 1], attld=attld):
                    e.wait_ge(sv.h, oe)
                    e.wait_ge(qw.h, attld)
                    for n in range(2):
                        for c in range(4):
                            last = e.matmul(PB[n][:, 0:512], OTS[:, 128 * c:128 * (c + 1)],
                                            WOS[:, D * c + 512 * n:D * c + 512 * (n + 1)],
                                            start=(c == 0), stop=(c == 3))
                    last.then_inc(st.h, 1)
                em("tensor", wo_mm)
                wo_d = st.inc()

                def att_comb(e, tglob=tglob, wo_d=wo_d):
                    e.wait_ge(st.h, wo_d)
                    t = tglob
                    Xt = X[:, D * t:D * (t + 1)]
                    e.tensor_scalar(TSC[:, :], Xt, STAT[:, t:t + 1], STAT[:, 16 + t:17 + t],
                                    ALU.subtract, ALU.mult)
                    e.drain()
                    e.tensor_tensor(Xt, Xt, TSC[:, :], ALU.add)
                    e.drain()
                    for n in range(2):
                        sl = slice(D * t + 512 * n, D * t + 512 * (n + 1))
                        last = e.tensor_tensor(X[:, sl], X[:, sl], PB[n][:, 0:512], ALU.add)
                    last.then_inc(sv.h, 1)
                pend_att[0] = (att_comb, sI, i == 3)

                if i == 1 and sI + 1 < T // 512:
                    for j in range(2):
                        t2 = (sI + 1) * 4 + j

                        def pref_v(e, t2=t2, j=j, sr=stats_ready[t2], gb_ld=gb_ld):
                            e.wait_ge(sv.h, sr)
                            e.wait_ge(qw.h, gb_ld)
                            dstb = XNB[:, 1024 * j:1024 * (j + 1)]
                            e.tensor_scalar(dstb, X[:, D * t2:D * (t2 + 1)],
                                            STAT[:, t2:t2 + 1], STAT[:, 16 + t2:17 + t2],
                                            ALU.subtract, ALU.mult)
                            e.drain()
                            e.tensor_scalar(dstb, dstb, STAT[:, 32 + t2:33 + t2],
                                            STAT[:, 48 + t2:49 + t2], ALU.subtract, ALU.mult)
                            e.drain()
                            e.tensor_tensor(dstb, dstb, GAM, ALU.mult)
                            e.drain()
                            e.tensor_tensor(dstb, dstb, BET, ALU.add).then_inc(sv.h, 1)
                        em("vector", pref_v)
                        pref_tgt[(sI + 1, j)] = sv.inc()
        flush_att()
        return f3_stats

    a_stats = None
    scr_ready = False
    if "f1" in stages:
        a_stats = {} if ("attn" in stages and attn_level >= 3) else None
        ffn_stage(first_call=True, attn_stats_out=a_stats,
                  riders=("f3" in stages and _BF16))
        scr_ready = "f3" in stages and _BF16
    f3_pre = None
    if "attn" in stages:
        stage_barrier()
        f3_pre = attn_stage(level=attn_level, stats_pre=a_stats,
                            emit_f3=("f3" in stages and attn_level >= 6))
    if "f3" in stages:
        stage_barrier()
        ffn_stage(first_call=False, stats_pre=f3_pre, bf16=scr_ready)

    def stores(e, n=sv.n):
        e.wait_ge(sv.h, n)
        for t in range(ntile):
            e.dma_start(out=y_h[128 * t:128 * (t + 1), :],
                        in_=X[:, D * t:D * (t + 1)]).then_inc(qw.h, 16)
    em("sync", stores)
    if debug:
        def dbg_stores(e, n=sv.n):
            e.wait_ge(sv.h, n)
            e.dma_start(out=dbg_desti_h[:, 0:ntile], in_=DESTI[:, 0:ntile]).then_inc(qw.h, 16)
            e.dma_start(out=dbg_idxf_h[:, 0:ntile], in_=IDXF[:, 0:ntile]).then_inc(qw.h, 16)
            e.dma_start(out=dbg_ambc_h[:, 0:ntile], in_=AMBC[:, 0:ntile]).then_inc(qw.h, 16)
            e.dma_start(out=dbg_yrsel_h[:, :], in_=YRSEL[:, :]).then_inc(qw.h, 16)
        em("gpsimd", lambda e, n=sv.n: e.wait_ge(sv.h, n))
        for i in range((E * CAP) // 128):
            em("gpsimd", lambda e, i=i: e.dma_start(out=dbg_xs_h[128 * i:128 * (i + 1), :],
                                                    in_=xs_h[128 * i:128 * (i + 1), :]).then_inc(qg.h, 16))
            qg.inc(16)
            em("gpsimd", lambda e, i=i: e.dma_start(out=dbg_ys_h[128 * i:128 * (i + 1), :],
                                                    in_=ys_h[128 * i:128 * (i + 1), :]).then_inc(qg.h, 16))
            qg.inc(16)
        em("sync", dbg_stores)
        for _ in range(4):
            qw.inc(16)
        em("sync", lambda e, n=qw.n: e.wait_ge(qw.h, n))
        em("gpsimd", lambda e, n=qg.n: e.wait_ge(qg.h, n))
    final_store = None
    for t in range(ntile):
        final_store = qw.inc(16)
    em("sync", lambda e, fs=final_store: e.wait_ge(qw.h, fs))

    with nc.Block() as block:
        for eng_name, deco in (("sync", block.sync), ("tensor", block.tensor),
                               ("vector", block.vector), ("scalar", block.scalar),
                               ("gpsimd", block.gpsimd)):
            fns = streams[eng_name]

            def body(e, fns=fns):
                for f in fns:
                    f(e)
            deco(body)
    return nc



def _host_layouts(inputs, bslice):
    up = np.asarray(inputs["L0_ffn_up"], np.float32)
    gate = np.asarray(inputs["L0_ffn_gate"], np.float32)
    down = np.asarray(inputs["L0_ffn_down"], np.float32)
    router = np.asarray(inputs["L0_ffn_router"], np.float32)
    wus = np.asarray(inputs["L0_ffn_w_up_s"], np.float32)
    wgs = np.asarray(inputs["L0_ffn_w_gate_s"], np.float32)
    wds = np.asarray(inputs["L0_ffn_w_down_s"], np.float32)

    def tile_ug(w):
        wt = w.transpose(0, 2, 1)
        return np.ascontiguousarray(wt.reshape(E, 8, 128, 16, 128).transpose(0, 3, 1, 2, 4))

    def tile_ug_s(w):
        return np.ascontiguousarray(w.T.reshape(8, 128, 16, 128).transpose(2, 0, 1, 3))

    base = {
        "upTt": tile_ug(up), "gateTt": tile_ug(gate),
        "downTt": np.ascontiguousarray(down.transpose(0, 2, 1).reshape(E, 16, 128, D)),
        "wupTt": tile_ug_s(wus), "wgateTt": tile_ug_s(wgs),
        "wdownTt": np.ascontiguousarray(wds.T.reshape(16, 128, D)),
        "routerT": np.ascontiguousarray(
            router.T.reshape(8, 128, 8).transpose(1, 0, 2).reshape(128, 64)),
        "Wq": np.ascontiguousarray(np.asarray(inputs["L0_attn_Wq"], np.float32)),
        "Wk": np.ascontiguousarray(np.asarray(inputs["L0_attn_Wk"], np.float32)),
        "Wv": np.ascontiguousarray(np.asarray(inputs["L0_attn_Wv"], np.float32)),
        "Wa": np.ascontiguousarray(np.asarray(inputs["L0_attn_Wa"], np.float32)),
        "Wo": np.ascontiguousarray(np.asarray(inputs["L0_attn_Wo"], np.float32)),
        "ba_tile": np.ascontiguousarray(
            np.tile(np.asarray(inputs["L0_attn_ba"], np.float32)[None, :], (128, 1))),
        "gamma_tile": np.ascontiguousarray(
            np.tile(np.asarray(inputs["L0_attn_gamma"], np.float32)[None, :], (128, 1))),
        "beta_tile": np.ascontiguousarray(
            np.tile(np.asarray(inputs["L0_attn_beta"], np.float32)[None, :], (128, 1))),
        "ident": np.eye(128, dtype=np.float32),
        "ut": np.triu(np.ones((128, 128), np.float32)),
        "ones_row": np.ones((1, 128), np.float32),
        "ones_col": np.ones((128, 1), np.float32),
        "iotaE": np.tile(np.arange(8, dtype=np.float32)[None, :], (128, 1)),
        "iota16": np.tile(np.arange(16, dtype=np.float32)[None, :], (128, 1)),
        "actconsts": np.tile(np.array([-1.0, LN_EPS, 0.5, 0, 0, 0, 0, 0],
                                      np.float32)[None, :], (128, 1)),
    }
    x = np.asarray(inputs["L0_attn_x"], np.float32)
    return [dict(base, x=np.ascontiguousarray(x[bidx])) for bidx in bslice]


def kernel(**inputs):
    x = np.asarray(inputs["L0_attn_x"], np.float32)
    Bn = x.shape[0]
    nc = build_core_program(T=x.shape[1])
    in_maps = _host_layouts(inputs, list(range(Bn)))
    res = run_bass_kernel_spmd(nc, in_maps, core_ids=list(range(Bn)))
    return np.stack([np.asarray(r["y"], np.float32) for r in res.results], axis=0)
